# version 05
# baseline (speedup 1.0000x reference)
import numpy as np
import ml_dtypes
import contextlib
import concourse.bass as bass
import concourse.mybir as mybir
from concourse.bass_utils import run_bass_kernel_spmd

F32 = mybir.dt.float32
BF16 = mybir.dt.bfloat16
AF = mybir.ActivationFunctionType
ALU = mybir.AluOpType
AX = mybir.AxisListType
BF = ml_dtypes.bfloat16

ENGS = ("tensor", "vector", "scalar", "gpsimd", "sync")

B, T, D = 4, 4096, 1024
TOWN = T // 2
NT = TOWN // 128


class Sched:

    def __init__(self):
        self.ops = []

    def add(self, eng, emit, deps=()):
        tok = len(self.ops)
        self.ops.append([eng, emit, [d for d in deps if d is not None], False])
        return tok

    def dma(self, eng, out, in_, deps=()):
        tok = len(self.ops)
        self.ops.append([eng, (out, in_, None), [d for d in deps if d is not None], True])
        return tok

    def dma_acc(self, eng, out, in_, op, deps=()):
        tok = len(self.ops)
        self.ops.append([eng, (out, in_, op), [d for d in deps if d is not None], True])
        return tok

    def emit(self, nc, es, final_waits=()):
        import os
        ops = self.ops
        pfx = os.environ.get("SCHED_PREFIX")
        if pfx:
            ops = ops[:int(pfx)]
            final_waits = [d for d in final_waits if d < len(ops)]
        n = len(ops)
        need_inc = [False] * n
        for (eng, emit, deps, isdma) in ops:
            for d in deps:
                need_inc[d] = True
        final_waits = list(final_waits)
        for i, (eng, emit, deps, isdma) in enumerate(ops):
            if isdma and not need_inc[i]:
                need_inc[i] = True
                final_waits.append(i)
        for d in final_waits:
            need_inc[d] = True
        val = [0] * n
        cnt = {}
        for i, (eng, emit, deps, isdma) in enumerate(ops):
            key = (eng, isdma)
            if need_inc[i]:
                cnt[key] = cnt.get(key, 0) + (16 if isdma else 1)
            val[i] = cnt.get(key, 0)
        sems = {}
        for e in ENGS:
            sems[(e, False)] = es.enter_context(nc.semaphore("Sc" + e))
            sems[(e, True)] = es.enter_context(nc.semaphore("Sd" + e))
        blk = es.enter_context(nc.Block())
        per_eng = {e: [i for i in range(n) if ops[i][0] == e] for e in ENGS}

        def run_engine(engine_obj, e):
            waited = {}
            for i in per_eng[e]:
                eng, emit, deps, isdma = ops[i]
                needs = {}
                for d in deps:
                    k2 = (ops[d][0], ops[d][3])
                    if k2 == (e, False) and e == "tensor":
                        continue
                    needs[k2] = max(needs.get(k2, 0), val[d])
                if isdma:
                    prev = val[i] - 16
                    if prev > 0:
                        needs[(e, True)] = max(needs.get((e, True), 0), prev)
                for k2 in sorted(needs):
                    if needs[k2] > waited.get(k2, 0):
                        engine_obj.wait_ge(sems[k2], needs[k2])
                        waited[k2] = needs[k2]
                if isdma:
                    out, in_, aop = emit
                    if callable(out):
                        out = out()
                    if callable(in_):
                        in_ = in_()
                    if aop is None:
                        try:
                            inst = engine_obj.dma_start(out=out, in_=in_)
                        except ValueError as _e:
                            import traceback as _tb
                            _tb.print_exc()
                            raise
                    else:
                        inst = engine_obj.dma_start(out=out, in_=in_, accum_op=aop)
                else:
                    inst = emit(engine_obj)
                if need_inc[i]:
                    inst.then_inc(sems[(eng, isdma)], 16 if isdma else 1)
            if e == "sync":
                fw = {}
                for d in final_waits:
                    k2 = (ops[d][0], ops[d][3])
                    fw[k2] = max(fw.get(k2, 0), val[d])
                for k2 in sorted(fw):
                    if fw[k2] > waited.get(k2, 0):
                        engine_obj.wait_ge(sems[k2], fw[k2])

        for e in ENGS:
            getattr(blk, e)(lambda engine_obj, e=e: run_engine(engine_obj, e))



def xpos_tables():
    QK = 1024
    SCALE_BASE = 512.0
    j2 = np.arange(0, QK, 2, dtype=np.float32)
    base_scale = (j2 + 0.4 * QK) / (1.4 * QK)
    pos = np.arange(T, dtype=np.float32)
    scale = base_scale[None, :] ** (pos[:, None] / SCALE_BASE)
    inv_freq = (1.0 / 10000.0 ** (np.arange(512, dtype=np.float32) / 512.0))
    sinu = pos[:, None] * inv_freq[None, :]
    sin, cos = np.sin(sinu), np.cos(sinu)
    def rep2(x):
        return np.repeat(x, 2, axis=1)
    cos_q = rep2(cos * scale)
    sin_q = rep2(sin * scale)
    cos_k = rep2(cos / scale)
    sin_k = rep2(sin / scale)
    return cos_q, sin_q, cos_k, sin_k


def rot_matT():
    P = np.zeros((128, 128), dtype=np.float32)
    for j in range(64):
        P[2 * j, 2 * j + 1] = -1.0
        P[2 * j + 1, 2 * j] = 1.0
    return np.ascontiguousarray(P.T).astype(BF)


IDENT_BF = np.eye(128, dtype=BF)
IDENT_F32 = np.eye(128, dtype=np.float32)



def build_l0prep():
    nc = bass.Bass("TRN2", target_bir_lowering=False, debug=False)
    io = {}
    io["x"] = nc.dram_tensor("x", [TOWN, D], F32, kind="ExternalInput").ap()
    io["wqkv"] = nc.dram_tensor("wqkv", [D, 4096], F32, kind="ExternalInput").ap()
    io["wg"] = nc.dram_tensor("wg", [D, 2048], F32, kind="ExternalInput").ap()
    io["cos_q"] = nc.dram_tensor("cos_q", [1024, TOWN], BF16, kind="ExternalInput").ap()
    io["sin_q"] = nc.dram_tensor("sin_q", [1024, TOWN], BF16, kind="ExternalInput").ap()
    io["cos_k"] = nc.dram_tensor("cos_k", [1024, TOWN], BF16, kind="ExternalInput").ap()
    io["sin_k"] = nc.dram_tensor("sin_k", [1024, TOWN], BF16, kind="ExternalInput").ap()
    io["rotmT"] = nc.dram_tensor("rotmT", [128, 128], BF16, kind="ExternalInput").ap()
    io["ident"] = nc.dram_tensor("ident", [128, 128], BF16, kind="ExternalInput").ap()
    io["qr"] = nc.dram_tensor("qr", [1024, TOWN], BF16, kind="ExternalOutput").ap()
    io["kr"] = nc.dram_tensor("kr", [1024, TOWN], BF16, kind="ExternalOutput").ap()
    io["v"] = nc.dram_tensor("v", [TOWN, 2048], BF16, kind="ExternalOutput").ap()
    io["gate"] = nc.dram_tensor("gate", [TOWN, 2048], BF16, kind="ExternalOutput").ap()
    io["xnfm"] = nc.dram_tensor("xnfm", [1024, TOWN], BF16, kind="ExternalOutput").ap()
    es = contextlib.ExitStack()
    S = Sched()
    outs = emit_l0prep(nc, S, es, io, [], ntok=TOWN)
    S.emit(nc, es, final_waits=outs)
    es.close()
    return nc


def emit_l0prep(nc, S, es, io, bar, ntok, nq=8, nv=2048, ng=2048, store_xnfm=True, pfx=""):
    i_x = io["x"]
    i_wqkv = io["wqkv"]; i_wg = io["wg"]
    i_cq = io["cos_q"]; i_sq = io["sin_q"]; i_ck = io["cos_k"]; i_sk = io["sin_k"]
    i_rot = io["rotmT"]; i_id = io["ident"]
    o_qr = io["qr"]; o_kr = io["kr"]; o_v = io["v"]; o_gate = io["gate"]
    o_xnfm = io.get("xnfm")
    TOWN = ntok
    NT = TOWN // 128
    NCOLS = 2 * 128 * nq + nv
    sb = lambda nme, s, d=F32: es.enter_context(nc.sbuf_tensor(pfx + nme, s, d))
    psum = lambda nme, s, d=F32: es.enter_context(nc.psum_tensor(pfx + nme, s, d))
    outs = []

    w_sb = sb("w_sb", [128, 8 * NCOLS], BF16)
    wg_sb = sb("wg_sb", [128, 8 * ng], BF16)
    rot_sb = sb("rot_sb", [128, 128], BF16)
    id_sb = sb("id_sb", [128, 128], BF16)
    ld_w = [S.dma("gpsimd", w_sb[:, hc * NCOLS:(hc + 1) * NCOLS], i_wqkv[hc * 128:(hc + 1) * 128, :], deps=bar)
            for hc in range(8)]
    ld_wg = [S.dma("gpsimd", wg_sb[:, hc * ng:(hc + 1) * ng], i_wg[hc * 128:(hc + 1) * 128, :], deps=bar)
             for hc in range(8)]
    ld_rot = S.dma("sync", rot_sb[:], i_rot, deps=bar)
    ld_id = S.dma("sync", id_sb[:], i_id, deps=bar)

    xn_fm = sb("xn_fm", [128, 8 * TOWN], BF16)
    x_t = [sb("x_t0", [128, 1024])]
    x2_t = sb("x2_t", [128, 1024])
    s1 = sb("s1", [128, 1]); s2 = sb("s2", [128, 1])
    mu = sb("mu", [128, 1]); m2 = sb("m2", [128, 1]); var = sb("var", [128, 1])
    sq = sb("sq", [128, 1]); rstd = sb("rstd", [128, 1])
    xn_tok = [sb(f"xn_tok{i}", [128, 1024], BF16) for i in range(2)]
    ps_tr = [psum("ps_tr0", [128, 128], BF16)]

    prev_tile_done = [None, None]
    prev_xn_done = [None, None]
    prev_tr_evac = [None, None]
    xn_ready = []
    for i in range(NT):
        bi = 0
        ld = S.dma("sync", x_t[bi][:], i_x[i * 128:(i + 1) * 128, :],
                   deps=(prev_tile_done[bi] or []) + bar)
        sqr = S.add("scalar", lambda s, bi=bi: s.activation(out=x2_t[:], in_=x_t[bi][:], func=AF.Square),
                    deps=[ld, prev_xn_done[(i + 1) % 2]])
        r1 = S.add("vector", lambda v, bi=bi: v.reduce_sum(out=s1[:], in_=x_t[bi][:], axis=AX.X), deps=[ld])
        r2 = S.add("vector", lambda v: v.reduce_sum(out=s2[:], in_=x2_t[:], axis=AX.X), deps=[sqr, r1])
        c1 = S.add("vector", lambda v: v.tensor_scalar(out=mu[:], in0=s1[:], scalar1=1.0 / 1024, scalar2=None, op0=ALU.mult), deps=[r2])
        c2 = S.add("vector", lambda v: v.tensor_tensor(out=m2[:], in0=mu[:], in1=mu[:], op=ALU.mult), deps=[c1])
        c3 = S.add("vector", lambda v: v.tensor_scalar(out=var[:], in0=s2[:], scalar1=1.0 / 1024, scalar2=None, op0=ALU.mult), deps=[c2])
        c4 = S.add("vector", lambda v: v.tensor_sub(out=var[:], in0=var[:], in1=m2[:]), deps=[c3])
        c4b = S.add("vector", lambda v: v.tensor_scalar(out=var[:], in0=var[:], scalar1=1e-5, scalar2=None, op0=ALU.add), deps=[c4])
        c5 = S.add("scalar", lambda s: s.activation(out=sq[:], in_=var[:], func=AF.Sqrt), deps=[c4b])
        c6 = S.add("vector", lambda v: v.reciprocal(out=rstd[:], in_=sq[:]), deps=[c5])
        xo = S.add("vector", lambda v, bi=bi: v.tensor_scalar(out=xn_tok[bi][:], in0=x_t[bi][:],
                   scalar1=mu[:], scalar2=rstd[:], op0=ALU.subtract, op1=ALU.mult),
                   deps=[c6, prev_xn_done[bi]])
        prev_tile_done[bi] = [xo, sqr]
        evs = []
        for c in range(8):
            tb = 0
            tr = S.add("tensor", lambda t, bi=bi, c=c, tb=tb: t.transpose(
                out=ps_tr[tb][:], in_=xn_tok[bi][:, c * 128:(c + 1) * 128], identity=id_sb[:]),
                deps=[xo, ld_id, prev_tr_evac[tb]])
            ev = S.add("vector", lambda v, i=i, c=c, tb=tb: v.tensor_copy(
                out=xn_fm[:, c * TOWN + i * 128: c * TOWN + (i + 1) * 128], in_=ps_tr[tb][:]),
                deps=[tr])
            prev_tr_evac[tb] = ev
            evs.append(ev)
        prev_xn_done[bi] = evs[-1]
        xn_ready.append(evs)
    xn_all = [e for evs in xn_ready for e in evs]

    if store_xnfm:
        for c in range(8):
            outs.append(S.dma("sync", o_xnfm[c * 128:(c + 1) * 128, :], xn_fm[:, c * TOWN:(c + 1) * TOWN],
                              deps=[e for evs in xn_ready for e in evs[c:c + 1]]))

    ps_qkv = [psum(f"ps_qkv{i}", [128, 512]) for i in range(2)]
    ps_rot = [psum(f"ps_rot{i}", [128, 512]) for i in range(2)]
    cs_t = [sb("cs_t0", [128, 2 * TOWN], BF16)]
    qf = [sb(f"qf{i}", [128, 512]) for i in range(2)]
    qb = [sb(f"qb{i}", [128, 512], BF16) for i in range(2)]
    t1b = [sb(f"t1b{i}", [128, 512]) for i in range(2)]
    qrot = [sb(f"qrot{i}", [128, 512], BF16) for i in range(2)]
    prev_cs_use = [None, None]
    prev_q_sto = {}
    for c in range(2 * nq):
        cb = 0
        is_q = c < nq
        cc = c if is_q else c - nq
        src_c = i_cq if is_q else i_ck
        src_s = i_sq if is_q else i_sk
        ldc = S.dma("sync", cs_t[cb][:, 0:TOWN], src_c[cc * 128:(cc + 1) * 128, :], deps=[prev_cs_use[cb]])
        lds = S.dma("sync", cs_t[cb][:, TOWN:2 * TOWN], src_s[cc * 128:(cc + 1) * 128, :], deps=[prev_cs_use[cb]])
        for j in range(TOWN // 512):
            pb = (c * (TOWN // 512) + j) % 2
            mms = []
            for h in range(8):
                mms.append(S.add("tensor", lambda t, h=h, c=c, j=j, pb=pb: t.matmul(
                    out=ps_qkv[pb][:], lhsT=w_sb[:, h * NCOLS + c * 128:h * NCOLS + (c + 1) * 128],
                    rhs=xn_fm[:, h * TOWN + j * 512:h * TOWN + (j + 1) * 512],
                    start=(h == 0), stop=(h == 7)),
                    deps=([ld_w[h]] + (xn_all if h == 0 else []) + ([prev_q_sto.get((pb, 'f'))] if h == 0 else []))))
            e1 = S.add("vector", lambda v, pb=pb: v.tensor_copy(out=qf[pb][:], in_=ps_qkv[pb][:]),
                       deps=[mms[-1], prev_q_sto.get((pb, 'use_f'))])
            e2 = S.add("vector", lambda v, pb=pb: v.tensor_copy(out=qb[pb][:], in_=qf[pb][:]), deps=[e1])
            rot = S.add("tensor", lambda t, pb=pb: t.matmul(out=ps_rot[pb][:], lhsT=rot_sb[:], rhs=qb[pb][:],
                        start=True, stop=True), deps=[e2, ld_rot, prev_q_sto.get((pb, 'rot'))])
            m1 = S.add("vector", lambda v, pb=pb, cb=cb, j=j: v.tensor_tensor(
                out=t1b[pb][:], in0=qf[pb][:], in1=cs_t[cb][:, j * 512:(j + 1) * 512], op=ALU.mult),
                deps=[e1, ldc])
            m2o = S.add("vector", lambda v, pb=pb, cb=cb, j=j: v.tensor_tensor(
                out=qf[pb][:], in0=ps_rot[pb][:], in1=cs_t[cb][:, TOWN + j * 512:TOWN + (j + 1) * 512], op=ALU.mult),
                deps=[rot, lds, m1])
            a1 = S.add("vector", lambda v, pb=pb: v.tensor_add(out=qrot[pb][:], in0=t1b[pb][:], in1=qf[pb][:]),
                       deps=[m2o, prev_q_sto.get((pb, 'sto'))])
            dst = o_qr if is_q else o_kr
            sto = S.dma("sync", dst[cc * 128:(cc + 1) * 128, j * 512:(j + 1) * 512], qrot[pb][:], deps=[a1])
            outs.append(sto)
            prev_cs_use[cb] = m2o
            prev_q_sto[(pb, 'f')] = e1
            prev_q_sto[(pb, 'use_f')] = m2o
            prev_q_sto[(pb, 'rot')] = m2o
            prev_q_sto[(pb, 'sto')] = sto
            prev_q_sto[(pb, 'use_f')] = a1

    ps_v = [psum(f"ps_v{i}", [128, 512]) for i in range(2)]
    vstg = [sb(f"vstg{i}", [128, 512], BF16) for i in range(2)]
    prev_v = {}
    for i in range(NT):
        for which, wbase, wwidth, dst, act in (("v", 2 * 128 * nq, nv, o_v, None), ("g", 0, ng, o_gate, AF.Silu)):
            for oc in range(wwidth // 512):
                pb = (i * 8 + (0 if which == "v" else 4) + oc) % 2
                mms = []
                for h in range(8):
                    if which == "v":
                        rhs = w_sb[:, h * NCOLS + wbase + oc * 512: h * NCOLS + wbase + (oc + 1) * 512]
                    else:
                        rhs = wg_sb[:, h * ng + oc * 512: h * ng + (oc + 1) * 512]
                    mms.append(S.add("tensor", lambda t, h=h, i=i, rhs=rhs, pb=pb: t.matmul(
                        out=ps_v[pb][:], lhsT=xn_fm[:, h * TOWN + i * 128:h * TOWN + (i + 1) * 128],
                        rhs=rhs, start=(h == 0), stop=(h == 7)),
                        deps=([ld_w[h] if which == "v" else ld_wg[h]] + (xn_all if h == 0 else [])
                              + ([prev_v.get((pb, 'ev'))] if h == 0 else []))))
                if act is None:
                    ev = S.add("vector", lambda v, pb=pb: v.tensor_copy(out=vstg[pb][:], in_=ps_v[pb][:]),
                               deps=[mms[-1], prev_v.get((pb, 'sto'))])
                else:
                    ev = S.add("scalar", lambda s, pb=pb: s.activation(out=vstg[pb][:], in_=ps_v[pb][:], func=AF.Silu),
                               deps=[mms[-1], prev_v.get((pb, 'sto'))])
                sto = S.dma("sync", dst[i * 128:(i + 1) * 128, oc * 512:(oc + 1) * 512], vstg[pb][:], deps=[ev])
                outs.append(sto)
                prev_v[(pb, 'ev')] = ev
                prev_v[(pb, 'sto')] = sto

    return outs



def shard_tokens(arr_btd):
    out = []
    for p in range(4):
        for half in range(2):
            out.append(np.ascontiguousarray(arr_btd[p, half * TOWN:(half + 1) * TOWN]))
    return out


def run_l0prep(inputs, run=run_bass_kernel_spmd):
    x = inputs["L0_attn_hidden_states"]
    cq, sq_, ck, sk = xpos_tables()
    rotT = rot_matT()
    wqkv = np.ascontiguousarray(inputs["L0_attn_W_qkv"], dtype=np.float32)
    wg = np.ascontiguousarray(inputs["L0_attn_W_g"], dtype=np.float32)
    nc = build_l0prep()
    in_maps = []
    for p in range(4):
        for half in range(2):
            sl = slice(half * TOWN, (half + 1) * TOWN)
            in_maps.append({
                "x": np.ascontiguousarray(x[p, sl]),
                "wqkv": wqkv, "wg": wg,
                "cos_q": np.ascontiguousarray(cq[sl].T),
                "sin_q": np.ascontiguousarray(sq_[sl].T),
                "cos_k": np.ascontiguousarray(ck[sl].T),
                "sin_k": np.ascontiguousarray(sk[sl].T),
                "rotmT": rotT, "ident": IDENT_BF,
            })
    res = run(nc, in_maps, core_ids=list(range(8)))
    return res.results






def build_l0ret():
    nc = bass.Bass("TRN2", target_bir_lowering=False, debug=False)
    io = {}
    q_ = nc.dram_tensor("qh", [4, 128, T], BF16, kind="ExternalInput").ap()
    k_ = nc.dram_tensor("kh", [4, 128, T], BF16, kind="ExternalInput").ap()
    v_ = nc.dram_tensor("vh", [4, T, 256], BF16, kind="ExternalInput").ap()
    g_ = nc.dram_tensor("gh", [4, T, 256], BF16, kind="ExternalInput").ap()
    io["qh"] = lambda m: q_[m]
    io["kh"] = lambda m: k_[m]
    io["vh"] = lambda m: v_[m]
    io["gh"] = lambda m: g_[m]
    io["decay"] = nc.dram_tensor("decay", [1, 8], F32, kind="ExternalInput").ap()
    io["causal"] = nc.dram_tensor("causal", [4, 128, 512], F32, kind="ExternalInput").ap()
    io["ident"] = nc.dram_tensor("ident", [128, 128], BF16, kind="ExternalInput").ap()
    io["ones128"] = nc.dram_tensor("ones128", [1, 128], F32, kind="ExternalInput").ap()
    nc.dram_tensor("hoff", [1, 1], F32, kind="ExternalInput").ap()
    gfm_ = nc.dram_tensor("gfm", [4, 256, T], BF16, kind="ExternalOutput").ap()
    io["gfm"] = lambda m: gfm_[m]
    es = contextlib.ExitStack()
    S = Sched()
    outs = emit_l0ret(nc, S, es, io, [])
    S.emit(nc, es, final_waits=outs)
    es.close()
    return nc


def emit_l0ret(nc, S, es, io, bar, pfx="r_"):
    i_q = io["qh"]; i_k = io["kh"]; i_v = io["vh"]; i_g = io["gh"]
    i_dec = io["decay"]; i_caus = io["causal"]; i_id = io["ident"]; i_ones = io["ones128"]
    o_gfm = io["gfm"]
    sb = lambda nme, s, d=F32: es.enter_context(nc.sbuf_tensor(pfx + nme, s, d))
    psum = lambda nme, s, d=F32: es.enter_context(nc.psum_tensor(pfx + nme, s, d))
    outs = []

    q_sb = sb("q_sb", [128, T], BF16)
    k_sb = sb("k_sb", [128, T], BF16)
    v_sb = sb("v_sb", [128, 32 * 256], BF16)
    g_sb = sb("g_sb", [128, 32 * 256], BF16)
    gfm_sb = sb("gfm_sb", [128, 2 * T], BF16)
    dec_sb = sb("dec_sb", [1, 8])
    lnd_sb = sb("lnd_sb", [1, 8])
    lndc = sb("lndc", [128, 8])
    ones_sb = sb("ones_sb", [1, 128])
    id_sb = sb("id_sb", [128, 128], BF16)
    caus_sb = sb("caus_sb", [128, 4 * 512])
    iob = sb("iob", [128, 512])
    io2 = sb("io2", [128, 36])
    Bt = [sb(f"Bt{v}", [128, 512]) for v in range(5)]
    dpow = sb("dpow", [128, 36])
    Pbf = [sb(f"Pbf{i}", [128, 512], BF16) for i in range(2)]
    sqt = sb("sqt", [128, 256])
    ot_sb = sb("ot_sb", [128, 256])
    s1c = sb("s1c", [128, 1]); s2c = sb("s2c", [128, 1])
    muc = sb("muc", [128, 1]); m2c = sb("m2c", [128, 1]); varc = sb("varc", [128, 1])
    sqc = sb("sqc", [128, 1]); rsc = sb("rsc", [128, 1])
    nmt = sb("nmt", [128, 256])
    gtd = sb("gtd", [128, 256], BF16)

    ps_ret = [psum(f"ps_ret{i}", [128, 512]) for i in range(2)]
    ps_av = [psum(f"ps_av{i}", [128, 512]) for i in range(4)]
    ps_tr = [psum("ps_tr0", [128, 1024], BF16)]

    ld_dec = S.dma("sync", dec_sb[:], i_dec, deps=bar)
    ld_ones = S.dma("sync", ones_sb[:], i_ones, deps=bar)
    ld_id = S.dma("sync", id_sb[:], i_id, deps=bar)
    ld_caus = [S.dma("sync", caus_sb[:, v * 512:(v + 1) * 512], i_caus[v], deps=bar) for v in range(4)]
    io1 = S.add("gpsimd", lambda g: g.iota(out=iob[:], pattern=[[1, 512]], base=0, channel_multiplier=-1,
                                           allow_small_or_imprecise_dtypes=True), deps=bar)
    io2o = S.add("gpsimd", lambda g: g.iota(out=io2[:], pattern=[[128, 36]], base=-384, channel_multiplier=0,
                                            allow_small_or_imprecise_dtypes=True), deps=bar)
    lno = S.add("scalar", lambda s: s.activation(out=lnd_sb[:], in_=dec_sb[:], func=AF.Ln), deps=[ld_dec])

    prev_map = {}
    for m in range(4):
        ldq = S.dma("sync", q_sb[:], i_q(m), deps=[prev_map.get('q')] + bar)
        ldk = S.dma("sync", k_sb[:], i_k(m), deps=[prev_map.get('k')] + bar)
        ldv = S.dma("sync", v_sb[:].rearrange("p (st c) -> p st c", c=256),
                    i_v(m).rearrange("(st p) c -> p st c", p=128), deps=[prev_map.get('v')] + bar)
        ldg = S.dma("sync", g_sb[:].rearrange("p (st c) -> p st c", c=256),
                    i_g(m).rearrange("(st p) c -> p st c", p=128), deps=[prev_map.get('g')] + bar)
        rep = S.add("tensor", lambda t: t.matmul(out=ps_ret[0][:, 0:8], lhsT=ones_sb[:], rhs=lnd_sb[:],
                                                 start=True, stop=True),
                    deps=[lno, ld_ones, prev_map.get('ret0')])
        cpl = S.add("vector", lambda v: v.tensor_copy(out=lndc[:], in_=ps_ret[0][:, 0:8]), deps=[rep, prev_map.get('lndc')])
        bfull = S.add("scalar", lambda s, m=m: s.activation(out=Bt[4][:], in_=iob[:], func=AF.Exp,
                                                            scale=lndc[:, m:m + 1]), deps=[cpl, io1, prev_map.get('Bfull')])
        bvs = [S.add("vector", lambda v, v_=v_: v.tensor_tensor(out=Bt[v_][:], in0=Bt[4][:],
                     in1=caus_sb[:, v_ * 512:(v_ + 1) * 512], op=ALU.mult),
                     deps=[bfull, ld_caus[v_], prev_map.get('Bv%d' % v_)]) for v_ in range(4)]
        dp1 = S.add("scalar", lambda s, m=m: s.activation(out=dpow[:], in_=io2[:], func=AF.Exp,
                                                          scale=lndc[:, m:m + 1]), deps=[cpl, io2o, prev_map.get('dpow')])
        dp2 = S.add("vector", lambda v: v.tensor_scalar(out=dpow[:], in0=dpow[:], scalar1=0.08838834764831845,
                                                        scalar2=None, op0=ALU.mult), deps=[dp1])

        ret_prev = {0: cpl, 1: None}
        stt_prev = {0: None, 1: None}
        av_last = {}
        gn_done_prev = [None, None, None, None]
        trev_prev = prev_map.get('trev')
        sto_g = []
        for tb in range(8):
            t0 = tb * 512
            n_st = tb * 4 + 4
            av_first = {}
            for st in range(n_st):
                s0 = st * 128
                rb = st % 2
                pb = st % 2
                mi = (t0 - s0) // 128 + 3
                var_idx = 4 if s0 <= t0 - 128 else (s0 - t0) // 128
                mm = S.add("tensor", lambda t, m=m, s0=s0, t0=t0, rb=rb: t.matmul(
                    out=ps_ret[rb][:], lhsT=k_sb[:, s0:s0 + 128], rhs=q_sb[:, t0:t0 + 512],
                    start=True, stop=True),
                    deps=[ldq, ldk, ret_prev[rb]])
                stt = S.add("vector", lambda v, rb=rb, pb=pb, mi=mi, var_idx=var_idx: v.scalar_tensor_tensor(
                    out=Pbf[pb][:], in0=ps_ret[rb][:], scalar=dpow[:, mi:mi + 1],
                    in1=Bt[var_idx][:], op0=ALU.mult, op1=ALU.mult),
                    deps=[mm, dp2, (bvs[var_idx] if var_idx < 4 else bfull), stt_prev[pb]])
                ret_prev[rb] = stt
                avd = []
                for ts4 in range(4):
                    first = (st == 0)
                    last = (st == n_st - 1)
                    av = S.add("tensor", lambda t, pb=pb, ts4=ts4, st=st, first=first, last=last: t.matmul(
                        out=ps_av[ts4][:, 0:256], lhsT=Pbf[pb][:, ts4 * 128:(ts4 + 1) * 128],
                        rhs=v_sb[:, st * 256:(st + 1) * 256],
                        start=first, stop=last),
                        deps=[stt, ldv] + ([gn_done_prev[ts4]] if first else []))
                    avd.append(av)
                    av_last[ts4] = av
                stt_prev[pb] = avd[-1]
            for ts4 in range(4):
                tt = tb * 4 + ts4
                pav = ps_av[ts4]
                d0 = av_last[ts4]
                g0 = S.add("vector", lambda v, pav=pav: v.tensor_copy(out=ot_sb[:], in_=pav[:, 0:256]),
                           deps=[d0, prev_map.get('ot')])
                g1 = S.add("scalar", lambda s: s.activation(out=sqt[:], in_=ot_sb[:], func=AF.Square),
                           deps=[g0, prev_map.get('sqt')])
                g2 = S.add("vector", lambda v: v.reduce_sum(out=s1c[:], in_=ot_sb[:], axis=AX.X),
                           deps=[g0, prev_map.get('s1c')])
                g3 = S.add("vector", lambda v: v.reduce_sum(out=s2c[:], in_=sqt[:], axis=AX.X), deps=[g1, g2])
                g4 = S.add("vector", lambda v: v.tensor_scalar(out=muc[:], in0=s1c[:], scalar1=1.0 / 256, scalar2=None, op0=ALU.mult), deps=[g3])
                g5 = S.add("vector", lambda v: v.tensor_tensor(out=m2c[:], in0=muc[:], in1=muc[:], op=ALU.mult), deps=[g4])
                g6 = S.add("vector", lambda v: v.tensor_scalar(out=varc[:], in0=s2c[:], scalar1=1.0 / 256, scalar2=1e-5, op0=ALU.mult, op1=ALU.add), deps=[g5])
                g7 = S.add("vector", lambda v: v.tensor_sub(out=varc[:], in0=varc[:], in1=m2c[:]), deps=[g6])
                g8 = S.add("scalar", lambda s: s.activation(out=sqc[:], in_=varc[:], func=AF.Sqrt), deps=[g7])
                g9 = S.add("vector", lambda v: v.reciprocal(out=rsc[:], in_=sqc[:]), deps=[g8])
                g10 = S.add("vector", lambda v: v.tensor_scalar(out=nmt[:], in0=ot_sb[:],
                            scalar1=muc[:], scalar2=rsc[:], op0=ALU.subtract, op1=ALU.mult), deps=[g9, prev_map.get('nmt')])
                prev_map['ot'] = g10
                g11 = S.add("vector", lambda v, tt=tt: v.tensor_tensor(out=gtd[:], in0=nmt[:],
                            in1=g_sb[:, tt * 256:(tt + 1) * 256], op=ALU.mult), deps=[g10, ldg, prev_map.get('gtd')])
                gn_done_prev[ts4] = g10
                prev_map['sqt'] = g3; prev_map['s1c'] = g4; prev_map['nmt'] = g11
                trs = []
                for half in range(2):
                    hb = 0
                    tr = S.add("tensor", lambda t, half=half, hb=hb: t.transpose(
                        out=ps_tr[hb][:, 0:128], in_=gtd[:, half * 128:(half + 1) * 128], identity=id_sb[:]),
                        deps=[g11, ld_id, trev_prev])
                    ev = S.add("vector", lambda v, half=half, hb=hb, tt=tt: v.tensor_copy(
                        out=gfm_sb[:, half * T + tt * 128: half * T + (tt + 1) * 128], in_=ps_tr[hb][:, 0:128]),
                        deps=[tr, prev_map.get('gfm%d' % ((tt * 2 + half) % 4))])
                    prev_map['gfm%d' % ((tt * 2 + half) % 4)] = ev
                    trev_prev = ev
                    trs.append(ev)
                prev_map['gtd'] = trs[0]
        if io.get("gfm_3d"):
            sto0 = S.dma("sync", o_gfm(m)[0:128],
                         gfm_sb[:, 0:T].rearrange("p (s t) -> p s t", s=2), deps=[trev_prev])
            sto1 = S.dma("sync", o_gfm(m)[128:256],
                         gfm_sb[:, T:2 * T].rearrange("p (s t) -> p s t", s=2), deps=[trev_prev])
        else:
            sto0 = S.dma("sync", o_gfm(m)[0:128, :], gfm_sb[:, 0:T], deps=[trev_prev])
            sto1 = S.dma("sync", o_gfm(m)[128:256, :], gfm_sb[:, T:2 * T], deps=[trev_prev])
        outs += [sto0, sto1]
        prev_map.update({'q': sto0, 'k': sto0, 'v': sto0, 'g': sto0, 'ret0': sto0,
                         'lndc': bfull, 'Bfull': sto0, 'dpow': sto0, 'trev': sto1,
                         'Bv0': sto0, 'Bv1': sto0, 'Bv2': sto0, 'Bv3': sto0})
        prev_map['trev'] = sto1

    return outs



GELU_A = 0.044715
FFN_ROUNDS = 4
GELU_2C = 2.0 * 0.7978845608028654
BIGC = float(2 ** 20)


def build_l0ffn():
    nc = bass.Bass("TRN2", target_bir_lowering=False, debug=False)
    io = {}
    io["x"] = nc.dram_tensor("x", [TOWN, D], F32, kind="ExternalInput").ap()
    io["gfm"] = nc.dram_tensor("gfm", [2048, TOWN], BF16, kind="ExternalInput").ap()
    io["wproj"] = nc.dram_tensor("wproj", [2048, 1024], F32, kind="ExternalInput").ap()
    io["wgr"] = nc.dram_tensor("wgr", [4, 1024, 4], F32, kind="ExternalInput").ap()
    io["bg"] = nc.dram_tensor("bg", [4, 4], F32, kind="ExternalInput").ap()
    io["w1"] = nc.dram_tensor("w1", [4, 1024, 4096], F32, kind="ExternalInput").ap()
    io["b1t"] = nc.dram_tensor("b1t", [128, 128], F32, kind="ExternalInput").ap()
    io["w2"] = nc.dram_tensor("w2", [4, 4096, 1024], F32, kind="ExternalInput").ap()
    io["b2"] = nc.dram_tensor("b2", [4, 1024], F32, kind="ExternalInput").ap()
    io["identb"] = nc.dram_tensor("identb", [128, 128], BF16, kind="ExternalInput").ap()
    io["ones128"] = nc.dram_tensor("ones128", [1, 128], F32, kind="ExternalInput").ap()
    io["x1"] = nc.dram_tensor("x1", [TOWN, D], F32, kind="ExternalOutput").ap()
    io["xn1fm"] = nc.dram_tensor("xn1fm", [1024, TOWN], BF16, kind="ExternalOutput").ap()
    io["w1b"] = nc.dram_tensor("w1b", [4, 1024, 4096], BF16, kind="Internal").ap()
    io["w2b"] = nc.dram_tensor("w2b", [4, 4096, 1024], BF16, kind="Internal").ap()
    io["xpa"] = nc.dram_tensor("xpa", [TOWN, D], F32, kind="Internal").ap()
    io["updsum"] = nc.dram_tensor("updsum", [TOWN, D], F32, kind="Internal").ap()
    es = contextlib.ExitStack()
    S = Sched()
    outs = emit_l0ffn(nc, S, es, io, [])
    S.emit(nc, es, final_waits=outs)
    es.close()
    return nc


def emit_l0ffn(nc, S, es, io, bar, pfx="f_", es_early=None, bar_early=None, es_ln_ext=None):
    if es_early is None:
        es_early = es
    if bar_early is None:
        bar_early = bar
    i_x = io["x"]; i_gfm = io["gfm"]; i_wp = io["wproj"]
    i_wgr = io["wgr"]; i_bg = io["bg"]; i_w1 = io["w1"]; i_b1 = io["b1t"]
    i_w2 = io["w2"]; i_b2 = io["b2"]; i_idb = io["identb"]; i_ones = io["ones128"]
    o_x1 = io["x1"]; o_xn1 = io["xn1fm"]
    s_w1b = io["w1b"]; s_w2b = io["w2b"]; s_xpa = io["xpa"]; s_upd = io["updsum"]
    sb = lambda nme, s, d=F32: es.enter_context(nc.sbuf_tensor(pfx + nme, s, d))
    psum = lambda nme, s, d=F32: es.enter_context(nc.psum_tensor(pfx + nme, s, d))
    outs = []

    standalone = es_early is es
    cur_fm = es_early.enter_context(nc.sbuf_tensor(pfx + "cur_fm", [128, 8 * TOWN], BF16))
    id_sb = es_early.enter_context(nc.sbuf_tensor(pfx + "id_sb", [128, 128], BF16))
    ones_sb = es_early.enter_context(nc.sbuf_tensor(pfx + "ones_sb", [1, 128], F32))
    ps_ct = es_early.enter_context(nc.psum_tensor(pfx + "ps_ct", [128, 1024], BF16))
    es_ln = es_ln_ext if es_ln_ext is not None else contextlib.ExitStack()
    sbE = lambda nme, s, d=F32: es_ln.enter_context(nc.sbuf_tensor(pfx + nme, s, d))
    ld_id = S.dma("sync", id_sb[:], i_idb, deps=bar_early)
    ld_ones = S.dma("sync", ones_sb[:], i_ones, deps=bar_early)
    es2 = contextlib.ExitStack()
    sb2 = lambda nme, s, d=F32: es2.enter_context(nc.sbuf_tensor(pfx + nme, s, d))

    wc = sbE("wc", [128, 4096], BF16)
    cast_done = {}
    prev_c = [None, None]
    ci = 0
    for e in range(4):
        toks = []
        for hc in range(8):
            l1 = S.dma("gpsimd", wc[:], i_w1[e, hc * 128:(hc + 1) * 128, :],
                       deps=[prev_c[0]] + bar_early)
            st = S.dma("gpsimd", s_w1b[e, hc * 128:(hc + 1) * 128, :], wc[:],
                       deps=[l1])
            prev_c[0] = st
            toks.append(st)
        cast_done[('w1', e)] = toks[-1]
        toks = []
        for fc8 in range(8):
            l1 = S.dma("gpsimd", wc[:].rearrange("p (s d) -> p s d", d=1024),
                       i_w2[e, fc8 * 512:(fc8 + 1) * 512, :].rearrange("(s p) d -> p s d", p=128),
                       deps=[prev_c[0]] + bar_early)
            st = S.dma("gpsimd", s_w2b[e, fc8 * 512:(fc8 + 1) * 512, :].rearrange("(s p) d -> p s d", p=128),
                       wc[:].rearrange("p (s d) -> p s d", d=1024),
                       deps=[l1])
            prev_c[0] = st
            toks.append(st)
        cast_done[('w2', e)] = toks[-1]

    xt_sb = [sbE("xt_sb0", [128, 1024])]
    xnt_sb = sbE("xnt_sb", [128, 1024])
    xnb_sb = sbE("xnb_sb", [128, 1024], BF16)
    x2p = sbE("x2p", [128, 1024])
    s1p = sbE("s1p", [128, 1]); s2p = sbE("s2p", [128, 1]); mup = sbE("mup", [128, 1])
    m2p = sbE("m2p", [128, 1]); varp = sbE("varp", [128, 1]); sqp = sbE("sqp", [128, 1]); rsp = sbE("rsp", [128, 1])
    xpa_st = x2p
    prev_p = {}
    xpa_stos = []
    cur_init = []
    for i in range(NT):
        b = 0
        lx = S.dma("sync", xt_sb[b][:], i_x[i * 128:(i + 1) * 128, :], deps=prev_p.get(('x', b), []) + bar_early)
        c1 = S.add("scalar", lambda s, b=b: s.activation(out=x2p[:], in_=xt_sb[b][:], func=AF.Square),
                   deps=[lx, prev_p.get('x2p')])
        c2 = S.add("vector", lambda v, b=b: v.reduce_sum(out=s1p[:], in_=xt_sb[b][:], axis=AX.X),
                   deps=[lx, prev_p.get('s1p')])
        c3 = S.add("vector", lambda v: v.reduce_sum(out=s2p[:], in_=x2p[:], axis=AX.X), deps=[c1, c2])
        c4 = S.add("vector", lambda v: v.tensor_scalar(out=mup[:], in0=s1p[:], scalar1=1.0 / 1024, scalar2=None, op0=ALU.mult), deps=[c3])
        c5 = S.add("vector", lambda v: v.tensor_tensor(out=m2p[:], in0=mup[:], in1=mup[:], op=ALU.mult), deps=[c4])
        c6 = S.add("vector", lambda v: v.tensor_scalar(out=varp[:], in0=s2p[:], scalar1=1.0 / 1024, scalar2=1e-5, op0=ALU.mult, op1=ALU.add), deps=[c5])
        c7 = S.add("vector", lambda v: v.tensor_sub(out=varp[:], in0=varp[:], in1=m2p[:]), deps=[c6])
        c8 = S.add("scalar", lambda s: s.activation(out=sqp[:], in_=varp[:], func=AF.Sqrt), deps=[c7, prev_p.get('sqp')])
        c9 = S.add("vector", lambda v: v.reciprocal(out=rsp[:], in_=sqp[:]), deps=[c8])
        c10 = S.add("vector", lambda v, b=b: v.tensor_scalar(out=xnt_sb[:], in0=xt_sb[b][:],
                    scalar1=mup[:], scalar2=rsp[:], op0=ALU.subtract, op1=ALU.mult),
                    deps=[c9, prev_p.get('xnt')])
        prev_p['s1p'] = c4; prev_p['sqp'] = c9
        cb10 = S.add("vector", lambda v: v.tensor_copy(out=xnb_sb[:], in_=xnt_sb[:]), deps=[c10, prev_p.get('xnb')])
        a0 = S.add("vector", lambda v, b=b: v.tensor_tensor(out=xpa_st[:], in0=xnt_sb[:], in1=xt_sb[b][:],
                   op=ALU.add), deps=[c10, prev_p.get('xpa')])
        prev_p['xnt'] = a0
        sto = S.dma("sync", s_xpa[i * 128:(i + 1) * 128, :], xpa_st[:], deps=[a0])
        prev_p['xpa'] = sto
        prev_p['x2p'] = sto
        prev_p[('x', b)] = [sto, c1]
        xpa_stos.append(sto)
        for c in range(8):
            tr = S.add("tensor", lambda t, c=c: t.transpose(
                out=ps_ct[:, 0:128], in_=xnb_sb[:, c * 128:(c + 1) * 128], identity=id_sb[:]),
                deps=[cb10, ld_id, prev_p.get('trx')])
            cc = S.add("vector", lambda v, c=c, i=i: v.tensor_copy(
                out=cur_fm[:, c * TOWN + i * 128: c * TOWN + (i + 1) * 128], in_=ps_ct[:, 0:128]),
                deps=[tr])
            prev_p['trx'] = cc
            cur_init.append(cc)
        prev_p['xnb'] = prev_p['trx']

    es2b = contextlib.ExitStack()
    sb2b = lambda nme, s, d=F32: es2b.enter_context(nc.sbuf_tensor(pfx + nme, s, d))
    ps2b = lambda nme, s, d=F32: es2b.enter_context(nc.psum_tensor(pfx + nme, s, d))
    wp_sb = sb2b("wp_sb", [128, 16 * 1024], BF16)
    stg_p = [sb2b(f"stg_p{i}", [128, 512]) for i in range(2)]
    ps_pj = [ps2b(f"ps_pj{i}", [128, 512]) for i in range(2)]
    ld_wp = [S.dma("gpsimd", wp_sb[:, vc * 1024:(vc + 1) * 1024], i_wp[vc * 128:(vc + 1) * 128, :], deps=bar)
             for vc in range(16)]
    if "goff" in io:
        gt4_sb = [sb2b(f"gt4_sb{i}", [128, 16 * 512], BF16) for i in range(2)]
        goff = io["goff"]
    else:
        gt_sb = [sb2b(f"gt_sb{i}", [128, 16 * 128], BF16) for i in range(2)]
    for i in range(NT):
        b = i % 2
        if "goff" in io:
            gb = (i // 4) % 2
            if i % 4 == 0:
                lgblk = S.dma("sync", gt4_sb[gb][:].rearrange("p (c t) -> p c t", t=512),
                              lambda i=i: i_gfm[:, bass.ds(goff[0] + (i // 4) * 512, 512)].rearrange("(c p) t -> p c t", p=128),
                              deps=[prev_p.get(('g4', gb)), io.get("goff_dep")] + bar)
                prev_p[('lg4', gb)] = lgblk
            lg = prev_p[('lg4', gb)]
            gt_cur = gt4_sb[gb]
            gt_off = lambda vc, i=i, gt_cur=gt_cur: gt_cur[:, vc * 512 + (i % 4) * 128: vc * 512 + (i % 4) * 128 + 128]
        else:
            lg = S.dma("sync", gt_sb[b][:].rearrange("p (c t) -> p c t", t=128),
                       i_gfm[:, i * 128:(i + 1) * 128].rearrange("(c p) t -> p c t", p=128),
                       deps=[prev_p.get(('g', b))] + bar)
            gt_b = gt_sb[b]
            gt_off = lambda vc, gt_b=gt_b: gt_b[:, vc * 128:(vc + 1) * 128]
        for dc in range(2):
            pb = (i * 2 + dc) % 2
            mms = [S.add("tensor", lambda t, vc=vc, dc=dc, pb=pb, gt_off=gt_off: t.matmul(
                out=ps_pj[pb][:], lhsT=gt_off(vc),
                rhs=wp_sb[:, vc * 1024 + dc * 512: vc * 1024 + (dc + 1) * 512],
                start=(vc == 0), stop=(vc == 15)),
                deps=[lg, ld_wp[vc]] + ([prev_p.get(('pp', pb))] if vc == 0 else []))
                for vc in range(16)]
            ev = S.add("vector", lambda v, pb=pb: v.tensor_copy(out=stg_p[pb][:], in_=ps_pj[pb][:]),
                       deps=[mms[-1], prev_p.get(('stg', pb))])
            acc = S.dma_acc("gpsimd", s_xpa[i * 128:(i + 1) * 128, dc * 512:(dc + 1) * 512],
                            stg_p[pb][:], ALU.add, deps=[ev, xpa_stos[i]])
            prev_p[('pp', pb)] = ev
            prev_p[('stg', pb)] = acc
            prev_p[('xpa_acc', i, dc)] = acc
        prev_p[('g', b)] = prev_p[('stg', 0)]
        if "goff" in io and i % 4 == 3:
            prev_p[('g4', (i // 4) % 2)] = prev_p[('stg', (i * 2 + 1) % 2)]
    xpa_stos = [prev_p[('xpa_acc', i, dc)] for i in range(NT) for dc in range(2)]
    xpa_accs = {i: [prev_p[('xpa_acc', i, 0)], prev_p[('xpa_acc', i, 1)]] for i in range(NT)}
    es2b.close()
    ln_end = [prev_p['xpa'], cur_init[-1], cast_done[('w1', 3)], cast_done[('w2', 3)]]
    es_ln.close()
    bar = bar + ln_end
    es2.close()

    ld_cur = None
    hmid = sb("hmid", [128, 32 * 1024], BF16)
    w2res = sb("w2res", [128, 32 * 1024], BF16)
    w1blk = [sb(f"w1blk{i}", [128, 8 * 512], BF16) for i in range(2)]
    upd = sb("upd", [128, 8 * 1024], BF16)
    wgr_sb = sb("wgr_sb", [128, 4 * 8 * 4], BF16)
    bgr_sb = sb("bgr_sb", [1, 16], BF16)
    b1_sb = sb("b1_sb", [128, 4 * 32])
    b2_sb = sb("b2_sb", [1, 1024], BF16)
    onesb_sb = sb("onesb_sb", [1, 128], BF16)
    xg = sb("xg", [128, 512])
    tg2 = sb("tg2", [128, 512])
    lgt = sb("lgt", [128, 4])
    io4 = sb("io4", [128, 4])
    io4b = sb("io4b", [128, 4])
    m1c = sb("m1c", [128, 1]); m2cg = sb("m2cg", [128, 1])
    eqt = sb("eqt", [128, 4]); cnd = sb("cnd", [128, 4])
    i1c = sb("i1c", [128, 1]); i2c = sb("i2c", [128, 1])
    oh1 = sb("oh1", [128, 4]); oh2 = sb("oh2", [128, 4])
    lg2 = sb("lg2", [128, 4])
    w1gc = sb("w1gc", [128, 1]); w2gc = sb("w2gc", [128, 1])
    cwt1 = sb("cwt1", [128, 4]); cwt2 = sb("cwt2", [128, 4])
    cw_all = sb("cw_all", [128, 16 * 4])
    us_sb = sb("us_sb", [128, 1024])
    xn1t = sb("xn1t", [128, 1024], BF16)
    ps_h = [psum(f"ps_h{i}", [128, 512]) for i in range(2)]
    ps_y = [psum(f"ps_y{i}", [128, 512]) for i in range(4)]
    ps_g = psum("ps_g", [128, 512])

    ld_wgr = S.dma("gpsimd", wgr_sb[:].rearrange("p (r hc e) -> p r hc e", hc=8, e=4),
                   i_wgr.rearrange("r (hc p) e -> p r hc e", p=128), deps=bar)
    ld_bgr = S.dma("gpsimd", bgr_sb[:].rearrange("a (r e) -> a r e", e=4), i_bg.rearrange("(o r) e -> o r e", o=1),
                   deps=bar)
    ld_b1 = S.dma("sync", b1_sb[:], i_b1, deps=bar)

    cob = S.add("vector", lambda v: v.tensor_copy(out=onesb_sb[:], in_=ones_sb[:]), deps=[ld_ones])
    iog1 = S.add("gpsimd", lambda g: g.iota(out=io4[:], pattern=[[1, 4]], base=0, channel_multiplier=0,
                                            allow_small_or_imprecise_dtypes=True), deps=bar)
    iog2 = S.add("gpsimd", lambda g: g.iota(out=io4b[:], pattern=[[1, 4]], base=int(BIGC), channel_multiplier=0,
                                            allow_small_or_imprecise_dtypes=True), deps=bar)

    prev_f = {'curdep': cur_init}
    upddeps = {}
    for r in range(FFN_ROUNDS):
        cw_deps = []
        for gt in range(NT):
            mmg = [S.add("tensor", lambda t, hc=hc, gt=gt, r=r: t.matmul(
                out=ps_g[:, 0:4], lhsT=cur_fm[:, hc * TOWN + gt * 128: hc * TOWN + (gt + 1) * 128],
                rhs=wgr_sb[:, r * 32 + hc * 4: r * 32 + (hc + 1) * 4],
                start=(hc == 0), stop=False),
                deps=[ld_wgr, ld_cur] + ([prev_f.get('psg')] if hc == 0 else []) + prev_f.get('curdep', []))
                for hc in range(8)]
            mgb = S.add("tensor", lambda t, r=r: t.matmul(
                out=ps_g[:, 0:4], lhsT=onesb_sb[:], rhs=bgr_sb[:, r * 4:(r + 1) * 4],
                start=False, stop=True), deps=[ld_bgr, cob])
            e0 = S.add("vector", lambda v: v.tensor_copy(out=lgt[:], in_=ps_g[:, 0:4]), deps=[mgb, prev_f.get('lgt')])
            prev_f['psg'] = e0
            e1 = S.add("vector", lambda v: v.reduce_max(out=m1c[:], in_=lgt[:], axis=AX.X), deps=[e0, prev_f.get('m1c')])
            e2 = S.add("vector", lambda v: v.tensor_scalar(out=eqt[:], in0=lgt[:], scalar1=m1c[:], scalar2=None, op0=ALU.is_ge), deps=[e1, prev_f.get('eqt')])
            e3 = S.add("vector", lambda v: v.scalar_tensor_tensor(out=cnd[:], in0=eqt[:], scalar=-BIGC, in1=io4b[:], op0=ALU.mult, op1=ALU.add), deps=[e2, iog2, prev_f.get('cnd')])
            e4 = S.add("vector", lambda v: v.tensor_reduce(out=i1c[:], in_=cnd[:], axis=AX.X, op=ALU.min), deps=[e3, prev_f.get('i1c')])
            e6 = S.add("vector", lambda v: v.tensor_scalar(out=oh1[:], in0=io4[:], scalar1=i1c[:], scalar2=None, op0=ALU.is_equal), deps=[e4, iog1, prev_f.get('oh1')])
            e7 = S.add("vector", lambda v: v.scalar_tensor_tensor(out=lg2[:], in0=oh1[:], scalar=-BIGC, in1=lgt[:], op0=ALU.mult, op1=ALU.add), deps=[e6, prev_f.get('lg2')])
            e8 = S.add("vector", lambda v: v.reduce_max(out=m2cg[:], in_=lg2[:], axis=AX.X), deps=[e7, prev_f.get('m2cg')])
            e9 = S.add("vector", lambda v: v.tensor_scalar(out=eqt[:], in0=lg2[:], scalar1=m2cg[:], scalar2=None, op0=ALU.is_ge), deps=[e8, e3])
            e10 = S.add("vector", lambda v: v.scalar_tensor_tensor(out=cnd[:], in0=eqt[:], scalar=-BIGC, in1=io4b[:], op0=ALU.mult, op1=ALU.add), deps=[e9, e4])
            e11 = S.add("vector", lambda v: v.tensor_reduce(out=i2c[:], in_=cnd[:], axis=AX.X, op=ALU.min), deps=[e10, prev_f.get('i2c')])
            e13 = S.add("vector", lambda v: v.tensor_scalar(out=oh2[:], in0=io4[:], scalar1=i2c[:], scalar2=None, op0=ALU.is_equal), deps=[e11, prev_f.get('oh2')])
            e14 = S.add("vector", lambda v: v.tensor_tensor(out=m2cg[:], in0=m1c[:], in1=m2cg[:], op=ALU.subtract), deps=[e8, e13, e9])
            e15 = S.add("scalar", lambda s: s.activation(out=w1gc[:], in_=m2cg[:], func=AF.Sigmoid), deps=[e14, prev_f.get('w1g')])
            e16 = S.add("vector", lambda v: v.tensor_scalar(out=w2gc[:], in0=w1gc[:], scalar1=-1.0, scalar2=1.0, op0=ALU.mult, op1=ALU.add), deps=[e15, prev_f.get('w2g')])
            e17 = S.add("vector", lambda v: v.tensor_scalar(out=cwt1[:], in0=oh1[:], scalar1=w1gc[:], scalar2=None, op0=ALU.mult), deps=[e16, e7, prev_f.get('cwt1')])
            e18 = S.add("vector", lambda v: v.tensor_scalar(out=cwt2[:], in0=oh2[:], scalar1=w2gc[:], scalar2=None, op0=ALU.mult), deps=[e16, e13, prev_f.get('cwt2')])
            e19 = S.add("vector", lambda v, gt=gt: v.tensor_tensor(out=cw_all[:, gt * 4:(gt + 1) * 4], in0=cwt1[:], in1=cwt2[:], op=ALU.add), deps=[e17, e18, prev_f.get('cw%d' % gt)])
            prev_f['oh1'] = e17; prev_f['oh2'] = e18; prev_f['w1g'] = e16; prev_f['w2g'] = e17
            prev_f['m1c'] = e17; prev_f['m2cg'] = e15; prev_f['eqt'] = e10; prev_f['cnd'] = e11
            prev_f['i1c'] = e6; prev_f['i2c'] = e13; prev_f['lg2'] = e9
            prev_f['lgt'] = e9; prev_f['cwt1'] = e19; prev_f['cwt2'] = e19
            prev_f['cw%d' % gt] = e19
            cw_deps.append(e19)
        for tblock in range(2):
            upd_dep = {}
            for e in range(4):
                ld_w2r = S.dma("scalar", w2res[:].rearrange("p (fc d) -> p fc d", d=1024),
                               s_w2b[e].rearrange("(fc p) d -> p fc d", p=128),
                               deps=[cast_done[('w2', e)]] + prev_f.get('w2res', []))
                ld_b2 = S.dma("gpsimd", b2_sb[:], i_b2[e:e + 1, :], deps=prev_f.get('b2use', []) + bar)
                hm_last = {}
                for fb in range(8):
                    wb_ = fb % 2
                    lw1 = S.dma("sync", w1blk[wb_][:].rearrange("p (hc f) -> p hc f", f=512),
                                s_w1b[e][:, fb * 512:(fb + 1) * 512].rearrange("(hc p) f -> p hc f", p=128),
                                deps=[cast_done[('w1', e)]] + prev_f.get(('w1b', wb_), []))
                    w1_readers = []
                    for fc4 in range(4):
                        fc = fb * 4 + fc4
                        b1col = b1_sb[:, e * 32 + fc: e * 32 + fc + 1]
                        for tc in range(2):
                            pb = (fc * 2 + tc) % 2
                            tcol = tblock * 1024 + tc * 512
                            mms = [S.add("tensor", lambda t, hc=hc, fc4=fc4, tcol=tcol, pb=pb, wb_=wb_: t.matmul(
                                out=ps_h[pb][:], lhsT=w1blk[wb_][:, hc * 512 + fc4 * 128: hc * 512 + (fc4 + 1) * 128],
                                rhs=cur_fm[:, hc * TOWN + tcol: hc * TOWN + tcol + 512],
                                start=(hc == 0), stop=(hc == 7)),
                                deps=[lw1, ld_cur] + ([prev_f.get(('psh', pb))] if hc == 0 else []) + prev_f.get('curdep', []))
                                for hc in range(8)]
                            w1_readers.append(mms[-1])
                            g1 = S.add("scalar", lambda s, pb=pb, b1col=b1col: s.activation(
                                out=xg[:], in_=ps_h[pb][:], func=AF.Identity, bias=b1col),
                                deps=[mms[-1], ld_b1, prev_f.get('xg')])
                            prev_f[('psh', pb)] = g1
                            g2 = S.add("gpsimd", lambda g: g.tensor_tensor(out=tg2[:], in0=xg[:], in1=xg[:], op=ALU.mult), deps=[g1, prev_f.get('tg2')])
                            g3 = S.add("vector", lambda v: v.tensor_scalar(out=tg2[:], in0=tg2[:], scalar1=GELU_A, scalar2=1.0, op0=ALU.mult, op1=ALU.add), deps=[g2])
                            g4 = S.add("vector", lambda v: v.tensor_tensor(out=tg2[:], in0=tg2[:], in1=xg[:], op=ALU.mult), deps=[g3])
                            g5 = S.add("scalar", lambda s: s.activation(out=tg2[:], in_=tg2[:], func=AF.Sigmoid, scale=GELU_2C), deps=[g4])
                            g6 = S.add("vector", lambda v, fc=fc, tc=tc: v.tensor_tensor(
                                out=hmid[:, fc * 1024 + tc * 512: fc * 1024 + (tc + 1) * 512],
                                in0=xg[:], in1=tg2[:], op=ALU.mult), deps=[g5, prev_f.get(('hm', fc, tc))])
                            prev_f['xg'] = g6; prev_f['tg2'] = g6
                            prev_f[('hm', fc, tc)] = g6
                            hm_last[(fc, tc)] = g6
                    prev_f[('w1b', wb_)] = w1_readers
                for tg in range(2):
                    for tt4 in range(4):
                        lt = tg * 4 + tt4
                        gt = tblock * 8 + lt
                        for dc in range(2):
                            pi = (lt * 2 + dc) % 4
                            pybank = ps_y[pi]
                            mms = [S.add("tensor", lambda t, fc=fc, lt=lt, dc=dc, pybank=pybank: t.matmul(
                                out=pybank[:], lhsT=hmid[:, fc * 1024 + lt * 128: fc * 1024 + (lt + 1) * 128],
                                rhs=w2res[:, fc * 1024 + dc * 512: fc * 1024 + (dc + 1) * 512],
                                start=(fc == 0), stop=False),
                                deps=[ld_w2r, hm_last[(fc, lt // 4)]] + ([prev_f.get(('psy', pi))] if fc == 0 else []))
                                for fc in range(32)]
                            myb = S.add("tensor", lambda t, dc=dc, pybank=pybank: t.matmul(
                                out=pybank[:], lhsT=onesb_sb[:], rhs=b2_sb[:, dc * 512:(dc + 1) * 512],
                                start=False, stop=True), deps=[ld_b2, cob])
                            ui = lt * 2 + dc
                            if e == 0:
                                ev = S.add("vector", lambda v, pybank=pybank, ui=ui, gt=gt, e=e: v.tensor_scalar(
                                    out=upd[:, ui * 512:(ui + 1) * 512], in0=pybank[:],
                                    scalar1=cw_all[:, gt * 4 + e: gt * 4 + e + 1], scalar2=None, op0=ALU.mult),
                                    deps=[myb, cw_deps[gt], upd_dep.get(ui)] + upddeps.get(ui, []))
                            else:
                                ev = S.add("vector", lambda v, pybank=pybank, ui=ui, gt=gt, e=e: v.scalar_tensor_tensor(
                                    out=upd[:, ui * 512:(ui + 1) * 512], in0=pybank[:],
                                    scalar=cw_all[:, gt * 4 + e: gt * 4 + e + 1],
                                    in1=upd[:, ui * 512:(ui + 1) * 512], op0=ALU.mult, op1=ALU.add),
                                    deps=[myb, cw_deps[gt], upd_dep.get(ui)])
                            upd_dep[ui] = ev
                            upddeps[ui] = []
                            prev_f[('psy', pi)] = ev
                prev_f['w2res'] = [upd_dep[15]]
                prev_f['b2use'] = [upd_dep[15]]
            curdeps = []
            for lt in range(8):
                gt = tblock * 8 + lt
                if r > 0:
                    lus = S.dma("sync", us_sb[:], s_upd[gt * 128:(gt + 1) * 128, :],
                                deps=prev_f.get('usdep', []))
                    a1 = S.add("vector", lambda v, lt=lt: v.tensor_tensor(
                        out=us_sb[:], in0=us_sb[:],
                        in1=upd[:, lt * 1024:(lt + 1) * 1024], op=ALU.add),
                        deps=[lus, upd_dep[lt * 2], upd_dep[lt * 2 + 1]])
                else:
                    a1 = S.add("vector", lambda v, lt=lt: v.tensor_copy(
                        out=us_sb[:], in_=upd[:, lt * 1024:(lt + 1) * 1024]),
                        deps=[upd_dep[lt * 2], upd_dep[lt * 2 + 1]] + prev_f.get('usdep', []))
                upddeps.setdefault(lt * 2, []).append(a1)
                upddeps.setdefault(lt * 2 + 1, []).append(a1)
                if r < FFN_ROUNDS - 1:
                    sus = S.dma("sync", s_upd[gt * 128:(gt + 1) * 128, :], us_sb[:], deps=[a1])
                    prev_f['usdep'] = [sus]
                    for c in range(8):
                        tr = S.add("tensor", lambda t, lt=lt, c=c: t.transpose(
                            out=ps_ct[:, 0:128], in_=upd[:, lt * 1024 + c * 128: lt * 1024 + (c + 1) * 128],
                            identity=id_sb[:]),
                            deps=[upd_dep[lt * 2 + c // 4], ld_id] + prev_f.get('ctdep', []))
                        ad = S.add("vector", lambda v, c=c, gt=gt: v.tensor_tensor(
                            out=cur_fm[:, c * TOWN + gt * 128: c * TOWN + (gt + 1) * 128],
                            in0=cur_fm[:, c * TOWN + gt * 128: c * TOWN + (gt + 1) * 128],
                            in1=ps_ct[:, 0:128], op=ALU.add),
                            deps=[tr])
                        prev_f['ctdep'] = [ad]
                        upddeps.setdefault(lt * 2 + c // 4, []).append(tr)
                        curdeps.append(ad)
                else:
                    a2 = S.dma_acc("gpsimd", us_sb[:], s_xpa[gt * 128:(gt + 1) * 128, :], ALU.add,
                                   deps=xpa_accs[gt] + [a1])
                    prev_f['usdep'] = [a2]
                    sx1 = S.dma("sync", o_x1[gt * 128:(gt + 1) * 128, :], us_sb[:], deps=[a2])
                    outs.append(sx1)
                    l1 = S.add("scalar", lambda s: s.activation(out=xg[:, 0:512], in_=us_sb[:, 0:512], func=AF.Square), deps=[a2, prev_f.get('xg'), prev_f.get('lnx2a')])
                    l1b = S.add("scalar", lambda s: s.activation(out=tg2[:, 0:512], in_=us_sb[:, 512:1024], func=AF.Square), deps=[a2, prev_f.get('tg2'), l1, prev_f.get('lnx2b')])
                    l2 = S.add("vector", lambda v: v.reduce_sum(out=m1c[:], in_=us_sb[:], axis=AX.X), deps=[a2, prev_f.get('m1c')])
                    l3 = S.add("vector", lambda v: v.reduce_sum(out=m2cg[:], in_=xg[:, 0:512], axis=AX.X), deps=[l1, l2, prev_f.get('m2cg')])
                    l3b = S.add("vector", lambda v: v.reduce_sum(out=i1c[:], in_=tg2[:, 0:512], axis=AX.X), deps=[l1b, l3, prev_f.get('i1c')])
                    l4 = S.add("vector", lambda v: v.tensor_tensor(out=m2cg[:], in0=m2cg[:], in1=i1c[:], op=ALU.add), deps=[l3b])
                    l5 = S.add("vector", lambda v: v.tensor_scalar(out=m1c[:], in0=m1c[:], scalar1=1.0 / 1024, scalar2=None, op0=ALU.mult), deps=[l4, l2])
                    l6 = S.add("vector", lambda v: v.tensor_tensor(out=i1c[:], in0=m1c[:], in1=m1c[:], op=ALU.mult), deps=[l5])
                    l7 = S.add("vector", lambda v: v.tensor_scalar(out=m2cg[:], in0=m2cg[:], scalar1=1.0 / 1024, scalar2=None, op0=ALU.mult), deps=[l6])
                    l8 = S.add("vector", lambda v: v.tensor_tensor(out=m2cg[:], in0=m2cg[:], in1=i1c[:], op=ALU.subtract), deps=[l7])
                    l9 = S.add("vector", lambda v: v.tensor_scalar(out=m2cg[:], in0=m2cg[:], scalar1=1e-5, scalar2=None, op0=ALU.add), deps=[l8])
                    l10 = S.add("scalar", lambda s: s.activation(out=i2c[:], in_=m2cg[:], func=AF.Sqrt), deps=[l9, prev_f.get('i2c')])
                    l11 = S.add("vector", lambda v: v.reciprocal(out=i2c[:], in_=i2c[:]), deps=[l10])
                    l12 = S.add("vector", lambda v: v.tensor_scalar(out=xn1t[:], in0=us_sb[:],
                                scalar1=m1c[:], scalar2=i2c[:], op0=ALU.subtract, op1=ALU.mult),
                                deps=[l11, prev_f.get('xn1t')])
                    prev_f['lnx2a'] = l3; prev_f['lnx2b'] = l3b; prev_f['m1c'] = l12; prev_f['m2cg'] = l8
                    prev_f['i1c'] = l8; prev_f['i2c'] = l12; prev_f['xg'] = l3; prev_f['tg2'] = l3b
                    prev_f['x1dep'] = [l12, sx1]
                    for c in range(8):
                        tr = S.add("tensor", lambda t, c=c: t.transpose(
                            out=ps_ct[:, 0:128], in_=xn1t[:, c * 128:(c + 1) * 128], identity=id_sb[:]),
                            deps=[l12, ld_id] + prev_f.get('ctdep', []))
                        stg = S.add("vector", lambda v, c=c: v.tensor_copy(
                            out=upd[:, c * 128:(c + 1) * 128], in_=ps_ct[:, 0:128]),
                            deps=[tr] + upddeps.get(c // 4, []) + prev_f.get(('xn1s', c), []))
                        sto = S.dma("sync", o_xn1[c * 128:(c + 1) * 128, gt * 128:(gt + 1) * 128],
                                    upd[:, c * 128:(c + 1) * 128], deps=[stg])
                        outs.append(sto)
                        prev_f['ctdep'] = [stg]
                        prev_f[('xn1s', c)] = [sto]
                        upddeps.setdefault(c // 4, []).append(sto)
                    prev_f['xn1t'] = prev_f['ctdep'][0]
            if r < FFN_ROUNDS - 1:
                prev_f['curdep'] = curdeps

    return outs



RMS_EPS = 1.1920929e-07


def build_l1attn():
    nc = bass.Bass("TRN2", target_bir_lowering=False, debug=False)
    io = {}
    for nm, shp, dt in (("xn1fm", [1024, T], BF16), ("wq", [1024, 512], F32), ("wk", [1024, 512], F32),
                        ("wv", [1024, 512], F32), ("bqc", [128, 4], F32), ("bkc", [128, 4], F32),
                        ("bvrow", [1, 512], F32), ("fwq", [1024, 512], F32), ("fwk", [1024, 512], F32),
                        ("fwv", [1024, 512], F32), ("fbqc", [128, 4], F32), ("fbkc", [128, 4], F32),
                        ("fbvrow", [1, 512], F32), ("causal", [4, 128, 512], BF16),
                        ("ones128", [1, 128], F32), ("onescol", [128, 1], F32), ("identf", [128, 128], F32)):
        io[nm] = nc.dram_tensor(nm, shp, dt, kind="ExternalInput").ap()
    z1_ = nc.dram_tensor("z1", [8, 64, T], BF16, kind="ExternalOutput").ap()
    z2_ = nc.dram_tensor("z2", [8, 64, T], BF16, kind="ExternalOutput").ap()
    io["z1"] = lambda h: z1_[h]
    io["z2"] = lambda h: z2_[h]
    es = contextlib.ExitStack()
    S = Sched()
    outs = emit_l1attn(nc, S, es, io, [])
    S.emit(nc, es, final_waits=outs)
    es.close()
    return nc


def emit_l1attn(nc, S, es, io, bar, pfx="a_"):
    i_xn = io.get("xn1fm")
    i_wq = io["wq"]; i_wk = io["wk"]; i_wv = io["wv"]
    i_bq = io["bqc"]; i_bk = io["bkc"]; i_bv = io["bvrow"]
    i_fwq = io["fwq"]; i_fwk = io["fwk"]; i_fwv = io["fwv"]
    i_fbq = io["fbqc"]; i_fbk = io["fbkc"]; i_fbv = io["fbvrow"]
    i_caus = io["causal"]; i_ones = io["ones128"]; i_onesc = io["onescol"]; i_idf = io["identf"]
    if "zslice" in io:
        zsl = io["zslice"]
        o_z1 = lambda h, tc: zsl(0, h, tc)
        o_z2 = lambda h, tc: zsl(1, h, tc)
    else:
        _z1 = io["z1"]; _z2 = io["z2"]
        o_z1 = lambda h, tc: _z1(h)[:, tc * 512:(tc + 1) * 512]
        o_z2 = lambda h, tc: _z2(h)[:, tc * 512:(tc + 1) * 512]
    sb = lambda nme, s, d=F32: es.enter_context(nc.sbuf_tensor(pfx + nme, s, d))
    psum = lambda nme, s, d=F32: es.enter_context(nc.psum_tensor(pfx + nme, s, d))
    outs = []
    prev = {}

    xn_sb = sb("xn_sb", [128, 8 * T], BF16)
    s_rep = sb("s_rep", [128, T], BF16)
    s_tok = sb("s_tok", [128, 32])
    s_row = sb("s_row", [1, T])
    wsbs = {}
    for nm in ("wq", "wk", "wv"):
        wsbs[nm] = sb(nm + "_sb", [128, 8 * 512], BF16)
    wsbs["fwq"] = wsbs["wq"]; wsbs["fwk"] = wsbs["wk"]; wsbs["fwv"] = wsbs["wv"]
    bsbs = {}
    for nm, src_, shp in (("bqc", i_bq, [128, 4]), ("bkc", i_bk, [128, 4]), ("bvrow", i_bv, [1, 512]),
                          ("fbqc", i_fbq, [128, 4]), ("fbkc", i_fbk, [128, 4]), ("fbvrow", i_fbv, [1, 512])):
        bsbs[nm] = sb(nm + "_sb", shp)
    caus_sb = sb("caus_sb", [128, 4 * 512], BF16)
    ones_sb = sb("ones_sb", [1, 128])
    onesc_sb = sb("onesc_sb", [128, 1])
    idf_sb = sb("idf_sb", [128, 128])
    q_h = sb("q_h", [64, T], BF16)
    k_h = sb("k_h", [64, T], BF16)
    v_h = sb("v_h", [128, 32 * 64], BF16)
    vaug = sb("vaug", [128, 32 * 65], BF16)
    vp = [sb(f"vp{i}", [128, 64], BF16) for i in range(2)]
    brep = sb("brep", [128, 64])
    stage = sb("stage", [128, T])
    P_sb = [sb(f"P_sb{i}", [128, T], BF16) for i in range(2)]
    Pq = [sb(f"Pq{i}", [128, 512], BF16) for i in range(2)]
    qe1 = sb("qe1", [64, 512])
    mcol = sb("mcol", [128, 1]); tmpc = sb("tmpc", [128, 1]); negm = sb("negm", [128, 1])
    Dcol = sb("Dcol", [128, 1]); rD = sb("rD", [128, 1])
    z_sb = sb("z_sb", [65, T])
    z2_sb = z_sb
    zos = [sb("zos0", [64, 512], BF16)]
    ps_s = [psum(f"ps_s{i}", [128, 512]) for i in range(2)]
    ps_z = [psum(f"ps_z{i}", [128, 512]) for i in range(2)]
    ps_q = [psum(f"ps_q{i}", [128, 512]) for i in range(2)]
    ps_r = psum("ps_r", [128, 512])

    lds = {}
    for nm, src_ in (("wq", i_wq), ("wk", i_wk), ("wv", i_wv)):
        lds[nm] = S.dma("gpsimd", wsbs[nm][:].rearrange("p (hc c) -> p hc c", c=512),
                        src_.rearrange("(hc p) c -> p hc c", p=128), deps=bar)
    fw_srcs = {"fwq": i_fwq, "fwk": i_fwk, "fwv": i_fwv}
    for nm, src_ in (("bqc", i_bq), ("bkc", i_bk), ("bvrow", i_bv), ("fbqc", i_fbq), ("fbkc", i_fbk),
                     ("fbvrow", i_fbv)):
        lds[nm] = S.dma("sync", bsbs[nm][:], src_, deps=bar)
    lds["ones"] = S.dma("sync", ones_sb[:], i_ones, deps=bar)
    lds["onesc"] = S.dma("sync", onesc_sb[:], i_onesc, deps=bar)
    if "xn1fm_parts" in io:
        parts = io["xn1fm_parts"]
        ld_xns = []
        for ap_, col0 in parts:
            w_ = TOWN
            ld_xns.append(S.dma("sync",
                xn_sb[:].rearrange("p (c t) -> p c t", t=T)[:, :, col0:col0 + w_],
                ap_.rearrange("(c p) t -> p c t", p=128), deps=bar + ld_xns[-1:]))
        ld_xn = ld_xns[-1]
    else:
        ld_xn = S.dma("sync", xn_sb[:].rearrange("p (c t) -> p c t", t=T),
                      i_xn.rearrange("(c p) t -> p c t", p=128), deps=bar)
    ld_caus = [S.dma("sync", caus_sb[:, v * 512:(v + 1) * 512], i_caus[v], deps=bar) for v in range(4)]

    srep = []
    for tc in range(8):
        mml = None
        for hc in range(8):
            sq = S.add("scalar", lambda s, hc=hc, tc=tc: s.activation(
                out=stage[:, 0:512], in_=xn_sb[:, hc * T + tc * 512: hc * T + (tc + 1) * 512], func=AF.Square),
                deps=[ld_xn, prev.get('sq')])
            mml = S.add("tensor", lambda t, hc=hc: t.matmul(
                out=ps_r[0:1, 0:512], lhsT=onesc_sb[:], rhs=stage[:, 0:512], start=(hc == 0), stop=(hc == 7)),
                deps=[sq, lds["onesc"]] + ([prev.get('psr')] if hc == 0 else []))
            prev['sq'] = mml
        ev = S.add("vector", lambda v, tc=tc: v.tensor_copy(
            out=s_row[0:1, tc * 512:(tc + 1) * 512], in_=ps_r[0:1, 0:512]),
            deps=[mml, prev.get('srow')])
        prev['psr'] = ev; prev['srow'] = ev
        srep.append(ev)
    r1 = S.add("vector", lambda v: v.tensor_scalar(out=s_row[:], in0=s_row[:], scalar1=1.0 / 1024,
               scalar2=RMS_EPS, op0=ALU.mult, op1=ALU.add), deps=srep)
    r2 = S.add("scalar", lambda s: s.activation(out=s_row[:], in_=s_row[:], func=AF.Sqrt), deps=[r1])
    r3 = S.add("vector", lambda v: v.reciprocal(out=s_row[:], in_=s_row[:]), deps=[r2])
    rr = []
    for tc in range(8):
        mm = S.add("tensor", lambda t, tc=tc: t.matmul(
            out=ps_r[:, 0:512], lhsT=ones_sb[:], rhs=s_row[0:1, tc * 512:(tc + 1) * 512],
            start=True, stop=True), deps=[r3, lds["ones"], prev.get('psr')])
        ev = S.add("vector", lambda v, tc=tc: v.tensor_copy(out=s_rep[:, tc * 512:(tc + 1) * 512],
                   in_=ps_r[:, 0:512]), deps=[mm, prev.get('srep2')])
        prev['psr'] = ev; prev['srep2'] = ev
        rr.append(ev)
    mkid = S.dma("sync", idf_sb[:], i_idf, deps=bar)
    for _qt in range(8):
        prev[('stg', _qt)] = prev['sq']
    srep32 = sb("srep32", [128, 128])
    stok_deps = []
    for tt in range(32):
        cp32 = S.add("vector", lambda v, tt=tt: v.tensor_copy(out=srep32[:], in_=s_rep[:, tt * 128:(tt + 1) * 128]),
                     deps=[rr[tt // 4], prev.get('srep32')])
        tr = S.add("tensor", lambda t, tt=tt: t.transpose(
            out=ps_r[:, 0:128], in_=srep32[:], identity=idf_sb[:]),
            deps=[cp32, mkid, prev.get('psr')])
        ev = S.add("vector", lambda v, tt=tt: v.tensor_copy(out=s_tok[:, tt:tt + 1], in_=ps_r[:, 0:1]),
                   deps=[tr, prev.get('stok')])
        prev['srep32'] = tr
        prev['psr'] = ev; prev['stok'] = ev
        stok_deps.append(ev)

    BIGM = float(2 ** 20)

    def proj_fm(dst, wname, bname, hl, use_scale, name):
        dd = []
        W = wsbs[wname]
        bcol = bsbs[bname][(hl % 2) * 64:(hl % 2) * 64 + 64, hl // 2: hl // 2 + 1]
        for tc in range(8):
            pb = tc % 2
            mms = None
            for hc in range(8):
                mms = S.add("tensor", lambda t, hc=hc, tc=tc, pb=pb, hl=hl, W=W: t.matmul(
                    out=ps_q[pb][0:64, 0:512],
                    lhsT=W[:, hc * 512 + hl * 64: hc * 512 + (hl + 1) * 64],
                    rhs=xn_sb[:, hc * T + tc * 512: hc * T + (tc + 1) * 512],
                    start=(hc == 0), stop=(hc == 7)),
                    deps=[lds[wname], ld_xn] + ([prev.get(('psq', pb))] if hc == 0 else []))
            if use_scale:
                e1 = S.add("vector", lambda v, tc=tc, pb=pb: v.tensor_tensor(
                    out=qe1[:], in0=ps_q[pb][0:64, 0:512], in1=s_rep[0:64, tc * 512:(tc + 1) * 512],
                    op=ALU.mult), deps=[mms, rr[tc], prev.get('qe1')])
                e2 = S.add("vector", lambda v, tc=tc, dst=dst, bcol=bcol: v.tensor_scalar(
                    out=dst[:, tc * 512:(tc + 1) * 512], in0=qe1[:], scalar1=bcol, scalar2=None,
                    op0=ALU.add), deps=[e1, lds[bname], prev.get((name, tc))])
                prev['qe1'] = e2
                prev[('psq', pb)] = e1
            else:
                e2 = S.add("vector", lambda v, tc=tc, dst=dst, pb=pb, bcol=bcol: v.tensor_scalar(
                    out=dst[:, tc * 512:(tc + 1) * 512], in0=ps_q[pb][0:64, 0:512], scalar1=bcol,
                    scalar2=None, op0=ALU.add), deps=[mms, lds[bname], prev.get((name, tc))])
                prev[('psq', pb)] = e2
            prev[(name, tc)] = e2
            dd.append(e2)
        return dd

    def proj_tok(wname, bname, hl, use_scale, name):
        mmb = S.add("tensor", lambda t, hl=hl: t.matmul(
            out=ps_r[:, 0:64], lhsT=ones_sb[:], rhs=bsbs[bname][0:1, hl * 64:(hl + 1) * 64],
            start=True, stop=True), deps=[lds[bname], lds["ones"], prev.get('psr')])
        evb = S.add("vector", lambda v: v.tensor_copy(out=brep[:], in_=ps_r[:, 0:64]),
                    deps=[mmb, prev.get('brep')])
        prev['psr'] = evb; prev['brep'] = evb
        dd = []
        W = wsbs[wname]
        for tt in range(32):
            pb = tt % 2
            mms = None
            for hc in range(8):
                mms = S.add("tensor", lambda t, hc=hc, tt=tt, pb=pb, hl=hl, W=W: t.matmul(
                    out=ps_q[pb][:, 0:64],
                    lhsT=xn_sb[:, hc * T + tt * 128: hc * T + (tt + 1) * 128],
                    rhs=W[:, hc * 512 + hl * 64: hc * 512 + (hl + 1) * 64],
                    start=(hc == 0), stop=(hc == 7)),
                    deps=[lds[wname], ld_xn] + ([prev.get(('psq', pb))] if hc == 0 else []))
            sc = s_tok[:, tt:tt + 1] if use_scale else 1.0
            e2 = S.add("vector", lambda v, tt=tt, pb=pb, sc=sc: v.scalar_tensor_tensor(
                out=v_h[:, tt * 64:(tt + 1) * 64], in0=ps_q[pb][:, 0:64],
                scalar=sc, in1=brep[:], op0=ALU.mult, op1=ALU.add),
                deps=[mms, evb, prev.get((name, tt % 4))] + (stok_deps if use_scale else []))
            prev[('psq', pb)] = e2
            prev[(name, tt % 4)] = e2
            dd.append(e2)
        return dd

    for hl in range(8):
        qd = proj_fm(q_h, "wq", "bqc", hl, True, 'q')
        kd = proj_fm(k_h, "wk", "bkc", hl, True, 'k')
        vd = proj_tok("wv", "bvrow", hl, True, 'v')
        mz = S.add("vector", lambda v: v.memset(z_sb[:], 0.0), deps=prev.get('zuse', []) + bar)
        zdeps = []
        for kbp in range(16):
            qt0 = (2 * kbp) // 4
            c0 = qt0 * 512
            for kpar in range(2):
                kb = 2 * kbp + kpar
                scd = []
                for qt in range(qt0, 8):
                    pb = qt % 2
                    mm = S.add("tensor", lambda t, kb=kb, qt=qt, pb=pb: t.matmul(
                        out=ps_s[pb][:], lhsT=k_h[:, kb * 128:(kb + 1) * 128],
                        rhs=q_h[:, qt * 512:(qt + 1) * 512], start=True, stop=True),
                        deps=[kd[kb // 4], qd[qt], prev.get(('pss', pb))])
                    if qt == qt0:
                        vv = kb % 4
                        d1 = S.add("vector", lambda v, qt=qt, pb=pb, vv=vv: v.scalar_tensor_tensor(
                            out=stage[:, qt * 512:(qt + 1) * 512], in0=ps_s[pb][:], scalar=BIGM,
                            in1=caus_sb[:, vv * 512:(vv + 1) * 512], op0=ALU.add, op1=ALU.mult),
                            deps=[mm, ld_caus[vv], prev.get(('stg', qt))])
                        ev = S.add("vector", lambda v, qt=qt: v.tensor_scalar(
                            out=stage[:, qt * 512:(qt + 1) * 512], in0=stage[:, qt * 512:(qt + 1) * 512],
                            scalar1=-BIGM, scalar2=None, op0=ALU.add), deps=[d1])
                    else:
                        ev = S.add("scalar", lambda s, qt=qt, pb=pb: s.activation(
                            out=stage[:, qt * 512:(qt + 1) * 512], in_=ps_s[pb][:], func=AF.Copy),
                            deps=[mm, prev.get(('stg', qt))])
                    prev[('pss', pb)] = ev
                    prev[('stg', qt)] = ev
                    scd.append(ev)
                mrow = S.add("vector", lambda v, c0=c0: v.reduce_max(out=negm[:], in_=stage[:, c0:T], axis=AX.X),
                             deps=scd + [prev.get('negm')])
                ng = S.add("vector", lambda v: v.tensor_scalar(out=negm[:], in0=negm[:], scalar1=-1.0,
                           scalar2=None, op0=ALU.mult), deps=[mrow])
                ex = S.add("scalar", lambda s, c0=c0, kpar=kpar: s.activation(
                    out=P_sb[kpar][:, c0:T], in_=stage[:, c0:T], func=AF.Exp, bias=negm[:], accum_out=Dcol[:]),
                    deps=[ng, prev.get('dcol'), prev.get(('puse', kpar))] + scd)
                for qt in range(qt0, 8):
                    prev[('stg', qt)] = ex
                rd = S.add("vector", lambda v: v.reciprocal(out=rD[:], in_=Dcol[:]), deps=[ex, prev.get('rd')])
                vpo = S.add("vector", lambda v, kb=kb, kpar=kpar: v.tensor_scalar(
                    out=vp[kpar][:], in0=v_h[:, kb * 64:(kb + 1) * 64], scalar1=rD[:], scalar2=None, op0=ALU.mult),
                    deps=[rd, vd[kb // 8], prev.get(('vp', kpar))])
                prev['dcol'] = rd; prev['rd'] = vpo; prev['negm'] = ex
                prev[('ex', kpar)] = ex; prev[('vpo', kpar)] = vpo
            lastav = None
            for qt in range(qt0, 8):
                pb = qt % 2
                for kpar in range(2):
                    lastav = S.add("tensor", lambda t, qt=qt, pb=pb, kpar=kpar: t.matmul(
                        out=ps_z[pb][0:64, 0:512], lhsT=vp[kpar][:], rhs=P_sb[kpar][:, qt * 512:(qt + 1) * 512],
                        start=(kpar == 0), stop=(kpar == 1)),
                        deps=[prev[('vpo', kpar)], prev[('ex', kpar)]] +
                             ([prev.get(('psz', pb))] if kpar == 0 else []))
                za = S.add("vector", lambda v, qt=qt, pb=pb: v.tensor_tensor(
                    out=z_sb[0:64, qt * 512:(qt + 1) * 512], in0=z_sb[0:64, qt * 512:(qt + 1) * 512],
                    in1=ps_z[pb][0:64, 0:512], op=ALU.add),
                    deps=[lastav, mz, prev.get(('za', qt))])
                prev[('psz', pb)] = za
                prev[('za', qt)] = za
                zdeps.append(za)
            prev[('puse', 0)] = lastav; prev[('puse', 1)] = lastav
            prev[('vp', 0)] = lastav; prev[('vp', 1)] = lastav
        zcs = []
        for tc in range(8):
            zb = 0
            zc = S.add("vector", lambda v, tc=tc, zb=zb: v.tensor_copy(
                out=zos[zb][:], in_=z_sb[0:64, tc * 512:(tc + 1) * 512]),
                deps=zdeps[-8:] + [prev.get(('zob', zb))])
            sto = S.dma("sync", o_z1(hl, tc), zos[zb][:], deps=[zc])
            outs.append(sto)
            prev[('zob', zb)] = sto
            zcs.append(zc)
        prev['zuse'] = zcs
        prev['q'] = qd[-1]

    wdone = [prev['q'], prev['zuse'][0]]
    for nm in ("fwq", "fwk", "fwv"):
        lds[nm] = S.dma("gpsimd", wsbs[nm][:].rearrange("p (hc c) -> p hc c", c=512),
                        fw_srcs[nm].rearrange("(hc p) c -> p hc c", p=128), deps=wdone)
    vaug_init = S.add("vector", lambda v: v.memset(vaug[:], 1.0), deps=bar)
    for hl in range(8):
        qd = proj_fm(q_h, "fwq", "fbqc", hl, False, 'fq')
        kd = proj_fm(k_h, "fwk", "fbkc", hl, False, 'fk')
        vd = proj_tok("fwv", "fbvrow", hl, False, 'fv')
        cpv = S.add("vector", lambda v: v.tensor_copy(
            out=vaug[:].rearrange("p (kb c) -> p kb c", c=65)[:, :, 0:64],
            in_=v_h[:].rearrange("p (kb c) -> p kb c", c=64)),
            deps=vd + [vaug_init] + prev.get('vauguse', []))
        mz = S.add("vector", lambda v: v.memset(z2_sb[:], 0.0),
                   deps=prev.get('z2use', []) + prev.get('zuse', []) + bar)
        zdeps = []
        for kb in range(32):
            for qt in range(8):
                pb = qt % 2
                mm = S.add("tensor", lambda t, kb=kb, qt=qt, pb=pb: t.matmul(
                    out=ps_s[pb][:], lhsT=k_h[:, kb * 128:(kb + 1) * 128],
                    rhs=q_h[:, qt * 512:(qt + 1) * 512], start=True, stop=True),
                    deps=[kd[kb // 4], qd[qt], prev.get(('pss', pb))])
                ex = S.add("scalar", lambda s, pb=pb: s.activation(
                    out=Pq[pb][:], in_=ps_s[pb][:], func=AF.Exp, scale=0.125),
                    deps=[mm, prev.get(('pq', pb))])
                prev[('pss', pb)] = ex
                av = S.add("tensor", lambda t, kb=kb, pb=pb: t.matmul(
                    out=ps_z[pb][0:65, 0:512], lhsT=vaug[:, kb * 65:(kb + 1) * 65], rhs=Pq[pb][:],
                    start=True, stop=True),
                    deps=[ex, cpv, prev.get(('psz', pb))])
                za = S.add("vector", lambda v, qt=qt, pb=pb: v.tensor_tensor(
                    out=z2_sb[:, qt * 512:(qt + 1) * 512], in0=z2_sb[:, qt * 512:(qt + 1) * 512],
                    in1=ps_z[pb][0:65, 0:512], op=ALU.add),
                    deps=[av, mz, prev.get(('za2', qt))])
                prev[('pq', pb)] = av
                prev[('psz', pb)] = za
                prev[('za2', qt)] = za
                zdeps.append(za)
        gr = S.add("vector", lambda v: v.reciprocal(out=s_row[:], in_=z2_sb[64:65, :]),
                   deps=zdeps[-8:] + [prev.get('grow')])
        zcs = []
        for tc in range(8):
            zb = 0
            mm = S.add("tensor", lambda t, tc=tc: t.matmul(
                out=ps_r[0:64, 0:512], lhsT=ones_sb[0:1, 0:64], rhs=s_row[0:1, tc * 512:(tc + 1) * 512],
                start=True, stop=True), deps=[gr, lds["ones"], prev.get('psr')])
            zo2 = S.add("vector", lambda v, tc=tc, zb=zb: v.tensor_tensor(
                out=zos[zb][:], in0=ps_r[0:64, 0:512],
                in1=z2_sb[0:64, tc * 512:(tc + 1) * 512], op=ALU.mult),
                deps=[mm, prev.get(('zob', zb))])
            prev['psr'] = zo2
            sto = S.dma("sync", o_z2(hl, tc), zos[zb][:], deps=[zo2])
            outs.append(sto)
            prev[('zob', zb)] = sto
            zcs.append(zo2)
        prev['z2use'] = zcs
        prev['vauguse'] = [prev[('pq', 0)], prev[('pq', 1)]]
        prev['grow'] = zcs[-1]

    return outs



def build_l1fin():
    nc = bass.Bass("TRN2", target_bir_lowering=False, debug=False)
    io = {}
    for nm, shp, dt in (("x1", [TOWN, D], F32), ("xn1fm", [1024, TOWN], BF16), ("z1fm", [1024, TOWN], BF16),
                        ("z2fm", [1024, TOWN], BF16), ("woT", [1024, 1024], F32), ("bo", [1, 1024], F32),
                        ("wgate", [1024, 8], F32), ("bgate", [1, 8], F32), ("wexp", [8, 1024, 1024], F32),
                        ("bexp", [1, 8 * 1024], F32), ("identb", [128, 128], BF16), ("ones128", [1, 128], F32)):
        io[nm] = nc.dram_tensor(nm, shp, dt, kind="ExternalInput").ap()
    io["out"] = nc.dram_tensor("out", [TOWN, D], F32, kind="ExternalOutput").ap()
    es = contextlib.ExitStack()
    S = Sched()
    outs = emit_l1fin(nc, S, es, io, [])
    S.emit(nc, es, final_waits=outs)
    es.close()
    return nc


def emit_l1fin(nc, S, es, io, bar, pfx="n_"):
    i_x1 = io["x1"]; i_xn = io["xn1fm"]
    if "z1get" in io:
        z1get = io["z1get"]; z2get = io["z2get"]
    else:
        i_z1 = io["z1fm"]; i_z2 = io["z2fm"]
        z1get = lambda c: i_z1[c * 128:(c + 1) * 128]
        z2get = lambda c: i_z2[c * 128:(c + 1) * 128]
    dyn = "goff" in io
    i_wo = io["woT"]; i_bo = io["bo"]; i_wg = io["wgate"]; i_bg = io["bgate"]
    i_we = io["wexp"]; i_be = io["bexp"]; i_idb = io["identb"]; i_ones = io["ones128"]
    o_out = io["out"]
    sb = lambda nme, s, d=F32: es.enter_context(nc.sbuf_tensor(pfx + nme, s, d))
    psum = lambda nme, s, d=F32: es.enter_context(nc.psum_tensor(pfx + nme, s, d))
    outs = []
    prev = {}

    xn_sb = sb("xn_sb", [128, 8 * TOWN], BF16)
    macc = sb("macc", [128, 16 * 1024])
    wo_sb = sb("wo_sb", [128, 8 * 1024], BF16)
    we_sb = [sb("we_sb0", [128, 8 * 1024], BF16)]
    bo_sb = sb("bo_sb", [1, 1024], BF16)
    be_sb = sb("be_sb", [1, 1024], BF16)
    bg_sb = sb("bg_sb", [1, 8], BF16)
    wg_sb = sb("wg_sb", [128, 8 * 8], BF16)
    onesb = sb("onesb", [1, 128], BF16)
    ones_sb = sb("ones_sb", [1, 128])
    id_sb = sb("id_sb", [128, 128], BF16)
    if dyn:
        z1res = sb("z1res", [128, 8 * TOWN], BF16)
        z2res = sb("z2res", [128, 8 * TOWN], BF16)
    else:
        z1t = [sb(f"z1t{i}", [128, 8 * 128], BF16) for i in range(2)]
        z2t = [sb(f"z2t{i}", [128, 8 * 128], BF16) for i in range(2)]
    x1t = [sb(f"x1t{i}", [128, 1024]) for i in range(2)]
    lgt = sb("lgt", [128, 8]); exl = sb("exl", [128, 8])
    m1c = sb("m1c", [128, 1]); ssc = sb("ssc", [128, 1]); rsc = sb("rsc", [128, 1])
    io8 = sb("io8", [128, 8]); cnd = sb("cnd", [128, 8])
    i1c = sb("i1c", [128, 1]); i2c = sb("i2c", [128, 1])
    oh1 = sb("oh1", [128, 8]); oh2 = sb("oh2", [128, 8]); lg2 = sb("lg2", [128, 8])
    cw_all = sb("cw_all", [128, 16 * 8])
    ps_a = [psum(f"ps_a{i}", [128, 512]) for i in range(2)]
    ps_t = [psum(f"ps_t{i}", [128, 1024], BF16) for i in range(2)]
    ps_g = psum("ps_g", [128, 512])
    ps_y = [psum(f"ps_y{i}", [128, 512]) for i in range(2)]

    ld_xn = S.dma("sync", xn_sb[:].rearrange("p (c t) -> p c t", t=TOWN),
                  i_xn.rearrange("(c p) t -> p c t", p=128), deps=bar)
    ld_wo = S.dma("gpsimd", wo_sb[:].rearrange("p (c d) -> p c d", d=1024),
                  i_wo.rearrange("(c p) d -> p c d", p=128), deps=bar)
    ld_bo = S.dma("gpsimd", bo_sb[:], i_bo, deps=bar)

    ld_bg = S.dma("gpsimd", bg_sb[:], i_bg, deps=bar)
    ld_wg = S.dma("gpsimd", wg_sb[:].rearrange("p (c e) -> p c e", e=8),
                  i_wg.rearrange("(c p) e -> p c e", p=128), deps=bar)
    ld_ones = S.dma("sync", ones_sb[:], i_ones, deps=bar)
    ld_id = S.dma("sync", id_sb[:], i_idb, deps=bar)
    cob = S.add("vector", lambda v: v.tensor_copy(out=onesb[:], in_=ones_sb[:]), deps=[ld_ones])
    iog = S.add("gpsimd", lambda g: g.iota(out=io8[:], pattern=[[1, 8]], base=0, channel_multiplier=0,
                                           allow_small_or_imprecise_dtypes=True), deps=bar)

    if dyn:
        goff = io["goff"]
        zldep = []
        for c in range(8):
            zldep.append(S.dma("sync", z1res[:, c * TOWN:(c + 1) * TOWN],
                               lambda c=c: z1get(c)[:, bass.ds(goff[0], TOWN)],
                               deps=bar + [io.get("goff_dep")] + zldep[-1:]))
            zldep.append(S.dma("sync", z2res[:, c * TOWN:(c + 1) * TOWN],
                               lambda c=c: z2get(c)[:, bass.ds(goff[0], TOWN)],
                               deps=bar + [io.get("goff_dep")] + zldep[-1:]))

    for i in range(NT):
        b = i % 2
        if dyn:
            lz1 = zldep[-1]
            lz2 = zldep[-1]
            z1sl = lambda c, i=i: z1res[:, c * TOWN + i * 128: c * TOWN + (i + 1) * 128]
            z2sl = lambda c, i=i: z2res[:, c * TOWN + i * 128: c * TOWN + (i + 1) * 128]
        else:
            lz1 = [S.dma("sync", z1t[b][:, c * 128:(c + 1) * 128],
                         z1get(c)[:, i * 128:(i + 1) * 128],
                         deps=prev.get(('z1', b), []) + bar) for c in range(8)][-1]
            lz2 = [S.dma("sync", z2t[b][:, c * 128:(c + 1) * 128],
                         z2get(c)[:, i * 128:(i + 1) * 128],
                         deps=prev.get(('z2', b), []) + bar) for c in range(8)][-1]
            z1sl = lambda c, b=b: z1t[b][:, c * 128:(c + 1) * 128]
            z2sl = lambda c, b=b: z2t[b][:, c * 128:(c + 1) * 128]
        lx1 = S.dma("sync", x1t[b][:], i_x1[i * 128:(i + 1) * 128, :], deps=prev.get(('x1', b), []) + bar)
        ztr = []
        for c in range(8):
            cb = c % 2
            tr = S.add("tensor", lambda t, c=c, cb=cb, z2sl=z2sl: t.transpose(
                out=ps_t[cb][:, 0:128], in_=z2sl(c), identity=id_sb[:]),
                deps=[lz2, ld_id] + prev.get(('pst', cb), []))
            ad = S.add("vector", lambda v, b=b, c=c, cb=cb: v.tensor_tensor(
                out=x1t[b][:, c * 128:(c + 1) * 128], in0=x1t[b][:, c * 128:(c + 1) * 128],
                in1=ps_t[cb][:, 0:128], op=ALU.add), deps=[tr, lx1, prev.get(('x1a', b))])
            prev[('pst', cb)] = [ad]
            prev[('x1a', b)] = ad
            ztr.append(ad)
        for dc in range(2):
            pb = dc
            mms = None
            for zc in range(8):
                mms = S.add("tensor", lambda t, zc=zc, dc=dc, pb=pb, z1sl=z1sl: t.matmul(
                    out=ps_a[pb][:], lhsT=z1sl(zc),
                    rhs=wo_sb[:, zc * 1024 + dc * 512: zc * 1024 + (dc + 1) * 512],
                    start=(zc == 0), stop=False),
                    deps=[lz1, ld_wo] + ([prev.get(('psa', pb))] if zc == 0 else []))
            mb = S.add("tensor", lambda t, dc=dc, pb=pb: t.matmul(
                out=ps_a[pb][:], lhsT=onesb[:], rhs=bo_sb[:, dc * 512:(dc + 1) * 512],
                start=False, stop=True), deps=[ld_bo, cob])
            ad = S.add("vector", lambda v, b=b, dc=dc, pb=pb, i=i: v.scalar_tensor_tensor(
                out=macc[:, i * 1024 + dc * 512: i * 1024 + (dc + 1) * 512], in0=ps_a[pb][:],
                scalar=1.0, in1=x1t[b][:, dc * 512:(dc + 1) * 512], op0=ALU.mult, op1=ALU.add),
                deps=[mb, prev[('x1a', b)], ztr[-1]] + prev.get(('macc', i), []))
            prev[('psa', pb)] = ad
            prev[('macc', i)] = [ad]
        prev[('z1', b)] = [prev[('psa', 0)], prev[('psa', 1)]]
        prev[('z2', b)] = ztr
        prev[('x1', b)] = [prev[('psa', 0)], prev[('psa', 1)]]
        mmg = None
        for hc in range(8):
            mmg = S.add("tensor", lambda t, hc=hc, i=i: t.matmul(
                out=ps_g[:, 0:8], lhsT=xn_sb[:, hc * TOWN + i * 128: hc * TOWN + (i + 1) * 128],
                rhs=wg_sb[:, hc * 8:(hc + 1) * 8], start=(hc == 0), stop=False),
                deps=[ld_xn, ld_wg] + ([prev.get('psg')] if hc == 0 else []))
        mgb = S.add("tensor", lambda t: t.matmul(out=ps_g[:, 0:8], lhsT=onesb[:], rhs=bg_sb[:],
                    start=False, stop=True), deps=[ld_bg, cob])
        e0 = S.add("vector", lambda v: v.tensor_copy(out=lgt[:], in_=ps_g[:, 0:8]), deps=[mgb, prev.get('lgt')])
        prev['psg'] = e0
        e1 = S.add("vector", lambda v: v.reduce_max(out=m1c[:], in_=lgt[:], axis=AX.X), deps=[e0, prev.get('m1c')])
        e1b = S.add("vector", lambda v: v.tensor_scalar(out=m1c[:], in0=m1c[:], scalar1=-1.0, scalar2=None, op0=ALU.mult), deps=[e1])
        e2 = S.add("scalar", lambda s: s.activation(out=exl[:], in_=lgt[:], func=AF.Exp, bias=m1c[:],
                   accum_out=ssc[:]), deps=[e1b, prev.get('exl')])
        e3 = S.add("vector", lambda v: v.reciprocal(out=rsc[:], in_=ssc[:]), deps=[e2, prev.get('rsc')])
        e4 = S.add("vector", lambda v: v.tensor_scalar(out=exl[:], in0=exl[:], scalar1=rsc[:], scalar2=None, op0=ALU.mult), deps=[e3])
        f1 = S.add("vector", lambda v: v.reduce_max(out=i1c[:], in_=exl[:], axis=AX.X), deps=[e4, prev.get('i1c')])
        f2 = S.add("vector", lambda v: v.tensor_scalar(out=cnd[:], in0=exl[:], scalar1=i1c[:], scalar2=None, op0=ALU.is_ge), deps=[f1, prev.get('cnd')])
        f3 = S.add("vector", lambda v: v.scalar_tensor_tensor(out=cnd[:], in0=cnd[:], scalar=-BIGC, in1=io8[:], op0=ALU.mult, op1=ALU.add), deps=[f2, iog])
        f4 = S.add("vector", lambda v: v.tensor_reduce(out=i1c[:], in_=cnd[:], axis=AX.X, op=ALU.min), deps=[f3])
        f4b = S.add("vector", lambda v: v.tensor_scalar(out=i1c[:], in0=i1c[:], scalar1=BIGC, scalar2=None, op0=ALU.add), deps=[f4])
        f5 = S.add("vector", lambda v: v.tensor_scalar(out=oh1[:], in0=io8[:], scalar1=i1c[:], scalar2=None, op0=ALU.is_equal), deps=[f4b, prev.get('oh1')])
        f6 = S.add("vector", lambda v: v.scalar_tensor_tensor(out=lg2[:], in0=oh1[:], scalar=-BIGC, in1=exl[:], op0=ALU.mult, op1=ALU.add), deps=[f5, prev.get('lg2')])
        f7 = S.add("vector", lambda v: v.reduce_max(out=i2c[:], in_=lg2[:], axis=AX.X), deps=[f6, prev.get('i2c')])
        f8 = S.add("vector", lambda v: v.tensor_scalar(out=cnd[:], in0=lg2[:], scalar1=i2c[:], scalar2=None, op0=ALU.is_ge), deps=[f7, f3])
        f9 = S.add("vector", lambda v: v.scalar_tensor_tensor(out=cnd[:], in0=cnd[:], scalar=-BIGC, in1=io8[:], op0=ALU.mult, op1=ALU.add), deps=[f8])
        f10 = S.add("vector", lambda v: v.tensor_reduce(out=i2c[:], in_=cnd[:], axis=AX.X, op=ALU.min), deps=[f9, f7])
        f10b = S.add("vector", lambda v: v.tensor_scalar(out=i2c[:], in0=i2c[:], scalar1=BIGC, scalar2=None, op0=ALU.add), deps=[f10])
        f11 = S.add("vector", lambda v: v.tensor_scalar(out=oh2[:], in0=io8[:], scalar1=i2c[:], scalar2=None, op0=ALU.is_equal), deps=[f10b, prev.get('oh2')])
        f12 = S.add("vector", lambda v: v.tensor_tensor(out=oh1[:], in0=oh1[:], in1=oh2[:], op=ALU.add), deps=[f11, f6])
        f13 = S.add("vector", lambda v, i=i: v.tensor_tensor(out=cw_all[:, i * 8:(i + 1) * 8], in0=exl[:], in1=oh1[:], op=ALU.mult), deps=[f12, e4, prev.get(('cw', i))])
        prev['lgt'] = f13; prev['m1c'] = f13; prev['exl'] = f13; prev['rsc'] = e4
        prev['i1c'] = f5; prev['cnd'] = f10; prev['oh1'] = f13; prev['lg2'] = f8
        prev['i2c'] = f11; prev['oh2'] = f12
        prev[('cw', i)] = [f13]
        prev[('cwd', i)] = f13

    for e in range(8):
        eb = 0
        lwe = S.dma("gpsimd", we_sb[eb][:].rearrange("p (c d) -> p c d", d=1024),
                    i_we[e].rearrange("(c p) d -> p c d", p=128), deps=prev.get(('we', eb), []) + bar)
        ld_be = S.dma("gpsimd", be_sb[:], i_be[:, e * 1024:(e + 1) * 1024],
                      deps=prev.get('beuse', []) + bar)
        last_mm = {}
        for i in range(NT):
            for dc in range(2):
                pb = dc
                mms = None
                for hc in range(8):
                    mms = S.add("tensor", lambda t, hc=hc, i=i, dc=dc, pb=pb, eb=eb: t.matmul(
                        out=ps_y[pb][:], lhsT=xn_sb[:, hc * TOWN + i * 128: hc * TOWN + (i + 1) * 128],
                        rhs=we_sb[eb][:, hc * 1024 + dc * 512: hc * 1024 + (dc + 1) * 512],
                        start=(hc == 0), stop=False),
                        deps=[ld_xn, lwe] + ([prev.get(('psy', pb))] if hc == 0 else []))
                mb = S.add("tensor", lambda t, dc=dc, pb=pb: t.matmul(
                    out=ps_y[pb][:], lhsT=onesb[:], rhs=be_sb[:, dc * 512:(dc + 1) * 512],
                    start=False, stop=True), deps=[ld_be, cob])
                ev = S.add("vector", lambda v, i=i, dc=dc, pb=pb, e=e: v.scalar_tensor_tensor(
                    out=macc[:, i * 1024 + dc * 512: i * 1024 + (dc + 1) * 512], in0=ps_y[pb][:],
                    scalar=cw_all[:, i * 8 + e: i * 8 + e + 1],
                    in1=macc[:, i * 1024 + dc * 512: i * 1024 + (dc + 1) * 512],
                    op0=ALU.mult, op1=ALU.add),
                    deps=[mb, prev[('cwd', i)], prev.get(('psy', pb))] + prev.get(('macc', i), []))
                prev[('psy', pb)] = ev
                prev[('macc', i)] = [ev]
                last_mm[i] = mms
        prev[('we', eb)] = [last_mm[NT - 1]]
        prev['beuse'] = [last_mm[NT - 1]]
    for i in range(NT):
        sto = S.dma("sync", o_out[i * 128:(i + 1) * 128, :], macc[:, i * 1024:(i + 1) * 1024],
                    deps=prev[('macc', i)])
        outs.append(sto)

    return outs



def kernel_unfused(**inputs):
    x = np.ascontiguousarray(inputs["L0_attn_hidden_states"], dtype=np.float32)
    f32 = lambda a: np.ascontiguousarray(a, dtype=np.float32)

    cq, sq_, ck, sk = xpos_tables()
    rotT = rot_matT()
    wqkv = f32(inputs["L0_attn_W_qkv"])
    wg = f32(inputs["L0_attn_W_g"])
    nc1 = build_l0prep()
    im1 = []
    for p in range(4):
        for half in range(2):
            sl = slice(half * TOWN, (half + 1) * TOWN)
            im1.append({
                "x": np.ascontiguousarray(x[p, sl]), "wqkv": wqkv, "wg": wg,
                "cos_q": np.ascontiguousarray(cq[sl].T).astype(BF), "sin_q": np.ascontiguousarray(sq_[sl].T).astype(BF),
                "cos_k": np.ascontiguousarray(ck[sl].T).astype(BF), "sin_k": np.ascontiguousarray(sk[sl].T).astype(BF),
                "rotmT": rotT, "ident": IDENT_BF,
            })
    r1 = run_bass_kernel_spmd(nc1, im1, core_ids=list(range(8))).results

    decay = f32(inputs["L0_attn_decay"])
    caus_f = np.stack([(np.arange(512)[None, :] - np.arange(128)[:, None] - 128 * v >= 0)
                       for v in range(4)]).astype(np.float32)
    nc2 = build_l0ret()
    im2 = []
    for p in range(4):
        qr_full = np.concatenate([r1[2 * p]["qr"], r1[2 * p + 1]["qr"]], axis=1)
        kr_full = np.concatenate([r1[2 * p]["kr"], r1[2 * p + 1]["kr"]], axis=1)
        v_full = np.concatenate([r1[2 * p]["v"], r1[2 * p + 1]["v"]], axis=0)
        g_full = np.concatenate([r1[2 * p]["gate"], r1[2 * p + 1]["gate"]], axis=0)
        for h2 in range(2):
            heads = [h2 * 4 + j for j in range(4)]
            dec_in = np.zeros((1, 8), np.float32)
            dec_in[0, :4] = decay[heads]
            dec_in[0, 4:] = 0.5
            im2.append({
                "qh": np.ascontiguousarray(np.stack([qr_full[h * 128:(h + 1) * 128] for h in heads])),
                "kh": np.ascontiguousarray(np.stack([kr_full[h * 128:(h + 1) * 128] for h in heads])),
                "vh": np.ascontiguousarray(np.stack([v_full[:, h * 256:(h + 1) * 256] for h in heads])),
                "gh": np.ascontiguousarray(np.stack([g_full[:, h * 256:(h + 1) * 256] for h in heads])),
                "decay": dec_in, "causal": caus_f, "ident": IDENT_BF,
                "ones128": np.ones((1, 128), np.float32), "hoff": np.zeros((1, 1), np.float32),
            })
    r2 = run_bass_kernel_spmd(nc2, im2, core_ids=list(range(8))).results

    nc3 = build_l0ffn()
    im3 = []
    for p in range(4):
        gfm_full = np.zeros((2048, T), dtype=BF)
        for h in range(8):
            gfm_full[h * 256:(h + 1) * 256] = r2[2 * p + (0 if h < 4 else 1)]["gfm"][h % 4]
        for half in range(2):
            sl = slice(half * TOWN, (half + 1) * TOWN)
            im3.append({
                "x": np.ascontiguousarray(x[p, sl]),
                "gfm": np.ascontiguousarray(gfm_full[:, sl]),
                "wproj": f32(inputs["L0_attn_W_proj"]),
                "wgr": f32(inputs["L0_ffn_Wg"]), "bg": f32(inputs["L0_ffn_bg"]),
                "w1": f32(inputs["L0_ffn_W1"]),
                "b1t": np.ascontiguousarray(f32(inputs["L0_ffn_b1"]).reshape(4, 32, 128).transpose(2, 0, 1).reshape(128, 128)),
                "w2": f32(inputs["L0_ffn_W2"]), "b2": f32(inputs["L0_ffn_b2"]),
                "identb": IDENT_BF, "ones128": np.ones((1, 128), np.float32),
            })
    r3 = run_bass_kernel_spmd(nc3, im3, core_ids=list(range(8))).results

    Wr = f32(inputs["L1_attn_W_qkv"]).reshape(16, 64, 3, 1024)
    br = f32(inputs["L1_attn_b_qkv"]).reshape(16, 64, 3)
    fW = f32(inputs["L1_ffn_W_qkv"])
    fb = f32(inputs["L1_ffn_b_qkv"])
    caus_b = caus_f.astype(BF)
    nc4 = build_l1attn()
    im4 = []
    for p in range(4):
        xn1_full = np.concatenate([r3[2 * p]["xn1fm"], r3[2 * p + 1]["xn1fm"]], axis=1)
        for h2 in range(2):
            heads = [h2 * 8 + j for j in range(8)]
            def bcol(which):
                out = np.zeros((128, 4), np.float32)
                for hl, h in enumerate(heads):
                    out[(hl % 2) * 64:(hl % 2) * 64 + 64, hl // 2] = br[h, :, which]
                return out
            bvrow = np.zeros((1, 512), np.float32)
            fbvrow = np.zeros((1, 512), np.float32)
            for hl, h in enumerate(heads):
                bvrow[0, hl * 64:(hl + 1) * 64] = br[h, :, 2]
                fbvrow[0, hl * 64:(hl + 1) * 64] = fb[2048 + h * 64: 2048 + (h + 1) * 64]
            def fbcol(which):
                out = np.zeros((128, 4), np.float32)
                for hl, h in enumerate(heads):
                    out[(hl % 2) * 64:(hl % 2) * 64 + 64, hl // 2] = fb[which * 1024 + h * 64: which * 1024 + (h + 1) * 64]
                return out
            cols = [h * 64 + j for h in heads for j in range(64)]
            im4.append({
                "xn1fm": np.ascontiguousarray(xn1_full),
                "wq": np.ascontiguousarray(np.concatenate([Wr[h, :, 0, :].T for h in heads], axis=1)),
                "wk": np.ascontiguousarray(np.concatenate([Wr[h, :, 1, :].T for h in heads], axis=1)),
                "wv": np.ascontiguousarray(np.concatenate([Wr[h, :, 2, :].T for h in heads], axis=1)),
                "bqc": bcol(0), "bkc": bcol(1), "bvrow": bvrow,
                "fwq": np.ascontiguousarray(fW[:, 0:1024][:, cols]),
                "fwk": np.ascontiguousarray(fW[:, 1024:2048][:, cols]),
                "fwv": np.ascontiguousarray(fW[:, 2048:3072][:, cols]),
                "fbqc": fbcol(0), "fbkc": fbcol(1), "fbvrow": fbvrow,
                "causal": caus_b, "ones128": np.ones((1, 128), np.float32),
                "onescol": np.ones((128, 1), np.float32), "identf": IDENT_F32,
            })
    r4 = run_bass_kernel_spmd(nc4, im4, core_ids=list(range(8))).results

    nc5 = build_l1fin()
    im5 = []
    for p in range(4):
        z1_full = np.zeros((1024, T), dtype=BF)
        z2_full = np.zeros((1024, T), dtype=BF)
        for h in range(16):
            src = r4[2 * p + (0 if h < 8 else 1)]
            z1_full[h * 64:(h + 1) * 64] = src["z1"][h % 8]
            z2_full[h * 64:(h + 1) * 64] = src["z2"][h % 8]
        for half in range(2):
            sl = slice(half * TOWN, (half + 1) * TOWN)
            im5.append({
                "x1": r3[2 * p + half]["x1"],
                "xn1fm": r3[2 * p + half]["xn1fm"],
                "z1fm": np.ascontiguousarray(z1_full[:, sl]),
                "z2fm": np.ascontiguousarray(z2_full[:, sl]),
                "woT": np.ascontiguousarray(f32(inputs["L1_attn_W_o"]).T),
                "bo": f32(inputs["L1_attn_b_o"]).reshape(1, 1024),
                "wgate": f32(inputs["L1_ffn_W_gate"]),
                "bgate": f32(inputs["L1_ffn_b_gate"]).reshape(1, 8),
                "wexp": f32(inputs["L1_ffn_W_experts"]),
                "bexp": f32(inputs["L1_ffn_b_experts"]).reshape(1, 8192),
                "identb": IDENT_BF, "ones128": np.ones((1, 128), np.float32),
            })
    r5 = run_bass_kernel_spmd(nc5, im5, core_ids=list(range(8))).results

    out = np.zeros((B, T, D), dtype=np.float32)
    for p in range(4):
        for half in range(2):
            out[p, half * TOWN:(half + 1) * TOWN] = r5[2 * p + half]["out"]
    return out



def build_fused(groups=None, detect_races=True):
    groups = groups or [[0, 1], [2, 3], [4, 5], [6, 7]]
    nc = bass.Bass("TRN2", target_bir_lowering=False, debug=False, detect_race_conditions=detect_races)
    S = Sched()
    es_main = contextlib.ExitStack()
    dram = lambda nm, shp, dt=F32, kind="ExternalInput": nc.dram_tensor(nm, shp, dt, kind=kind).ap()

    x_full = dram("x", [T, D])
    x_own = dram("x_own", [TOWN, D])
    wqkv_my = dram("wqkv", [D, 2048])
    wg_my = dram("wg", [D, 1024])
    tabs = {nm: dram(nm, [512, T], BF16) for nm in ("cos_q", "sin_q", "cos_k", "sin_k")}
    rotmT_i = dram("rotmT", [128, 128], BF16)
    identb_i = dram("ident", [128, 128], BF16)
    identf_i = dram("identf", [128, 128], F32)
    ones_i = dram("ones128", [1, 128])
    onesc_i = dram("onescol", [128, 1])
    decay_i = dram("decay", [1, 8])
    causf_i = dram("causf", [4, 128, 512])
    causb_i = dram("causb", [4, 128, 512], BF16)
    wproj_i = dram("wproj", [2048, 1024])
    wgr_i = dram("wgr", [4, 1024, 4])
    bg_i = dram("bg", [4, 4])
    w1_i = dram("w1", [4, 1024, 4096])
    b1t_i = dram("b1t", [128, 128])
    w2_i = dram("w2", [4, 4096, 1024])
    b2_i = dram("b2", [4, 1024])
    l1w = {nm: dram(nm, [1024, 512]) for nm in ("wq", "wk", "wv", "fwq", "fwk", "fwv")}
    l1b = {nm: dram(nm, [128, 4]) for nm in ("bqc", "bkc", "fbqc", "fbkc")}
    l1br = {nm: dram(nm, [1, 512]) for nm in ("bvrow", "fbvrow")}
    woT_i = dram("woT", [1024, 1024])
    bo_i = dram("bo", [1, 1024])
    wgate_i = dram("wgate", [1024, 8])
    bgate_i = dram("bgate", [1, 8])
    wexp_i = dram("wexp", [8, 1024, 1024])
    bexp_i = dram("bexp", [1, 8 * 1024])
    o_out = dram("out", [TOWN, D], F32, kind="ExternalOutput")

    qr_t = dram("qr_t", [512, T], BF16, kind="Internal")
    kr_t = dram("kr_t", [512, T], BF16, kind="Internal")
    v_t = dram("v_t", [T, 1024], BF16, kind="Internal")
    g_t = dram("g_t", [T, 1024], BF16, kind="Internal")
    ccg_in = dram("ccg_in", [1024, T], BF16, kind="Internal")
    ccg_out = dram("ccg_out", [2048, T], BF16, kind="Internal")
    ccx_in = dram("ccx_in", [1024, TOWN], BF16, kind="Internal")
    ccx_out = dram("ccx_out", [2048, TOWN], BF16, kind="Internal")
    ccz_in = dram("ccz_in", [1024, T], BF16, kind="Internal")
    ccz_out = dram("ccz_out", [2048, T], BF16, kind="Internal")
    own0_i = dram("own0", [1, 1], mybir.dt.int32)
    x1_t = dram("x1_t", [TOWN, D], F32, kind="Internal")
    w1b_t = dram("w1b", [4, 1024, 4096], BF16, kind="Internal")
    w2b_t = dram("w2b", [4, 4096, 1024], BF16, kind="Internal")
    xpa_t = dram("xpa", [TOWN, D], F32, kind="Internal")
    upd_t = dram("updsum", [TOWN, D], F32, kind="Internal")

    esE = contextlib.ExitStack()
    esE2 = contextlib.ExitStack()

    off_sb = es_main.enter_context(nc.sbuf_tensor("off_sb", [1, 1], mybir.dt.int32))
    goff_cell = [None]
    ld_off = S.dma("sync", off_sb[:], own0_i)
    def _setup_off(sync):
        reg = sync.alloc_register("own0reg")
        inst = sync.reg_load(reg, off_sb[0:1, 0:1])
        goff_cell[0] = sync.snap(reg)
        return inst
    off_op = S.add("sync", _setup_off, deps=[ld_off])

    es0 = contextlib.ExitStack()
    io0 = {"x": x_full, "wqkv": wqkv_my, "wg": wg_my, "rotmT": rotmT_i, "ident": identb_i,
           "qr": qr_t, "kr": kr_t, "v": v_t, "gate": g_t,
           "cos_q": tabs["cos_q"], "sin_q": tabs["sin_q"], "cos_k": tabs["cos_k"], "sin_k": tabs["sin_k"]}
    p0 = emit_l0prep(nc, S, es0, io0, [], ntok=T, nq=4, nv=1024, ng=1024, store_xnfm=False, pfx="p_")
    es0.close()

    es2_ = contextlib.ExitStack()
    qr_v = qr_t.rearrange("(h p) t -> h p t", p=128)
    kr_v = kr_t.rearrange("(h p) t -> h p t", p=128)
    io2 = {"qh": lambda m: qr_v[m], "kh": lambda m: kr_v[m],
           "vh": lambda m: v_t[:, m * 256:(m + 1) * 256],
           "gh": lambda m: g_t[:, m * 256:(m + 1) * 256],
           "decay": decay_i, "causal": causf_i, "ident": identb_i, "ones128": ones_i,
           "gfm": lambda m: ccg_in[m * 256:(m + 1) * 256]}
    p2 = emit_l0ret(nc, S, es2_, io2, bar=p0, pfx="r_")
    es2_.close()

    cc1 = S.add("gpsimd", lambda g: g.collective_compute(
        "AllGather", ALU.bypass, replica_groups=groups, ins=[ccg_in.opt()], outs=[ccg_out.opt()]),
        deps=p2)

    es3 = contextlib.ExitStack()
    io3 = {"x": x_own, "gfm": ccg_out, "goff": goff_cell, "goff_dep": off_op,
           "wproj": wproj_i, "wgr": wgr_i, "bg": bg_i,
           "w1": w1_i, "b1t": b1t_i, "w2": w2_i, "b2": b2_i, "identb": identb_i,
           "ones128": ones_i, "x1": x1_t, "xn1fm": ccx_in,
           "w1b": w1b_t, "w2b": w2b_t, "xpa": xpa_t, "updsum": upd_t}
    p3 = emit_l0ffn(nc, S, es3, io3, bar=[cc1] + p2 + p0, pfx="f_", es_early=esE, bar_early=[], es_ln_ext=esE2)
    es3.close()
    esE.close()

    cc2 = S.add("gpsimd", lambda g: g.collective_compute(
        "AllGather", ALU.bypass, replica_groups=groups, ins=[ccx_in.opt()], outs=[ccx_out.opt()]),
        deps=p3)

    es4 = contextlib.ExitStack()
    io4 = {"xn1fm_parts": [(ccx_out[0:1024], 0), (ccx_out[1024:2048], TOWN)],
           "causal": causb_i, "ones128": ones_i, "onescol": onesc_i, "identf": identf_i,
           "z1": lambda h: ccz_in[h * 64:(h + 1) * 64],
           "z2": lambda h: ccz_in[512 + h * 64: 512 + (h + 1) * 64]}
    io4.update(l1w); io4.update(l1b); io4.update(l1br)
    p4 = emit_l1attn(nc, S, es4, io4, bar=[cc2] + p3, pfx="a_")
    es4.close()

    cc3 = S.add("gpsimd", lambda g: g.collective_compute(
        "AllGather", ALU.bypass, replica_groups=groups, ins=[ccz_in.opt()], outs=[ccz_out.opt()]),
        deps=p4)

    es5 = contextlib.ExitStack()
    io5 = {"x1": x1_t, "xn1fm": ccx_in, "goff": goff_cell, "goff_dep": off_op,
           "z1get": lambda c: ccz_out[c * 128:(c + 1) * 128] if c < 4 else ccz_out[1024 + (c - 4) * 128: 1024 + (c - 3) * 128],
           "z2get": lambda c: ccz_out[512 + c * 128: 512 + (c + 1) * 128] if c < 4 else ccz_out[1536 + (c - 4) * 128: 1536 + (c - 3) * 128],
           "woT": woT_i, "bo": bo_i, "wgate": wgate_i, "bgate": bgate_i,
           "wexp": wexp_i, "bexp": bexp_i, "identb": identb_i, "ones128": ones_i,
           "out": o_out}
    p5 = emit_l1fin(nc, S, es5, io5, bar=[cc3] + p4, pfx="n_")
    es5.close()

    S.emit(nc, es_main, final_waits=p5)
    es_main.close()
    return nc


def fused_in_maps(inputs):
    x = np.ascontiguousarray(inputs["L0_attn_hidden_states"], dtype=np.float32)
    f32 = lambda a: np.ascontiguousarray(a, dtype=np.float32)
    cq, sq_, ck, sk = xpos_tables()
    rotT = rot_matT()
    Wqkv = f32(inputs["L0_attn_W_qkv"])
    Wg = f32(inputs["L0_attn_W_g"])
    decay = f32(inputs["L0_attn_decay"])
    caus_f = np.stack([(np.arange(512)[None, :] - np.arange(128)[:, None] - 128 * v >= 0)
                       for v in range(4)]).astype(np.float32)
    Wr = f32(inputs["L1_attn_W_qkv"]).reshape(16, 64, 3, 1024)
    br = f32(inputs["L1_attn_b_qkv"]).reshape(16, 64, 3)
    fW = f32(inputs["L1_ffn_W_qkv"])
    fb = f32(inputs["L1_ffn_b_qkv"])
    maps = []
    for p in range(4):
        for h2 in range(2):
            heads0 = [h2 * 4 + j for j in range(4)]
            heads1 = [h2 * 8 + j for j in range(8)]
            own = slice(h2 * TOWN, (h2 + 1) * TOWN)
            qcols = [h * 128 + j for h in heads0 for j in range(128)]
            vcols = [h * 256 + j for h in heads0 for j in range(256)]
            wqkv_my = np.concatenate([Wqkv[:, qcols], Wqkv[:, [1024 + c for c in qcols]],
                                      Wqkv[:, [2048 + c for c in vcols]]], axis=1)
            dec_in = np.zeros((1, 8), np.float32)
            dec_in[0, :4] = decay[heads0]
            dec_in[0, 4:] = 0.5
            def bcol(which):
                o = np.zeros((128, 4), np.float32)
                for hl, h in enumerate(heads1):
                    o[(hl % 2) * 64:(hl % 2) * 64 + 64, hl // 2] = br[h, :, which]
                return o
            def fbcol(which):
                o = np.zeros((128, 4), np.float32)
                for hl, h in enumerate(heads1):
                    o[(hl % 2) * 64:(hl % 2) * 64 + 64, hl // 2] = fb[which * 1024 + h * 64: which * 1024 + (h + 1) * 64]
                return o
            bvrow = np.zeros((1, 512), np.float32)
            fbvrow = np.zeros((1, 512), np.float32)
            for hl, h in enumerate(heads1):
                bvrow[0, hl * 64:(hl + 1) * 64] = br[h, :, 2]
                fbvrow[0, hl * 64:(hl + 1) * 64] = fb[2048 + h * 64: 2048 + (h + 1) * 64]
            cols1 = [h * 64 + j for h in heads1 for j in range(64)]
            maps.append({
                "x": x[p], "x_own": np.ascontiguousarray(x[p, own]),
                "own0": np.array([[h2 * TOWN]], np.int32),
                "wqkv": np.ascontiguousarray(wqkv_my), "wg": np.ascontiguousarray(Wg[:, vcols]),
                "cos_q": np.ascontiguousarray(cq.T[qcols]).astype(BF),
                "sin_q": np.ascontiguousarray(sq_.T[qcols]).astype(BF),
                "cos_k": np.ascontiguousarray(ck.T[qcols]).astype(BF),
                "sin_k": np.ascontiguousarray(sk.T[qcols]).astype(BF),
                "rotmT": rotT, "ident": IDENT_BF, "identf": IDENT_F32,
                "ones128": np.ones((1, 128), np.float32), "onescol": np.ones((128, 1), np.float32),
                "decay": dec_in, "causf": caus_f, "causb": caus_f.astype(BF),
                "wproj": f32(inputs["L0_attn_W_proj"]),
                "wgr": f32(inputs["L0_ffn_Wg"]), "bg": f32(inputs["L0_ffn_bg"]),
                "w1": f32(inputs["L0_ffn_W1"]),
                "b1t": np.ascontiguousarray(f32(inputs["L0_ffn_b1"]).reshape(4, 32, 128).transpose(2, 0, 1).reshape(128, 128)),
                "w2": f32(inputs["L0_ffn_W2"]), "b2": f32(inputs["L0_ffn_b2"]),
                "wq": np.ascontiguousarray(np.concatenate([Wr[h, :, 0, :].T for h in heads1], axis=1)),
                "wk": np.ascontiguousarray(np.concatenate([Wr[h, :, 1, :].T for h in heads1], axis=1)),
                "wv": np.ascontiguousarray(np.concatenate([Wr[h, :, 2, :].T for h in heads1], axis=1)),
                "bqc": bcol(0), "bkc": bcol(1), "bvrow": bvrow,
                "fwq": np.ascontiguousarray(fW[:, 0:1024][:, cols1]),
                "fwk": np.ascontiguousarray(fW[:, 1024:2048][:, cols1]),
                "fwv": np.ascontiguousarray(fW[:, 2048:3072][:, cols1]),
                "fbqc": fbcol(0), "fbkc": fbcol(1), "fbvrow": fbvrow,
                "woT": np.ascontiguousarray(f32(inputs["L1_attn_W_o"]).T),
                "bo": f32(inputs["L1_attn_b_o"]).reshape(1, 1024),
                "wgate": f32(inputs["L1_ffn_W_gate"]),
                "bgate": f32(inputs["L1_ffn_b_gate"]).reshape(1, 8),
                "wexp": f32(inputs["L1_ffn_W_experts"]),
                "bexp": f32(inputs["L1_ffn_b_experts"]).reshape(1, 8192),
            })
    return maps


def kernel(**inputs):
    maps = fused_in_maps(inputs)
    nc = build_fused()
    res = run_bass_kernel_spmd(nc, maps, core_ids=list(range(8))).results
    out = np.zeros((B, T, D), dtype=np.float32)
    for p in range(4):
        for half in range(2):
            out[p, half * TOWN:(half + 1) * TOWN] = res[2 * p + half]["out"]
    return out


# version 06
# speedup vs baseline: 1.0448x; 1.0448x over previous
import numpy as np
import ml_dtypes
import contextlib
import concourse.bass as bass
import concourse.mybir as mybir
from concourse.bass_utils import run_bass_kernel_spmd

F32 = mybir.dt.float32
BF16 = mybir.dt.bfloat16
AF = mybir.ActivationFunctionType
ALU = mybir.AluOpType
AX = mybir.AxisListType
BF = ml_dtypes.bfloat16

ENGS = ("tensor", "vector", "scalar", "gpsimd", "sync")

B, T, D = 4, 4096, 1024
TOWN = T // 2
NT = TOWN // 128


class Sched:

    def __init__(self):
        self.ops = []

    def add(self, eng, emit, deps=()):
        tok = len(self.ops)
        self.ops.append([eng, emit, [d for d in deps if d is not None], False])
        return tok

    def dma(self, eng, out, in_, deps=()):
        tok = len(self.ops)
        self.ops.append([eng, (out, in_, None), [d for d in deps if d is not None], True])
        return tok

    def dma_acc(self, eng, out, in_, op, deps=()):
        tok = len(self.ops)
        self.ops.append([eng, (out, in_, op), [d for d in deps if d is not None], True])
        return tok

    def emit(self, nc, es, final_waits=()):
        import os
        ops = self.ops
        pfx = os.environ.get("SCHED_PREFIX")
        if pfx:
            ops = ops[:int(pfx)]
            final_waits = [d for d in final_waits if d < len(ops)]
        n = len(ops)
        need_inc = [False] * n
        for (eng, emit, deps, isdma) in ops:
            for d in deps:
                need_inc[d] = True
        final_waits = list(final_waits)
        for i, (eng, emit, deps, isdma) in enumerate(ops):
            if isdma and not need_inc[i]:
                need_inc[i] = True
                final_waits.append(i)
        for d in final_waits:
            need_inc[d] = True
        val = [0] * n
        cnt = {}
        for i, (eng, emit, deps, isdma) in enumerate(ops):
            key = (eng, isdma)
            if need_inc[i]:
                cnt[key] = cnt.get(key, 0) + (16 if isdma else 1)
            val[i] = cnt.get(key, 0)
        sems = {}
        for e in ENGS:
            sems[(e, False)] = es.enter_context(nc.semaphore("Sc" + e))
            sems[(e, True)] = es.enter_context(nc.semaphore("Sd" + e))
        blk = es.enter_context(nc.Block())
        per_eng = {e: [i for i in range(n) if ops[i][0] == e] for e in ENGS}

        def run_engine(engine_obj, e):
            waited = {}
            for i in per_eng[e]:
                eng, emit, deps, isdma = ops[i]
                needs = {}
                for d in deps:
                    k2 = (ops[d][0], ops[d][3])
                    if k2 == (e, False) and e == "tensor":
                        continue
                    needs[k2] = max(needs.get(k2, 0), val[d])
                if isdma:
                    prev = val[i] - 16
                    if prev > 0:
                        needs[(e, True)] = max(needs.get((e, True), 0), prev)
                for k2 in sorted(needs):
                    if needs[k2] > waited.get(k2, 0):
                        engine_obj.wait_ge(sems[k2], needs[k2])
                        waited[k2] = needs[k2]
                if isdma:
                    out, in_, aop = emit
                    if callable(out):
                        out = out()
                    if callable(in_):
                        in_ = in_()
                    if aop is None:
                        try:
                            inst = engine_obj.dma_start(out=out, in_=in_)
                        except ValueError as _e:
                            import traceback as _tb
                            _tb.print_exc()
                            raise
                    else:
                        inst = engine_obj.dma_start(out=out, in_=in_, accum_op=aop)
                else:
                    inst = emit(engine_obj)
                if need_inc[i]:
                    inst.then_inc(sems[(eng, isdma)], 16 if isdma else 1)
            if e == "sync":
                fw = {}
                for d in final_waits:
                    k2 = (ops[d][0], ops[d][3])
                    fw[k2] = max(fw.get(k2, 0), val[d])
                for k2 in sorted(fw):
                    if fw[k2] > waited.get(k2, 0):
                        engine_obj.wait_ge(sems[k2], fw[k2])

        for e in ENGS:
            getattr(blk, e)(lambda engine_obj, e=e: run_engine(engine_obj, e))



def xpos_tables():
    QK = 1024
    SCALE_BASE = 512.0
    j2 = np.arange(0, QK, 2, dtype=np.float32)
    base_scale = (j2 + 0.4 * QK) / (1.4 * QK)
    pos = np.arange(T, dtype=np.float32)
    scale = base_scale[None, :] ** (pos[:, None] / SCALE_BASE)
    inv_freq = (1.0 / 10000.0 ** (np.arange(512, dtype=np.float32) / 512.0))
    sinu = pos[:, None] * inv_freq[None, :]
    sin, cos = np.sin(sinu), np.cos(sinu)
    def rep2(x):
        return np.repeat(x, 2, axis=1)
    cos_q = rep2(cos * scale)
    sin_q = rep2(sin * scale)
    cos_k = rep2(cos / scale)
    sin_k = rep2(sin / scale)
    return cos_q, sin_q, cos_k, sin_k


def rot_matT():
    P = np.zeros((128, 128), dtype=np.float32)
    for j in range(64):
        P[2 * j, 2 * j + 1] = -1.0
        P[2 * j + 1, 2 * j] = 1.0
    return np.ascontiguousarray(P.T).astype(BF)


IDENT_BF = np.eye(128, dtype=BF)
IDENT_F32 = np.eye(128, dtype=np.float32)



def build_l0prep():
    nc = bass.Bass("TRN2", target_bir_lowering=False, debug=False)
    io = {}
    io["x"] = nc.dram_tensor("x", [TOWN, D], F32, kind="ExternalInput").ap()
    io["wqkv"] = nc.dram_tensor("wqkv", [D, 4096], F32, kind="ExternalInput").ap()
    io["wg"] = nc.dram_tensor("wg", [D, 2048], F32, kind="ExternalInput").ap()
    io["cos_q"] = nc.dram_tensor("cos_q", [1024, TOWN], BF16, kind="ExternalInput").ap()
    io["sin_q"] = nc.dram_tensor("sin_q", [1024, TOWN], BF16, kind="ExternalInput").ap()
    io["cos_k"] = nc.dram_tensor("cos_k", [1024, TOWN], BF16, kind="ExternalInput").ap()
    io["sin_k"] = nc.dram_tensor("sin_k", [1024, TOWN], BF16, kind="ExternalInput").ap()
    io["rotmT"] = nc.dram_tensor("rotmT", [128, 128], BF16, kind="ExternalInput").ap()
    io["ident"] = nc.dram_tensor("ident", [128, 128], BF16, kind="ExternalInput").ap()
    io["qr"] = nc.dram_tensor("qr", [1024, TOWN], BF16, kind="ExternalOutput").ap()
    io["kr"] = nc.dram_tensor("kr", [1024, TOWN], BF16, kind="ExternalOutput").ap()
    io["v"] = nc.dram_tensor("v", [TOWN, 2048], BF16, kind="ExternalOutput").ap()
    io["gate"] = nc.dram_tensor("gate", [TOWN, 2048], BF16, kind="ExternalOutput").ap()
    io["xnfm"] = nc.dram_tensor("xnfm", [1024, TOWN], BF16, kind="ExternalOutput").ap()
    es = contextlib.ExitStack()
    S = Sched()
    outs = emit_l0prep(nc, S, es, io, [], ntok=TOWN)
    S.emit(nc, es, final_waits=outs)
    es.close()
    return nc


def emit_l0prep(nc, S, es, io, bar, ntok, nq=8, nv=2048, ng=2048, store_xnfm=True, pfx=""):
    i_x = io["x"]
    i_wqkv = io["wqkv"]; i_wg = io["wg"]
    i_cq = io["cos_q"]; i_sq = io["sin_q"]; i_ck = io["cos_k"]; i_sk = io["sin_k"]
    i_rot = io["rotmT"]; i_id = io["ident"]
    o_qr = io["qr"]; o_kr = io["kr"]; o_v = io["v"]; o_gate = io["gate"]
    o_xnfm = io.get("xnfm")
    TOWN = ntok
    NT = TOWN // 128
    NCOLS = 2 * 128 * nq + nv
    sb = lambda nme, s, d=F32: es.enter_context(nc.sbuf_tensor(pfx + nme, s, d))
    psum = lambda nme, s, d=F32: es.enter_context(nc.psum_tensor(pfx + nme, s, d))
    outs = []

    w_sb = sb("w_sb", [128, 8 * NCOLS], BF16)
    wg_sb = sb("wg_sb", [128, 8 * ng], BF16)
    rot_sb = sb("rot_sb", [128, 128], BF16)
    id_sb = sb("id_sb", [128, 128], BF16)
    ld_w = [S.dma("gpsimd", w_sb[:, hc * NCOLS:(hc + 1) * NCOLS], i_wqkv[hc * 128:(hc + 1) * 128, :], deps=bar)
            for hc in range(8)]
    ld_wg = [S.dma("gpsimd", wg_sb[:, hc * ng:(hc + 1) * ng], i_wg[hc * 128:(hc + 1) * 128, :], deps=bar)
             for hc in range(8)]
    ld_rot = S.dma("sync", rot_sb[:], i_rot, deps=bar)
    ld_id = S.dma("sync", id_sb[:], i_id, deps=bar)

    xn_fm = sb("xn_fm", [128, 8 * TOWN], BF16)
    x_t = [sb("x_t0", [128, 1024])]
    x2_t = sb("x2_t", [128, 1024])
    s1 = sb("s1", [128, 1]); s2 = sb("s2", [128, 1])
    mu = sb("mu", [128, 1]); m2 = sb("m2", [128, 1]); var = sb("var", [128, 1])
    sq = sb("sq", [128, 1]); rstd = sb("rstd", [128, 1])
    xn_tok = [sb(f"xn_tok{i}", [128, 1024], BF16) for i in range(2)]
    ps_tr = [psum("ps_tr0", [128, 128], BF16)]

    prev_tile_done = [None, None]
    prev_xn_done = [None, None]
    prev_tr_evac = [None, None]
    xn_ready = []
    for i in range(NT):
        bi = 0
        ld = S.dma("sync", x_t[bi][:], i_x[i * 128:(i + 1) * 128, :],
                   deps=(prev_tile_done[bi] or []) + bar)
        sqr = S.add("scalar", lambda s, bi=bi: s.activation(out=x2_t[:], in_=x_t[bi][:], func=AF.Square),
                    deps=[ld, prev_xn_done[(i + 1) % 2]])
        r1 = S.add("vector", lambda v, bi=bi: v.reduce_sum(out=s1[:], in_=x_t[bi][:], axis=AX.X), deps=[ld])
        r2 = S.add("vector", lambda v: v.reduce_sum(out=s2[:], in_=x2_t[:], axis=AX.X), deps=[sqr, r1])
        c1 = S.add("vector", lambda v: v.tensor_scalar(out=mu[:], in0=s1[:], scalar1=1.0 / 1024, scalar2=None, op0=ALU.mult), deps=[r2])
        c2 = S.add("vector", lambda v: v.tensor_tensor(out=m2[:], in0=mu[:], in1=mu[:], op=ALU.mult), deps=[c1])
        c3 = S.add("vector", lambda v: v.tensor_scalar(out=var[:], in0=s2[:], scalar1=1.0 / 1024, scalar2=None, op0=ALU.mult), deps=[c2])
        c4 = S.add("vector", lambda v: v.tensor_sub(out=var[:], in0=var[:], in1=m2[:]), deps=[c3])
        c4b = S.add("vector", lambda v: v.tensor_scalar(out=var[:], in0=var[:], scalar1=1e-5, scalar2=None, op0=ALU.add), deps=[c4])
        c5 = S.add("scalar", lambda s: s.activation(out=sq[:], in_=var[:], func=AF.Sqrt), deps=[c4b])
        c6 = S.add("vector", lambda v: v.reciprocal(out=rstd[:], in_=sq[:]), deps=[c5])
        xo = S.add("vector", lambda v, bi=bi: v.tensor_scalar(out=xn_tok[bi][:], in0=x_t[bi][:],
                   scalar1=mu[:], scalar2=rstd[:], op0=ALU.subtract, op1=ALU.mult),
                   deps=[c6, prev_xn_done[bi]])
        prev_tile_done[bi] = [xo, sqr]
        evs = []
        for c in range(8):
            tb = 0
            tr = S.add("tensor", lambda t, bi=bi, c=c, tb=tb: t.transpose(
                out=ps_tr[tb][:], in_=xn_tok[bi][:, c * 128:(c + 1) * 128], identity=id_sb[:]),
                deps=[xo, ld_id, prev_tr_evac[tb]])
            ev = S.add("vector", lambda v, i=i, c=c, tb=tb: v.tensor_copy(
                out=xn_fm[:, c * TOWN + i * 128: c * TOWN + (i + 1) * 128], in_=ps_tr[tb][:]),
                deps=[tr])
            prev_tr_evac[tb] = ev
            evs.append(ev)
        prev_xn_done[bi] = evs[-1]
        xn_ready.append(evs)
    xn_all = [e for evs in xn_ready for e in evs]

    if store_xnfm:
        for c in range(8):
            outs.append(S.dma("sync", o_xnfm[c * 128:(c + 1) * 128, :], xn_fm[:, c * TOWN:(c + 1) * TOWN],
                              deps=[e for evs in xn_ready for e in evs[c:c + 1]]))

    ps_qkv = [psum(f"ps_qkv{i}", [128, 512]) for i in range(2)]
    ps_rot = [psum(f"ps_rot{i}", [128, 512]) for i in range(2)]
    cs_t = [sb("cs_t0", [128, 2 * TOWN], BF16)]
    qf = [sb(f"qf{i}", [128, 512]) for i in range(2)]
    qb = [sb(f"qb{i}", [128, 512], BF16) for i in range(2)]
    t1b = [sb(f"t1b{i}", [128, 512]) for i in range(2)]
    qrot = [sb(f"qrot{i}", [128, 512], BF16) for i in range(2)]
    prev_cs_use = [None, None]
    prev_q_sto = {}
    for c in range(2 * nq):
        cb = 0
        is_q = c < nq
        cc = c if is_q else c - nq
        src_c = i_cq if is_q else i_ck
        src_s = i_sq if is_q else i_sk
        ldc = S.dma("sync", cs_t[cb][:, 0:TOWN], src_c[cc * 128:(cc + 1) * 128, :], deps=[prev_cs_use[cb]])
        lds = S.dma("sync", cs_t[cb][:, TOWN:2 * TOWN], src_s[cc * 128:(cc + 1) * 128, :], deps=[prev_cs_use[cb]])
        for j in range(TOWN // 512):
            pb = (c * (TOWN // 512) + j) % 2
            mms = []
            for h in range(8):
                mms.append(S.add("tensor", lambda t, h=h, c=c, j=j, pb=pb: t.matmul(
                    out=ps_qkv[pb][:], lhsT=w_sb[:, h * NCOLS + c * 128:h * NCOLS + (c + 1) * 128],
                    rhs=xn_fm[:, h * TOWN + j * 512:h * TOWN + (j + 1) * 512],
                    start=(h == 0), stop=(h == 7)),
                    deps=([ld_w[h]] + (xn_all if h == 0 else []) + ([prev_q_sto.get((pb, 'f'))] if h == 0 else []))))
            e1 = S.add("vector", lambda v, pb=pb: v.tensor_copy(out=qf[pb][:], in_=ps_qkv[pb][:]),
                       deps=[mms[-1], prev_q_sto.get((pb, 'use_f'))])
            e2 = S.add("vector", lambda v, pb=pb: v.tensor_copy(out=qb[pb][:], in_=qf[pb][:]), deps=[e1])
            rot = S.add("tensor", lambda t, pb=pb: t.matmul(out=ps_rot[pb][:], lhsT=rot_sb[:], rhs=qb[pb][:],
                        start=True, stop=True), deps=[e2, ld_rot, prev_q_sto.get((pb, 'rot'))])
            m1 = S.add("vector", lambda v, pb=pb, cb=cb, j=j: v.tensor_tensor(
                out=t1b[pb][:], in0=qf[pb][:], in1=cs_t[cb][:, j * 512:(j + 1) * 512], op=ALU.mult),
                deps=[e1, ldc])
            m2o = S.add("vector", lambda v, pb=pb, cb=cb, j=j: v.tensor_tensor(
                out=qf[pb][:], in0=ps_rot[pb][:], in1=cs_t[cb][:, TOWN + j * 512:TOWN + (j + 1) * 512], op=ALU.mult),
                deps=[rot, lds, m1])
            a1 = S.add("vector", lambda v, pb=pb: v.tensor_add(out=qrot[pb][:], in0=t1b[pb][:], in1=qf[pb][:]),
                       deps=[m2o, prev_q_sto.get((pb, 'sto'))])
            dst = o_qr if is_q else o_kr
            sto = S.dma("sync", dst[cc * 128:(cc + 1) * 128, j * 512:(j + 1) * 512], qrot[pb][:], deps=[a1])
            outs.append(sto)
            prev_cs_use[cb] = m2o
            prev_q_sto[(pb, 'f')] = e1
            prev_q_sto[(pb, 'use_f')] = m2o
            prev_q_sto[(pb, 'rot')] = m2o
            prev_q_sto[(pb, 'sto')] = sto
            prev_q_sto[(pb, 'use_f')] = a1

    ps_v = [psum(f"ps_v{i}", [128, 512]) for i in range(2)]
    vstg = [sb(f"vstg{i}", [128, 512], BF16) for i in range(2)]
    prev_v = {}
    for i in range(NT):
        for which, wbase, wwidth, dst, act in (("v", 2 * 128 * nq, nv, o_v, None), ("g", 0, ng, o_gate, AF.Silu)):
            for oc in range(wwidth // 512):
                pb = (i * 8 + (0 if which == "v" else 4) + oc) % 2
                mms = []
                for h in range(8):
                    if which == "v":
                        rhs = w_sb[:, h * NCOLS + wbase + oc * 512: h * NCOLS + wbase + (oc + 1) * 512]
                    else:
                        rhs = wg_sb[:, h * ng + oc * 512: h * ng + (oc + 1) * 512]
                    mms.append(S.add("tensor", lambda t, h=h, i=i, rhs=rhs, pb=pb: t.matmul(
                        out=ps_v[pb][:], lhsT=xn_fm[:, h * TOWN + i * 128:h * TOWN + (i + 1) * 128],
                        rhs=rhs, start=(h == 0), stop=(h == 7)),
                        deps=([ld_w[h] if which == "v" else ld_wg[h]] + (xn_all if h == 0 else [])
                              + ([prev_v.get((pb, 'ev'))] if h == 0 else []))))
                if act is None:
                    ev = S.add("vector", lambda v, pb=pb: v.tensor_copy(out=vstg[pb][:], in_=ps_v[pb][:]),
                               deps=[mms[-1], prev_v.get((pb, 'sto'))])
                else:
                    ev = S.add("scalar", lambda s, pb=pb: s.activation(out=vstg[pb][:], in_=ps_v[pb][:], func=AF.Silu),
                               deps=[mms[-1], prev_v.get((pb, 'sto'))])
                sto = S.dma("sync", dst[i * 128:(i + 1) * 128, oc * 512:(oc + 1) * 512], vstg[pb][:], deps=[ev])
                outs.append(sto)
                prev_v[(pb, 'ev')] = ev
                prev_v[(pb, 'sto')] = sto

    return outs



def shard_tokens(arr_btd):
    out = []
    for p in range(4):
        for half in range(2):
            out.append(np.ascontiguousarray(arr_btd[p, half * TOWN:(half + 1) * TOWN]))
    return out


def run_l0prep(inputs, run=run_bass_kernel_spmd):
    x = inputs["L0_attn_hidden_states"]
    cq, sq_, ck, sk = xpos_tables()
    rotT = rot_matT()
    wqkv = np.ascontiguousarray(inputs["L0_attn_W_qkv"], dtype=np.float32)
    wg = np.ascontiguousarray(inputs["L0_attn_W_g"], dtype=np.float32)
    nc = build_l0prep()
    in_maps = []
    for p in range(4):
        for half in range(2):
            sl = slice(half * TOWN, (half + 1) * TOWN)
            in_maps.append({
                "x": np.ascontiguousarray(x[p, sl]),
                "wqkv": wqkv, "wg": wg,
                "cos_q": np.ascontiguousarray(cq[sl].T),
                "sin_q": np.ascontiguousarray(sq_[sl].T),
                "cos_k": np.ascontiguousarray(ck[sl].T),
                "sin_k": np.ascontiguousarray(sk[sl].T),
                "rotmT": rotT, "ident": IDENT_BF,
            })
    res = run(nc, in_maps, core_ids=list(range(8)))
    return res.results






def build_l0ret():
    nc = bass.Bass("TRN2", target_bir_lowering=False, debug=False)
    io = {}
    q_ = nc.dram_tensor("qh", [4, 128, T], BF16, kind="ExternalInput").ap()
    k_ = nc.dram_tensor("kh", [4, 128, T], BF16, kind="ExternalInput").ap()
    v_ = nc.dram_tensor("vh", [4, T, 256], BF16, kind="ExternalInput").ap()
    g_ = nc.dram_tensor("gh", [4, T, 256], BF16, kind="ExternalInput").ap()
    io["qh"] = lambda m: q_[m]
    io["kh"] = lambda m: k_[m]
    io["vh"] = lambda m: v_[m]
    io["gh"] = lambda m: g_[m]
    io["decay"] = nc.dram_tensor("decay", [1, 8], F32, kind="ExternalInput").ap()
    io["causal"] = nc.dram_tensor("causal", [4, 128, 512], F32, kind="ExternalInput").ap()
    io["ident"] = nc.dram_tensor("ident", [128, 128], BF16, kind="ExternalInput").ap()
    io["ones128"] = nc.dram_tensor("ones128", [1, 128], F32, kind="ExternalInput").ap()
    nc.dram_tensor("hoff", [1, 1], F32, kind="ExternalInput").ap()
    gfm_ = nc.dram_tensor("gfm", [4, 256, T], BF16, kind="ExternalOutput").ap()
    io["gfm"] = lambda m: gfm_[m]
    es = contextlib.ExitStack()
    S = Sched()
    outs = emit_l0ret(nc, S, es, io, [])
    S.emit(nc, es, final_waits=outs)
    es.close()
    return nc


def emit_l0ret(nc, S, es, io, bar, pfx="r_"):
    i_q = io["qh"]; i_k = io["kh"]; i_v = io["vh"]; i_g = io["gh"]
    i_dec = io["decay"]; i_caus = io["causal"]; i_id = io["ident"]; i_ones = io["ones128"]
    o_gfm = io["gfm"]
    sb = lambda nme, s, d=F32: es.enter_context(nc.sbuf_tensor(pfx + nme, s, d))
    psum = lambda nme, s, d=F32: es.enter_context(nc.psum_tensor(pfx + nme, s, d))
    outs = []

    q_sb = sb("q_sb", [128, T], BF16)
    k_sb = sb("k_sb", [128, T], BF16)
    v_sb = sb("v_sb", [128, 32 * 256], BF16)
    g_sb = sb("g_sb", [128, 32 * 256], BF16)
    gfm_sb = sb("gfm_sb", [128, 2 * T], BF16)
    dec_sb = sb("dec_sb", [1, 8])
    lnd_sb = sb("lnd_sb", [1, 8])
    lndc = sb("lndc", [128, 8])
    ones_sb = sb("ones_sb", [1, 128])
    id_sb = sb("id_sb", [128, 128], BF16)
    caus_sb = sb("caus_sb", [128, 4 * 512])
    iob = sb("iob", [128, 512])
    io2 = sb("io2", [128, 36])
    Bt = [sb(f"Bt{v}", [128, 512]) for v in range(5)]
    dpow = sb("dpow", [128, 36])
    Pbf = [sb(f"Pbf{i}", [128, 512], BF16) for i in range(2)]
    sqt = sb("sqt", [128, 256])
    ot_sb = sb("ot_sb", [128, 256])
    s1c = sb("s1c", [128, 1]); s2c = sb("s2c", [128, 1])
    muc = sb("muc", [128, 1]); m2c = sb("m2c", [128, 1]); varc = sb("varc", [128, 1])
    sqc = sb("sqc", [128, 1]); rsc = sb("rsc", [128, 1])
    nmt = sb("nmt", [128, 256])
    gtd = sb("gtd", [128, 256], BF16)

    ps_ret = [psum(f"ps_ret{i}", [128, 512]) for i in range(2)]
    ps_av = [psum(f"ps_av{i}", [128, 512]) for i in range(4)]
    ps_tr = [psum("ps_tr0", [128, 1024], BF16)]

    ld_dec = S.dma("sync", dec_sb[:], i_dec, deps=bar)
    ld_ones = S.dma("sync", ones_sb[:], i_ones, deps=bar)
    ld_id = S.dma("sync", id_sb[:], i_id, deps=bar)
    ld_caus = [S.dma("sync", caus_sb[:, v * 512:(v + 1) * 512], i_caus[v], deps=bar) for v in range(4)]
    io1 = S.add("gpsimd", lambda g: g.iota(out=iob[:], pattern=[[1, 512]], base=0, channel_multiplier=-1,
                                           allow_small_or_imprecise_dtypes=True), deps=bar)
    io2o = S.add("gpsimd", lambda g: g.iota(out=io2[:], pattern=[[128, 36]], base=-384, channel_multiplier=0,
                                            allow_small_or_imprecise_dtypes=True), deps=bar)
    lno = S.add("scalar", lambda s: s.activation(out=lnd_sb[:], in_=dec_sb[:], func=AF.Ln), deps=[ld_dec])

    prev_map = {}
    for m in range(4):
        ldq = S.dma("sync", q_sb[:], i_q(m), deps=[prev_map.get('q')] + bar)
        ldk = S.dma("sync", k_sb[:], i_k(m), deps=[prev_map.get('k')] + bar)
        ldv = S.dma("sync", v_sb[:].rearrange("p (st c) -> p st c", c=256),
                    i_v(m).rearrange("(st p) c -> p st c", p=128), deps=[prev_map.get('v')] + bar)
        ldg = S.dma("sync", g_sb[:].rearrange("p (st c) -> p st c", c=256),
                    i_g(m).rearrange("(st p) c -> p st c", p=128), deps=[prev_map.get('g')] + bar)
        rep = S.add("tensor", lambda t: t.matmul(out=ps_ret[0][:, 0:8], lhsT=ones_sb[:], rhs=lnd_sb[:],
                                                 start=True, stop=True),
                    deps=[lno, ld_ones, prev_map.get('ret0')])
        cpl = S.add("vector", lambda v: v.tensor_copy(out=lndc[:], in_=ps_ret[0][:, 0:8]), deps=[rep, prev_map.get('lndc')])
        bfull = S.add("scalar", lambda s, m=m: s.activation(out=Bt[4][:], in_=iob[:], func=AF.Exp,
                                                            scale=lndc[:, m:m + 1]), deps=[cpl, io1, prev_map.get('Bfull')])
        bvs = [S.add("vector", lambda v, v_=v_: v.tensor_tensor(out=Bt[v_][:], in0=Bt[4][:],
                     in1=caus_sb[:, v_ * 512:(v_ + 1) * 512], op=ALU.mult),
                     deps=[bfull, ld_caus[v_], prev_map.get('Bv%d' % v_)]) for v_ in range(4)]
        dp1 = S.add("scalar", lambda s, m=m: s.activation(out=dpow[:], in_=io2[:], func=AF.Exp,
                                                          scale=lndc[:, m:m + 1]), deps=[cpl, io2o, prev_map.get('dpow')])
        dp2 = S.add("vector", lambda v: v.tensor_scalar(out=dpow[:], in0=dpow[:], scalar1=0.08838834764831845,
                                                        scalar2=None, op0=ALU.mult), deps=[dp1])

        ret_prev = {0: cpl, 1: None}
        stt_prev = {0: None, 1: None}
        av_last = {}
        gn_done_prev = [None, None, None, None]
        trev_prev = prev_map.get('trev')
        sto_g = []
        for tb in range(8):
            t0 = tb * 512
            n_st = tb * 4 + 4
            av_first = {}
            for st in range(n_st):
                s0 = st * 128
                rb = st % 2
                pb = st % 2
                mi = (t0 - s0) // 128 + 3
                var_idx = 4 if s0 <= t0 - 128 else (s0 - t0) // 128
                mm = S.add("tensor", lambda t, m=m, s0=s0, t0=t0, rb=rb: t.matmul(
                    out=ps_ret[rb][:], lhsT=k_sb[:, s0:s0 + 128], rhs=q_sb[:, t0:t0 + 512],
                    start=True, stop=True),
                    deps=[ldq, ldk, ret_prev[rb]])
                stt = S.add("vector", lambda v, rb=rb, pb=pb, mi=mi, var_idx=var_idx: v.scalar_tensor_tensor(
                    out=Pbf[pb][:], in0=ps_ret[rb][:], scalar=dpow[:, mi:mi + 1],
                    in1=Bt[var_idx][:], op0=ALU.mult, op1=ALU.mult),
                    deps=[mm, dp2, (bvs[var_idx] if var_idx < 4 else bfull), stt_prev[pb]])
                ret_prev[rb] = stt
                avd = []
                for ts4 in range(4):
                    first = (st == 0)
                    last = (st == n_st - 1)
                    av = S.add("tensor", lambda t, pb=pb, ts4=ts4, st=st, first=first, last=last: t.matmul(
                        out=ps_av[ts4][:, 0:256], lhsT=Pbf[pb][:, ts4 * 128:(ts4 + 1) * 128],
                        rhs=v_sb[:, st * 256:(st + 1) * 256],
                        start=first, stop=last),
                        deps=[stt, ldv] + ([gn_done_prev[ts4]] if first else []))
                    avd.append(av)
                    av_last[ts4] = av
                stt_prev[pb] = avd[-1]
            for ts4 in range(4):
                tt = tb * 4 + ts4
                pav = ps_av[ts4]
                d0 = av_last[ts4]
                g0 = S.add("vector", lambda v, pav=pav: v.tensor_copy(out=ot_sb[:], in_=pav[:, 0:256]),
                           deps=[d0, prev_map.get('ot')])
                g1 = S.add("scalar", lambda s: s.activation(out=sqt[:], in_=ot_sb[:], func=AF.Square),
                           deps=[g0, prev_map.get('sqt')])
                g2 = S.add("vector", lambda v: v.reduce_sum(out=s1c[:], in_=ot_sb[:], axis=AX.X),
                           deps=[g0, prev_map.get('s1c')])
                g3 = S.add("vector", lambda v: v.reduce_sum(out=s2c[:], in_=sqt[:], axis=AX.X), deps=[g1, g2])
                g4 = S.add("vector", lambda v: v.tensor_scalar(out=muc[:], in0=s1c[:], scalar1=1.0 / 256, scalar2=None, op0=ALU.mult), deps=[g3])
                g5 = S.add("vector", lambda v: v.tensor_tensor(out=m2c[:], in0=muc[:], in1=muc[:], op=ALU.mult), deps=[g4])
                g6 = S.add("vector", lambda v: v.tensor_scalar(out=varc[:], in0=s2c[:], scalar1=1.0 / 256, scalar2=1e-5, op0=ALU.mult, op1=ALU.add), deps=[g5])
                g7 = S.add("vector", lambda v: v.tensor_sub(out=varc[:], in0=varc[:], in1=m2c[:]), deps=[g6])
                g8 = S.add("scalar", lambda s: s.activation(out=sqc[:], in_=varc[:], func=AF.Sqrt), deps=[g7])
                g9 = S.add("vector", lambda v: v.reciprocal(out=rsc[:], in_=sqc[:]), deps=[g8])
                g10 = S.add("vector", lambda v: v.tensor_scalar(out=nmt[:], in0=ot_sb[:],
                            scalar1=muc[:], scalar2=rsc[:], op0=ALU.subtract, op1=ALU.mult), deps=[g9, prev_map.get('nmt')])
                prev_map['ot'] = g10
                g11 = S.add("vector", lambda v, tt=tt: v.tensor_tensor(out=gtd[:], in0=nmt[:],
                            in1=g_sb[:, tt * 256:(tt + 1) * 256], op=ALU.mult), deps=[g10, ldg, prev_map.get('gtd')])
                gn_done_prev[ts4] = g10
                prev_map['sqt'] = g3; prev_map['s1c'] = g4; prev_map['nmt'] = g11
                trs = []
                for half in range(2):
                    hb = 0
                    tr = S.add("tensor", lambda t, half=half, hb=hb: t.transpose(
                        out=ps_tr[hb][:, 0:128], in_=gtd[:, half * 128:(half + 1) * 128], identity=id_sb[:]),
                        deps=[g11, ld_id, trev_prev])
                    ev = S.add("vector", lambda v, half=half, hb=hb, tt=tt: v.tensor_copy(
                        out=gfm_sb[:, half * T + tt * 128: half * T + (tt + 1) * 128], in_=ps_tr[hb][:, 0:128]),
                        deps=[tr, prev_map.get('gfm%d' % ((tt * 2 + half) % 4))])
                    prev_map['gfm%d' % ((tt * 2 + half) % 4)] = ev
                    trev_prev = ev
                    trs.append(ev)
                prev_map['gtd'] = trs[0]
        if io.get("gfm_3d"):
            sto0 = S.dma("sync", o_gfm(m)[0:128],
                         gfm_sb[:, 0:T].rearrange("p (s t) -> p s t", s=2), deps=[trev_prev])
            sto1 = S.dma("sync", o_gfm(m)[128:256],
                         gfm_sb[:, T:2 * T].rearrange("p (s t) -> p s t", s=2), deps=[trev_prev])
        else:
            sto0 = S.dma("sync", o_gfm(m)[0:128, :], gfm_sb[:, 0:T], deps=[trev_prev])
            sto1 = S.dma("sync", o_gfm(m)[128:256, :], gfm_sb[:, T:2 * T], deps=[trev_prev])
        outs += [sto0, sto1]
        prev_map.update({'q': sto0, 'k': sto0, 'v': sto0, 'g': sto0, 'ret0': sto0,
                         'lndc': bfull, 'Bfull': sto0, 'dpow': sto0, 'trev': sto1,
                         'Bv0': sto0, 'Bv1': sto0, 'Bv2': sto0, 'Bv3': sto0})
        prev_map['trev'] = sto1

    return outs



GELU_A = 0.044715
FFN_ROUNDS = 4
GELU_2C = 2.0 * 0.7978845608028654
BIGC = float(2 ** 20)


def build_l0ffn():
    nc = bass.Bass("TRN2", target_bir_lowering=False, debug=False)
    io = {}
    io["x"] = nc.dram_tensor("x", [TOWN, D], F32, kind="ExternalInput").ap()
    io["gfm"] = nc.dram_tensor("gfm", [2048, TOWN], BF16, kind="ExternalInput").ap()
    io["wproj"] = nc.dram_tensor("wproj", [2048, 1024], F32, kind="ExternalInput").ap()
    io["wgr"] = nc.dram_tensor("wgr", [4, 1024, 4], F32, kind="ExternalInput").ap()
    io["bg"] = nc.dram_tensor("bg", [4, 4], F32, kind="ExternalInput").ap()
    io["w1"] = nc.dram_tensor("w1", [4, 1024, 4096], F32, kind="ExternalInput").ap()
    io["b1t"] = nc.dram_tensor("b1t", [128, 128], F32, kind="ExternalInput").ap()
    io["w2"] = nc.dram_tensor("w2", [4, 4096, 1024], F32, kind="ExternalInput").ap()
    io["b2"] = nc.dram_tensor("b2", [4, 1024], F32, kind="ExternalInput").ap()
    io["identb"] = nc.dram_tensor("identb", [128, 128], BF16, kind="ExternalInput").ap()
    io["ones128"] = nc.dram_tensor("ones128", [1, 128], F32, kind="ExternalInput").ap()
    io["x1"] = nc.dram_tensor("x1", [TOWN, D], F32, kind="ExternalOutput").ap()
    io["xn1fm"] = nc.dram_tensor("xn1fm", [1024, TOWN], BF16, kind="ExternalOutput").ap()
    io["w1b"] = nc.dram_tensor("w1b", [4, 1024, 4096], BF16, kind="Internal").ap()
    io["w2b"] = nc.dram_tensor("w2b", [4, 4096, 1024], BF16, kind="Internal").ap()
    io["xpa"] = nc.dram_tensor("xpa", [TOWN, D], F32, kind="Internal").ap()
    io["updsum"] = nc.dram_tensor("updsum", [TOWN, D], F32, kind="Internal").ap()
    es = contextlib.ExitStack()
    S = Sched()
    outs = emit_l0ffn(nc, S, es, io, [])
    S.emit(nc, es, final_waits=outs)
    es.close()
    return nc


def emit_l0ffn(nc, S, es, io, bar, pfx="f_", es_early=None, bar_early=None, es_ln_ext=None):
    if es_early is None:
        es_early = es
    if bar_early is None:
        bar_early = bar
    i_x = io["x"]; i_gfm = io["gfm"]; i_wp = io["wproj"]
    i_wgr = io["wgr"]; i_bg = io["bg"]; i_w1 = io["w1"]; i_b1 = io["b1t"]
    i_w2 = io["w2"]; i_b2 = io["b2"]; i_idb = io["identb"]; i_ones = io["ones128"]
    o_x1 = io["x1"]; o_xn1 = io["xn1fm"]
    s_w1b = io["w1b"]; s_w2b = io["w2b"]; s_xpa = io["xpa"]; s_upd = io["updsum"]
    sb = lambda nme, s, d=F32: es.enter_context(nc.sbuf_tensor(pfx + nme, s, d))
    psum = lambda nme, s, d=F32: es.enter_context(nc.psum_tensor(pfx + nme, s, d))
    outs = []

    standalone = es_early is es
    cur_fm = es_early.enter_context(nc.sbuf_tensor(pfx + "cur_fm", [128, 8 * TOWN], BF16))
    id_sb = es_early.enter_context(nc.sbuf_tensor(pfx + "id_sb", [128, 128], BF16))
    ones_sb = es_early.enter_context(nc.sbuf_tensor(pfx + "ones_sb", [1, 128], F32))
    ps_ct = es_early.enter_context(nc.psum_tensor(pfx + "ps_ct", [128, 1024], BF16))
    es_ln = es_ln_ext if es_ln_ext is not None else contextlib.ExitStack()
    sbE = lambda nme, s, d=F32: es_ln.enter_context(nc.sbuf_tensor(pfx + nme, s, d))
    ld_id = S.dma("sync", id_sb[:], i_idb, deps=bar_early)
    ld_ones = S.dma("sync", ones_sb[:], i_ones, deps=bar_early)
    es2 = contextlib.ExitStack()
    sb2 = lambda nme, s, d=F32: es2.enter_context(nc.sbuf_tensor(pfx + nme, s, d))

    wc = sbE("wc", [128, 4096], BF16)
    cast_done = {}
    prev_c = [None, None]
    ci = 0
    for e in range(4):
        toks = []
        for hc in range(8):
            l1 = S.dma("gpsimd", wc[:], i_w1[e, hc * 128:(hc + 1) * 128, :],
                       deps=[prev_c[0]] + bar_early)
            st = S.dma("gpsimd", s_w1b[e, hc * 128:(hc + 1) * 128, :], wc[:],
                       deps=[l1])
            prev_c[0] = st
            toks.append(st)
        cast_done[('w1', e)] = toks[-1]
        toks = []
        for fc8 in range(8):
            l1 = S.dma("gpsimd", wc[:].rearrange("p (s d) -> p s d", d=1024),
                       i_w2[e, fc8 * 512:(fc8 + 1) * 512, :].rearrange("(s p) d -> p s d", p=128),
                       deps=[prev_c[0]] + bar_early)
            st = S.dma("gpsimd", s_w2b[e, fc8 * 512:(fc8 + 1) * 512, :].rearrange("(s p) d -> p s d", p=128),
                       wc[:].rearrange("p (s d) -> p s d", d=1024),
                       deps=[l1])
            prev_c[0] = st
            toks.append(st)
        cast_done[('w2', e)] = toks[-1]

    xt_sb = [sbE("xt_sb0", [128, 1024])]
    xnt_sb = sbE("xnt_sb", [128, 1024])
    xnb_sb = sbE("xnb_sb", [128, 1024], BF16)
    x2p = sbE("x2p", [128, 1024])
    s1p = sbE("s1p", [128, 1]); s2p = sbE("s2p", [128, 1]); mup = sbE("mup", [128, 1])
    m2p = sbE("m2p", [128, 1]); varp = sbE("varp", [128, 1]); sqp = sbE("sqp", [128, 1]); rsp = sbE("rsp", [128, 1])
    xpa_st = x2p
    prev_p = {}
    xpa_stos = []
    cur_init = []
    for i in range(NT):
        b = 0
        lx = S.dma("sync", xt_sb[b][:], i_x[i * 128:(i + 1) * 128, :], deps=prev_p.get(('x', b), []) + bar_early)
        c1 = S.add("scalar", lambda s, b=b: s.activation(out=x2p[:], in_=xt_sb[b][:], func=AF.Square),
                   deps=[lx, prev_p.get('x2p')])
        c2 = S.add("vector", lambda v, b=b: v.reduce_sum(out=s1p[:], in_=xt_sb[b][:], axis=AX.X),
                   deps=[lx, prev_p.get('s1p')])
        c3 = S.add("vector", lambda v: v.reduce_sum(out=s2p[:], in_=x2p[:], axis=AX.X), deps=[c1, c2])
        c4 = S.add("vector", lambda v: v.tensor_scalar(out=mup[:], in0=s1p[:], scalar1=1.0 / 1024, scalar2=None, op0=ALU.mult), deps=[c3])
        c5 = S.add("vector", lambda v: v.tensor_tensor(out=m2p[:], in0=mup[:], in1=mup[:], op=ALU.mult), deps=[c4])
        c6 = S.add("vector", lambda v: v.tensor_scalar(out=varp[:], in0=s2p[:], scalar1=1.0 / 1024, scalar2=1e-5, op0=ALU.mult, op1=ALU.add), deps=[c5])
        c7 = S.add("vector", lambda v: v.tensor_sub(out=varp[:], in0=varp[:], in1=m2p[:]), deps=[c6])
        c8 = S.add("scalar", lambda s: s.activation(out=sqp[:], in_=varp[:], func=AF.Sqrt), deps=[c7, prev_p.get('sqp')])
        c9 = S.add("vector", lambda v: v.reciprocal(out=rsp[:], in_=sqp[:]), deps=[c8])
        c10 = S.add("vector", lambda v, b=b: v.tensor_scalar(out=xnt_sb[:], in0=xt_sb[b][:],
                    scalar1=mup[:], scalar2=rsp[:], op0=ALU.subtract, op1=ALU.mult),
                    deps=[c9, prev_p.get('xnt')])
        prev_p['s1p'] = c4; prev_p['sqp'] = c9
        cb10 = S.add("vector", lambda v: v.tensor_copy(out=xnb_sb[:], in_=xnt_sb[:]), deps=[c10, prev_p.get('xnb')])
        a0 = S.add("vector", lambda v, b=b: v.tensor_tensor(out=xpa_st[:], in0=xnt_sb[:], in1=xt_sb[b][:],
                   op=ALU.add), deps=[c10, prev_p.get('xpa')])
        prev_p['xnt'] = a0
        sto = S.dma("sync", s_xpa[i * 128:(i + 1) * 128, :], xpa_st[:], deps=[a0])
        prev_p['xpa'] = sto
        prev_p['x2p'] = sto
        prev_p[('x', b)] = [sto, c1]
        xpa_stos.append(sto)
        for c in range(8):
            tr = S.add("tensor", lambda t, c=c: t.transpose(
                out=ps_ct[:, 0:128], in_=xnb_sb[:, c * 128:(c + 1) * 128], identity=id_sb[:]),
                deps=[cb10, ld_id, prev_p.get('trx')])
            cc = S.add("vector", lambda v, c=c, i=i: v.tensor_copy(
                out=cur_fm[:, c * TOWN + i * 128: c * TOWN + (i + 1) * 128], in_=ps_ct[:, 0:128]),
                deps=[tr])
            prev_p['trx'] = cc
            cur_init.append(cc)
        prev_p['xnb'] = prev_p['trx']

    es2b = contextlib.ExitStack()
    sb2b = lambda nme, s, d=F32: es2b.enter_context(nc.sbuf_tensor(pfx + nme, s, d))
    ps2b = lambda nme, s, d=F32: es2b.enter_context(nc.psum_tensor(pfx + nme, s, d))
    wp_sb = sb2b("wp_sb", [128, 16 * 1024], BF16)
    stg_p = [sb2b(f"stg_p{i}", [128, 512]) for i in range(2)]
    ps_pj = [ps2b(f"ps_pj{i}", [128, 512]) for i in range(2)]
    ld_wp = [S.dma("gpsimd", wp_sb[:, vc * 1024:(vc + 1) * 1024], i_wp[vc * 128:(vc + 1) * 128, :], deps=bar)
             for vc in range(16)]
    if "goff" in io:
        gt4_sb = [sb2b(f"gt4_sb{i}", [128, 16 * 512], BF16) for i in range(2)]
        goff = io["goff"]
    else:
        gt_sb = [sb2b(f"gt_sb{i}", [128, 16 * 128], BF16) for i in range(2)]
    for i in range(NT):
        b = i % 2
        if "goff" in io:
            gb = (i // 4) % 2
            if i % 4 == 0:
                lgblk = S.dma("sync", gt4_sb[gb][:].rearrange("p (c t) -> p c t", t=512),
                              lambda i=i: i_gfm[:, bass.ds(goff[0] + (i // 4) * 512, 512)].rearrange("(c p) t -> p c t", p=128),
                              deps=[prev_p.get(('g4', gb)), io.get("goff_dep")] + bar)
                prev_p[('lg4', gb)] = lgblk
            lg = prev_p[('lg4', gb)]
            gt_cur = gt4_sb[gb]
            gt_off = lambda vc, i=i, gt_cur=gt_cur: gt_cur[:, vc * 512 + (i % 4) * 128: vc * 512 + (i % 4) * 128 + 128]
        else:
            lg = S.dma("sync", gt_sb[b][:].rearrange("p (c t) -> p c t", t=128),
                       i_gfm[:, i * 128:(i + 1) * 128].rearrange("(c p) t -> p c t", p=128),
                       deps=[prev_p.get(('g', b))] + bar)
            gt_b = gt_sb[b]
            gt_off = lambda vc, gt_b=gt_b: gt_b[:, vc * 128:(vc + 1) * 128]
        for dc in range(2):
            pb = (i * 2 + dc) % 2
            mms = [S.add("tensor", lambda t, vc=vc, dc=dc, pb=pb, gt_off=gt_off: t.matmul(
                out=ps_pj[pb][:], lhsT=gt_off(vc),
                rhs=wp_sb[:, vc * 1024 + dc * 512: vc * 1024 + (dc + 1) * 512],
                start=(vc == 0), stop=(vc == 15)),
                deps=[lg, ld_wp[vc]] + ([prev_p.get(('pp', pb))] if vc == 0 else []))
                for vc in range(16)]
            ev = S.add("vector", lambda v, pb=pb: v.tensor_copy(out=stg_p[pb][:], in_=ps_pj[pb][:]),
                       deps=[mms[-1], prev_p.get(('stg', pb))])
            acc = S.dma_acc("gpsimd", s_xpa[i * 128:(i + 1) * 128, dc * 512:(dc + 1) * 512],
                            stg_p[pb][:], ALU.add, deps=[ev, xpa_stos[i]])
            prev_p[('pp', pb)] = ev
            prev_p[('stg', pb)] = acc
            prev_p[('xpa_acc', i, dc)] = acc
        prev_p[('g', b)] = prev_p[('stg', 1)]
        if "goff" in io and i % 4 == 3:
            prev_p[('g4', (i // 4) % 2)] = prev_p[('stg', (i * 2 + 1) % 2)]
    xpa_stos = [prev_p[('xpa_acc', i, dc)] for i in range(NT) for dc in range(2)]
    xpa_accs = {i: [prev_p[('xpa_acc', i, 0)], prev_p[('xpa_acc', i, 1)]] for i in range(NT)}
    es2b.close()
    ln_end = [prev_p['xpa'], cur_init[-1], cast_done[('w1', 3)], cast_done[('w2', 3)]]
    es_ln.close()
    bar = bar + ln_end
    es2.close()

    ld_cur = None
    hmid = sb("hmid", [128, 32 * 1024], BF16)
    w2res = sb("w2res", [128, 32 * 1024], BF16)
    w1blk = [sb(f"w1blk{i}", [128, 8 * 512], BF16) for i in range(2)]
    upd = sb("upd", [128, 8 * 1024], BF16)
    wgr_sb = sb("wgr_sb", [128, 4 * 8 * 4], BF16)
    bgr_sb = sb("bgr_sb", [1, 16], BF16)
    b1_sb = sb("b1_sb", [128, 4 * 32])
    b2_sb = sb("b2_sb", [1, 1024], BF16)
    onesb_sb = sb("onesb_sb", [1, 128], BF16)
    xg = sb("xg", [128, 512])
    tg2 = sb("tg2", [128, 512])
    lgt = sb("lgt", [128, 4])
    io4 = sb("io4", [128, 4])
    io4b = sb("io4b", [128, 4])
    m1c = sb("m1c", [128, 1]); m2cg = sb("m2cg", [128, 1])
    eqt = sb("eqt", [128, 4]); cnd = sb("cnd", [128, 4])
    i1c = sb("i1c", [128, 1]); i2c = sb("i2c", [128, 1])
    oh1 = sb("oh1", [128, 4]); oh2 = sb("oh2", [128, 4])
    lg2 = sb("lg2", [128, 4])
    w1gc = sb("w1gc", [128, 1]); w2gc = sb("w2gc", [128, 1])
    cwt1 = sb("cwt1", [128, 4]); cwt2 = sb("cwt2", [128, 4])
    cw_all = sb("cw_all", [128, 16 * 4])
    us_sb = sb("us_sb", [128, 1024])
    xn1t = sb("xn1t", [128, 1024], BF16)
    ps_h = [psum(f"ps_h{i}", [128, 512]) for i in range(2)]
    ps_y = [psum(f"ps_y{i}", [128, 512]) for i in range(4)]
    ps_g = psum("ps_g", [128, 512])

    ld_wgr = S.dma("gpsimd", wgr_sb[:].rearrange("p (r hc e) -> p r hc e", hc=8, e=4),
                   i_wgr.rearrange("r (hc p) e -> p r hc e", p=128), deps=bar)
    ld_bgr = S.dma("gpsimd", bgr_sb[:].rearrange("a (r e) -> a r e", e=4), i_bg.rearrange("(o r) e -> o r e", o=1),
                   deps=bar)
    ld_b1 = S.dma("sync", b1_sb[:], i_b1, deps=bar)

    cob = S.add("vector", lambda v: v.tensor_copy(out=onesb_sb[:], in_=ones_sb[:]), deps=[ld_ones])
    iog1 = S.add("gpsimd", lambda g: g.iota(out=io4[:], pattern=[[1, 4]], base=0, channel_multiplier=0,
                                            allow_small_or_imprecise_dtypes=True), deps=bar)
    iog2 = S.add("gpsimd", lambda g: g.iota(out=io4b[:], pattern=[[1, 4]], base=int(BIGC), channel_multiplier=0,
                                            allow_small_or_imprecise_dtypes=True), deps=bar)

    prev_f = {'curdep': cur_init}
    upddeps = {}
    for r in range(FFN_ROUNDS):
        cw_deps = []
        for gt in range(NT):
            mmg = [S.add("tensor", lambda t, hc=hc, gt=gt, r=r: t.matmul(
                out=ps_g[:, 0:4], lhsT=cur_fm[:, hc * TOWN + gt * 128: hc * TOWN + (gt + 1) * 128],
                rhs=wgr_sb[:, r * 32 + hc * 4: r * 32 + (hc + 1) * 4],
                start=(hc == 0), stop=False),
                deps=[ld_wgr, ld_cur] + ([prev_f.get('psg')] if hc == 0 else []) + prev_f.get('curdep', []))
                for hc in range(8)]
            mgb = S.add("tensor", lambda t, r=r: t.matmul(
                out=ps_g[:, 0:4], lhsT=onesb_sb[:], rhs=bgr_sb[:, r * 4:(r + 1) * 4],
                start=False, stop=True), deps=[ld_bgr, cob])
            e0 = S.add("vector", lambda v: v.tensor_copy(out=lgt[:], in_=ps_g[:, 0:4]), deps=[mgb, prev_f.get('lgt')])
            prev_f['psg'] = e0
            e1 = S.add("vector", lambda v: v.reduce_max(out=m1c[:], in_=lgt[:], axis=AX.X), deps=[e0, prev_f.get('m1c')])
            e2 = S.add("vector", lambda v: v.tensor_scalar(out=eqt[:], in0=lgt[:], scalar1=m1c[:], scalar2=None, op0=ALU.is_ge), deps=[e1, prev_f.get('eqt')])
            e3 = S.add("vector", lambda v: v.scalar_tensor_tensor(out=cnd[:], in0=eqt[:], scalar=-BIGC, in1=io4b[:], op0=ALU.mult, op1=ALU.add), deps=[e2, iog2, prev_f.get('cnd')])
            e4 = S.add("vector", lambda v: v.tensor_reduce(out=i1c[:], in_=cnd[:], axis=AX.X, op=ALU.min), deps=[e3, prev_f.get('i1c')])
            e6 = S.add("vector", lambda v: v.tensor_scalar(out=oh1[:], in0=io4[:], scalar1=i1c[:], scalar2=None, op0=ALU.is_equal), deps=[e4, iog1, prev_f.get('oh1')])
            e7 = S.add("vector", lambda v: v.scalar_tensor_tensor(out=lg2[:], in0=oh1[:], scalar=-BIGC, in1=lgt[:], op0=ALU.mult, op1=ALU.add), deps=[e6, prev_f.get('lg2')])
            e8 = S.add("vector", lambda v: v.reduce_max(out=m2cg[:], in_=lg2[:], axis=AX.X), deps=[e7, prev_f.get('m2cg')])
            e9 = S.add("vector", lambda v: v.tensor_scalar(out=eqt[:], in0=lg2[:], scalar1=m2cg[:], scalar2=None, op0=ALU.is_ge), deps=[e8, e3])
            e10 = S.add("vector", lambda v: v.scalar_tensor_tensor(out=cnd[:], in0=eqt[:], scalar=-BIGC, in1=io4b[:], op0=ALU.mult, op1=ALU.add), deps=[e9, e4])
            e11 = S.add("vector", lambda v: v.tensor_reduce(out=i2c[:], in_=cnd[:], axis=AX.X, op=ALU.min), deps=[e10, prev_f.get('i2c')])
            e13 = S.add("vector", lambda v: v.tensor_scalar(out=oh2[:], in0=io4[:], scalar1=i2c[:], scalar2=None, op0=ALU.is_equal), deps=[e11, prev_f.get('oh2')])
            e14 = S.add("vector", lambda v: v.tensor_tensor(out=m2cg[:], in0=m1c[:], in1=m2cg[:], op=ALU.subtract), deps=[e8, e13, e9])
            e15 = S.add("scalar", lambda s: s.activation(out=w1gc[:], in_=m2cg[:], func=AF.Sigmoid), deps=[e14, prev_f.get('w1g')])
            e16 = S.add("vector", lambda v: v.tensor_scalar(out=w2gc[:], in0=w1gc[:], scalar1=-1.0, scalar2=1.0, op0=ALU.mult, op1=ALU.add), deps=[e15, prev_f.get('w2g')])
            e17 = S.add("vector", lambda v: v.tensor_scalar(out=cwt1[:], in0=oh1[:], scalar1=w1gc[:], scalar2=None, op0=ALU.mult), deps=[e16, e7, prev_f.get('cwt1')])
            e18 = S.add("vector", lambda v: v.tensor_scalar(out=cwt2[:], in0=oh2[:], scalar1=w2gc[:], scalar2=None, op0=ALU.mult), deps=[e16, e13, prev_f.get('cwt2')])
            e19 = S.add("vector", lambda v, gt=gt: v.tensor_tensor(out=cw_all[:, gt * 4:(gt + 1) * 4], in0=cwt1[:], in1=cwt2[:], op=ALU.add), deps=[e17, e18, prev_f.get('cw%d' % gt)])
            prev_f['oh1'] = e17; prev_f['oh2'] = e18; prev_f['w1g'] = e16; prev_f['w2g'] = e17
            prev_f['m1c'] = e17; prev_f['m2cg'] = e15; prev_f['eqt'] = e10; prev_f['cnd'] = e11
            prev_f['i1c'] = e6; prev_f['i2c'] = e13; prev_f['lg2'] = e9
            prev_f['lgt'] = e9; prev_f['cwt1'] = e19; prev_f['cwt2'] = e19
            prev_f['cw%d' % gt] = e19
            cw_deps.append(e19)
        for tblock in range(2):
            upd_dep = {}
            for e in range(4):
                ld_w2r = S.dma("scalar", w2res[:].rearrange("p (fc d) -> p fc d", d=1024),
                               s_w2b[e].rearrange("(fc p) d -> p fc d", p=128),
                               deps=[cast_done[('w2', e)]] + prev_f.get('w2res', []))
                ld_b2 = S.dma("gpsimd", b2_sb[:], i_b2[e:e + 1, :], deps=prev_f.get('b2use', []) + bar)
                hm_last = {}
                for fb in range(8):
                    wb_ = fb % 2
                    lw1 = S.dma("sync", w1blk[wb_][:].rearrange("p (hc f) -> p hc f", f=512),
                                s_w1b[e][:, fb * 512:(fb + 1) * 512].rearrange("(hc p) f -> p hc f", p=128),
                                deps=[cast_done[('w1', e)]] + prev_f.get(('w1b', wb_), []))
                    w1_readers = []
                    for fc4 in range(4):
                        fc = fb * 4 + fc4
                        b1col = b1_sb[:, e * 32 + fc: e * 32 + fc + 1]
                        for tc in range(2):
                            pb = (fc * 2 + tc) % 2
                            tcol = tblock * 1024 + tc * 512
                            mms = [S.add("tensor", lambda t, hc=hc, fc4=fc4, tcol=tcol, pb=pb, wb_=wb_: t.matmul(
                                out=ps_h[pb][:], lhsT=w1blk[wb_][:, hc * 512 + fc4 * 128: hc * 512 + (fc4 + 1) * 128],
                                rhs=cur_fm[:, hc * TOWN + tcol: hc * TOWN + tcol + 512],
                                start=(hc == 0), stop=(hc == 7)),
                                deps=[lw1, ld_cur] + ([prev_f.get(('psh', pb))] if hc == 0 else []) + prev_f.get('curdep', []))
                                for hc in range(8)]
                            w1_readers.append(mms[-1])
                            g1 = S.add("scalar", lambda s, pb=pb, b1col=b1col: s.activation(
                                out=xg[:], in_=ps_h[pb][:], func=AF.Identity, bias=b1col),
                                deps=[mms[-1], ld_b1, prev_f.get('xg')])
                            prev_f[('psh', pb)] = g1
                            g2 = S.add("scalar", lambda s: s.activation(out=tg2[:], in_=xg[:], func=AF.Square), deps=[g1, prev_f.get('tg2')])
                            g3 = S.add("vector", lambda v: v.tensor_scalar(out=tg2[:], in0=tg2[:], scalar1=GELU_A, scalar2=1.0, op0=ALU.mult, op1=ALU.add), deps=[g2])
                            g4 = S.add("vector", lambda v: v.tensor_tensor(out=tg2[:], in0=tg2[:], in1=xg[:], op=ALU.mult), deps=[g3])
                            g5 = S.add("scalar", lambda s: s.activation(out=tg2[:], in_=tg2[:], func=AF.Sigmoid, scale=GELU_2C), deps=[g4])
                            g6 = S.add("vector", lambda v, fc=fc, tc=tc: v.tensor_tensor(
                                out=hmid[:, fc * 1024 + tc * 512: fc * 1024 + (tc + 1) * 512],
                                in0=xg[:], in1=tg2[:], op=ALU.mult), deps=[g5, prev_f.get(('hm', fc, tc))])
                            prev_f['xg'] = g6; prev_f['tg2'] = g6
                            prev_f[('hm', fc, tc)] = g6
                            hm_last[(fc, tc)] = g6
                    prev_f[('w1b', wb_)] = w1_readers
                for tg in range(2):
                    for tt4 in range(4):
                        lt = tg * 4 + tt4
                        gt = tblock * 8 + lt
                        for dc in range(2):
                            pi = (lt * 2 + dc) % 4
                            pybank = ps_y[pi]
                            mms = [S.add("tensor", lambda t, fc=fc, lt=lt, dc=dc, pybank=pybank: t.matmul(
                                out=pybank[:], lhsT=hmid[:, fc * 1024 + lt * 128: fc * 1024 + (lt + 1) * 128],
                                rhs=w2res[:, fc * 1024 + dc * 512: fc * 1024 + (dc + 1) * 512],
                                start=(fc == 0), stop=False),
                                deps=[ld_w2r, hm_last[(fc, lt // 4)]] + ([prev_f.get(('psy', pi))] if fc == 0 else []))
                                for fc in range(32)]
                            myb = S.add("tensor", lambda t, dc=dc, pybank=pybank: t.matmul(
                                out=pybank[:], lhsT=onesb_sb[:], rhs=b2_sb[:, dc * 512:(dc + 1) * 512],
                                start=False, stop=True), deps=[ld_b2, cob])
                            ui = lt * 2 + dc
                            if e == 0:
                                ev = S.add("vector", lambda v, pybank=pybank, ui=ui, gt=gt, e=e: v.tensor_scalar(
                                    out=upd[:, ui * 512:(ui + 1) * 512], in0=pybank[:],
                                    scalar1=cw_all[:, gt * 4 + e: gt * 4 + e + 1], scalar2=None, op0=ALU.mult),
                                    deps=[myb, cw_deps[gt], upd_dep.get(ui)] + upddeps.get(ui, []))
                            else:
                                ev = S.add("vector", lambda v, pybank=pybank, ui=ui, gt=gt, e=e: v.scalar_tensor_tensor(
                                    out=upd[:, ui * 512:(ui + 1) * 512], in0=pybank[:],
                                    scalar=cw_all[:, gt * 4 + e: gt * 4 + e + 1],
                                    in1=upd[:, ui * 512:(ui + 1) * 512], op0=ALU.mult, op1=ALU.add),
                                    deps=[myb, cw_deps[gt], upd_dep.get(ui)])
                            upd_dep[ui] = ev
                            upddeps[ui] = []
                            prev_f[('psy', pi)] = ev
                prev_f['w2res'] = [upd_dep[15]]
                prev_f['b2use'] = [upd_dep[15]]
            curdeps = []
            for lt in range(8):
                gt = tblock * 8 + lt
                if r > 0:
                    lus = S.dma("sync", us_sb[:], s_upd[gt * 128:(gt + 1) * 128, :],
                                deps=prev_f.get('usdep', []))
                    a1 = S.add("vector", lambda v, lt=lt: v.tensor_tensor(
                        out=us_sb[:], in0=us_sb[:],
                        in1=upd[:, lt * 1024:(lt + 1) * 1024], op=ALU.add),
                        deps=[lus, upd_dep[lt * 2], upd_dep[lt * 2 + 1]])
                else:
                    a1 = S.add("vector", lambda v, lt=lt: v.tensor_copy(
                        out=us_sb[:], in_=upd[:, lt * 1024:(lt + 1) * 1024]),
                        deps=[upd_dep[lt * 2], upd_dep[lt * 2 + 1]] + prev_f.get('usdep', []))
                upddeps.setdefault(lt * 2, []).append(a1)
                upddeps.setdefault(lt * 2 + 1, []).append(a1)
                if r < FFN_ROUNDS - 1:
                    sus = S.dma("sync", s_upd[gt * 128:(gt + 1) * 128, :], us_sb[:], deps=[a1])
                    prev_f['usdep'] = [sus]
                    for c in range(8):
                        tr = S.add("tensor", lambda t, lt=lt, c=c: t.transpose(
                            out=ps_ct[:, 0:128], in_=upd[:, lt * 1024 + c * 128: lt * 1024 + (c + 1) * 128],
                            identity=id_sb[:]),
                            deps=[upd_dep[lt * 2 + c // 4], ld_id] + prev_f.get('ctdep', []))
                        ad = S.add("vector", lambda v, c=c, gt=gt: v.tensor_tensor(
                            out=cur_fm[:, c * TOWN + gt * 128: c * TOWN + (gt + 1) * 128],
                            in0=cur_fm[:, c * TOWN + gt * 128: c * TOWN + (gt + 1) * 128],
                            in1=ps_ct[:, 0:128], op=ALU.add),
                            deps=[tr])
                        prev_f['ctdep'] = [ad]
                        upddeps.setdefault(lt * 2 + c // 4, []).append(tr)
                        curdeps.append(ad)
                else:
                    a2 = S.dma_acc("gpsimd", us_sb[:], s_xpa[gt * 128:(gt + 1) * 128, :], ALU.add,
                                   deps=xpa_accs[gt] + [a1])
                    prev_f['usdep'] = [a2]
                    sx1 = S.dma("sync", o_x1[gt * 128:(gt + 1) * 128, :], us_sb[:], deps=[a2])
                    outs.append(sx1)
                    l1 = S.add("scalar", lambda s: s.activation(out=xg[:, 0:512], in_=us_sb[:, 0:512], func=AF.Square), deps=[a2, prev_f.get('xg'), prev_f.get('lnx2a')])
                    l1b = S.add("scalar", lambda s: s.activation(out=tg2[:, 0:512], in_=us_sb[:, 512:1024], func=AF.Square), deps=[a2, prev_f.get('tg2'), l1, prev_f.get('lnx2b')])
                    l2 = S.add("vector", lambda v: v.reduce_sum(out=m1c[:], in_=us_sb[:], axis=AX.X), deps=[a2, prev_f.get('m1c')])
                    l3 = S.add("vector", lambda v: v.reduce_sum(out=m2cg[:], in_=xg[:, 0:512], axis=AX.X), deps=[l1, l2, prev_f.get('m2cg')])
                    l3b = S.add("vector", lambda v: v.reduce_sum(out=i1c[:], in_=tg2[:, 0:512], axis=AX.X), deps=[l1b, l3, prev_f.get('i1c')])
                    l4 = S.add("vector", lambda v: v.tensor_tensor(out=m2cg[:], in0=m2cg[:], in1=i1c[:], op=ALU.add), deps=[l3b])
                    l5 = S.add("vector", lambda v: v.tensor_scalar(out=m1c[:], in0=m1c[:], scalar1=1.0 / 1024, scalar2=None, op0=ALU.mult), deps=[l4, l2])
                    l6 = S.add("vector", lambda v: v.tensor_tensor(out=i1c[:], in0=m1c[:], in1=m1c[:], op=ALU.mult), deps=[l5])
                    l7 = S.add("vector", lambda v: v.tensor_scalar(out=m2cg[:], in0=m2cg[:], scalar1=1.0 / 1024, scalar2=None, op0=ALU.mult), deps=[l6])
                    l8 = S.add("vector", lambda v: v.tensor_tensor(out=m2cg[:], in0=m2cg[:], in1=i1c[:], op=ALU.subtract), deps=[l7])
                    l9 = S.add("vector", lambda v: v.tensor_scalar(out=m2cg[:], in0=m2cg[:], scalar1=1e-5, scalar2=None, op0=ALU.add), deps=[l8])
                    l10 = S.add("scalar", lambda s: s.activation(out=i2c[:], in_=m2cg[:], func=AF.Sqrt), deps=[l9, prev_f.get('i2c')])
                    l11 = S.add("vector", lambda v: v.reciprocal(out=i2c[:], in_=i2c[:]), deps=[l10])
                    l12 = S.add("vector", lambda v: v.tensor_scalar(out=xn1t[:], in0=us_sb[:],
                                scalar1=m1c[:], scalar2=i2c[:], op0=ALU.subtract, op1=ALU.mult),
                                deps=[l11, prev_f.get('xn1t')])
                    prev_f['lnx2a'] = l3; prev_f['lnx2b'] = l3b; prev_f['m1c'] = l12; prev_f['m2cg'] = l8
                    prev_f['i1c'] = l8; prev_f['i2c'] = l12; prev_f['xg'] = l3; prev_f['tg2'] = l3b
                    prev_f['x1dep'] = [l12, sx1]
                    for c in range(8):
                        tr = S.add("tensor", lambda t, c=c: t.transpose(
                            out=ps_ct[:, 0:128], in_=xn1t[:, c * 128:(c + 1) * 128], identity=id_sb[:]),
                            deps=[l12, ld_id] + prev_f.get('ctdep', []))
                        stg = S.add("vector", lambda v, c=c: v.tensor_copy(
                            out=upd[:, c * 128:(c + 1) * 128], in_=ps_ct[:, 0:128]),
                            deps=[tr] + upddeps.get(c // 4, []) + prev_f.get(('xn1s', c), []))
                        sto = S.dma("sync", o_xn1[c * 128:(c + 1) * 128, gt * 128:(gt + 1) * 128],
                                    upd[:, c * 128:(c + 1) * 128], deps=[stg])
                        outs.append(sto)
                        prev_f['ctdep'] = [stg]
                        prev_f[('xn1s', c)] = [sto]
                        upddeps.setdefault(c // 4, []).append(sto)
                    prev_f['xn1t'] = prev_f['ctdep'][0]
            if r < FFN_ROUNDS - 1:
                prev_f['curdep'] = curdeps

    return outs



RMS_EPS = 1.1920929e-07


def build_l1attn():
    nc = bass.Bass("TRN2", target_bir_lowering=False, debug=False)
    io = {}
    for nm, shp, dt in (("xn1fm", [1024, T], BF16), ("wq", [1024, 512], F32), ("wk", [1024, 512], F32),
                        ("wv", [1024, 512], F32), ("bqc", [128, 4], F32), ("bkc", [128, 4], F32),
                        ("bvrow", [1, 512], F32), ("fwq", [1024, 512], F32), ("fwk", [1024, 512], F32),
                        ("fwv", [1024, 512], F32), ("fbqc", [128, 4], F32), ("fbkc", [128, 4], F32),
                        ("fbvrow", [1, 512], F32), ("causal", [4, 128, 512], BF16),
                        ("ones128", [1, 128], F32), ("onescol", [128, 1], F32), ("identf", [128, 128], F32)):
        io[nm] = nc.dram_tensor(nm, shp, dt, kind="ExternalInput").ap()
    z1_ = nc.dram_tensor("z1", [8, 64, T], BF16, kind="ExternalOutput").ap()
    z2_ = nc.dram_tensor("z2", [8, 64, T], BF16, kind="ExternalOutput").ap()
    io["z1"] = lambda h: z1_[h]
    io["z2"] = lambda h: z2_[h]
    es = contextlib.ExitStack()
    S = Sched()
    outs = emit_l1attn(nc, S, es, io, [])
    S.emit(nc, es, final_waits=outs)
    es.close()
    return nc


def emit_l1attn(nc, S, es, io, bar, pfx="a_"):
    i_xn = io.get("xn1fm")
    i_wq = io["wq"]; i_wk = io["wk"]; i_wv = io["wv"]
    i_bq = io["bqc"]; i_bk = io["bkc"]; i_bv = io["bvrow"]
    i_fwq = io["fwq"]; i_fwk = io["fwk"]; i_fwv = io["fwv"]
    i_fbq = io["fbqc"]; i_fbk = io["fbkc"]; i_fbv = io["fbvrow"]
    i_caus = io["causal"]; i_ones = io["ones128"]; i_onesc = io["onescol"]; i_idf = io["identf"]
    if "zslice" in io:
        zsl = io["zslice"]
        o_z1 = lambda h, tc: zsl(0, h, tc)
        o_z2 = lambda h, tc: zsl(1, h, tc)
    else:
        _z1 = io["z1"]; _z2 = io["z2"]
        o_z1 = lambda h, tc: _z1(h)[:, tc * 512:(tc + 1) * 512]
        o_z2 = lambda h, tc: _z2(h)[:, tc * 512:(tc + 1) * 512]
    sb = lambda nme, s, d=F32: es.enter_context(nc.sbuf_tensor(pfx + nme, s, d))
    psum = lambda nme, s, d=F32: es.enter_context(nc.psum_tensor(pfx + nme, s, d))
    outs = []
    prev = {}

    xn_sb = sb("xn_sb", [128, 8 * T], BF16)
    s_rep = sb("s_rep", [128, T], BF16)
    s_tok = sb("s_tok", [128, 32])
    s_row = sb("s_row", [1, T])
    wsbs = {}
    for nm in ("wq", "wk", "wv"):
        wsbs[nm] = sb(nm + "_sb", [128, 8 * 512], BF16)
    wsbs["fwq"] = wsbs["wq"]; wsbs["fwk"] = wsbs["wk"]; wsbs["fwv"] = wsbs["wv"]
    bsbs = {}
    for nm, src_, shp in (("bqc", i_bq, [128, 4]), ("bkc", i_bk, [128, 4]), ("bvrow", i_bv, [1, 512]),
                          ("fbqc", i_fbq, [128, 4]), ("fbkc", i_fbk, [128, 4]), ("fbvrow", i_fbv, [1, 512])):
        bsbs[nm] = sb(nm + "_sb", shp)
    caus_sb = sb("caus_sb", [128, 4 * 512], BF16)
    ones_sb = sb("ones_sb", [1, 128])
    onesc_sb = sb("onesc_sb", [128, 1])
    idf_sb = sb("idf_sb", [128, 128])
    q_h = sb("q_h", [64, T], BF16)
    k_h = sb("k_h", [64, T], BF16)
    v_h = sb("v_h", [128, 32 * 64], BF16)
    vaug = sb("vaug", [128, 32 * 65], BF16)
    vp = [sb(f"vp{i}", [128, 64], BF16) for i in range(2)]
    brep = sb("brep", [128, 64])
    stage = sb("stage", [128, T])
    P_sb = [sb(f"P_sb{i}", [128, T], BF16) for i in range(2)]
    Pq = [sb(f"Pq{i}", [128, 512], BF16) for i in range(2)]
    qe1 = sb("qe1", [64, 512])
    mcol = sb("mcol", [128, 1]); tmpc = sb("tmpc", [128, 1]); negm = sb("negm", [128, 1])
    Dcol = sb("Dcol", [128, 1]); rD = sb("rD", [128, 1])
    z_sb = sb("z_sb", [65, T])
    z2_sb = z_sb
    zos = [sb("zos0", [64, 512], BF16)]
    ps_s = [psum(f"ps_s{i}", [128, 512]) for i in range(2)]
    ps_z = [psum(f"ps_z{i}", [128, 512]) for i in range(2)]
    ps_q = [psum(f"ps_q{i}", [128, 512]) for i in range(2)]
    ps_r = psum("ps_r", [128, 512])

    lds = {}
    for nm, src_ in (("wq", i_wq), ("wk", i_wk), ("wv", i_wv)):
        lds[nm] = S.dma("gpsimd", wsbs[nm][:].rearrange("p (hc c) -> p hc c", c=512),
                        src_.rearrange("(hc p) c -> p hc c", p=128), deps=bar)
    fw_srcs = {"fwq": i_fwq, "fwk": i_fwk, "fwv": i_fwv}
    for nm, src_ in (("bqc", i_bq), ("bkc", i_bk), ("bvrow", i_bv), ("fbqc", i_fbq), ("fbkc", i_fbk),
                     ("fbvrow", i_fbv)):
        lds[nm] = S.dma("sync", bsbs[nm][:], src_, deps=bar)
    lds["ones"] = S.dma("sync", ones_sb[:], i_ones, deps=bar)
    lds["onesc"] = S.dma("sync", onesc_sb[:], i_onesc, deps=bar)
    if "xn1fm_parts" in io:
        parts = io["xn1fm_parts"]
        ld_xns = []
        for ap_, col0 in parts:
            w_ = TOWN
            ld_xns.append(S.dma("sync",
                xn_sb[:].rearrange("p (c t) -> p c t", t=T)[:, :, col0:col0 + w_],
                ap_.rearrange("(c p) t -> p c t", p=128), deps=bar + ld_xns[-1:]))
        ld_xn = ld_xns[-1]
    else:
        ld_xn = S.dma("sync", xn_sb[:].rearrange("p (c t) -> p c t", t=T),
                      i_xn.rearrange("(c p) t -> p c t", p=128), deps=bar)
    ld_caus = [S.dma("sync", caus_sb[:, v * 512:(v + 1) * 512], i_caus[v], deps=bar) for v in range(4)]

    srep = []
    for tc in range(8):
        mml = None
        for hc in range(8):
            sq = S.add("scalar", lambda s, hc=hc, tc=tc: s.activation(
                out=stage[:, 0:512], in_=xn_sb[:, hc * T + tc * 512: hc * T + (tc + 1) * 512], func=AF.Square),
                deps=[ld_xn, prev.get('sq')])
            mml = S.add("tensor", lambda t, hc=hc: t.matmul(
                out=ps_r[0:1, 0:512], lhsT=onesc_sb[:], rhs=stage[:, 0:512], start=(hc == 0), stop=(hc == 7)),
                deps=[sq, lds["onesc"]] + ([prev.get('psr')] if hc == 0 else []))
            prev['sq'] = mml
        ev = S.add("vector", lambda v, tc=tc: v.tensor_copy(
            out=s_row[0:1, tc * 512:(tc + 1) * 512], in_=ps_r[0:1, 0:512]),
            deps=[mml, prev.get('srow')])
        prev['psr'] = ev; prev['srow'] = ev
        srep.append(ev)
    r1 = S.add("vector", lambda v: v.tensor_scalar(out=s_row[:], in0=s_row[:], scalar1=1.0 / 1024,
               scalar2=RMS_EPS, op0=ALU.mult, op1=ALU.add), deps=srep)
    r2 = S.add("scalar", lambda s: s.activation(out=s_row[:], in_=s_row[:], func=AF.Sqrt), deps=[r1])
    r3 = S.add("vector", lambda v: v.reciprocal(out=s_row[:], in_=s_row[:]), deps=[r2])
    rr = []
    for tc in range(8):
        mm = S.add("tensor", lambda t, tc=tc: t.matmul(
            out=ps_r[:, 0:512], lhsT=ones_sb[:], rhs=s_row[0:1, tc * 512:(tc + 1) * 512],
            start=True, stop=True), deps=[r3, lds["ones"], prev.get('psr')])
        ev = S.add("vector", lambda v, tc=tc: v.tensor_copy(out=s_rep[:, tc * 512:(tc + 1) * 512],
                   in_=ps_r[:, 0:512]), deps=[mm, prev.get('srep2')])
        prev['psr'] = ev; prev['srep2'] = ev
        rr.append(ev)
    mkid = S.dma("sync", idf_sb[:], i_idf, deps=bar)
    for _qt in range(8):
        prev[('stg', _qt)] = prev['sq']
    srep32 = sb("srep32", [128, 128])
    stok_deps = []
    for tt in range(32):
        cp32 = S.add("vector", lambda v, tt=tt: v.tensor_copy(out=srep32[:], in_=s_rep[:, tt * 128:(tt + 1) * 128]),
                     deps=[rr[tt // 4], prev.get('srep32')])
        tr = S.add("tensor", lambda t, tt=tt: t.transpose(
            out=ps_r[:, 0:128], in_=srep32[:], identity=idf_sb[:]),
            deps=[cp32, mkid, prev.get('psr')])
        ev = S.add("vector", lambda v, tt=tt: v.tensor_copy(out=s_tok[:, tt:tt + 1], in_=ps_r[:, 0:1]),
                   deps=[tr, prev.get('stok')])
        prev['srep32'] = tr
        prev['psr'] = ev; prev['stok'] = ev
        stok_deps.append(ev)

    BIGM = float(2 ** 20)

    def proj_fm(dst, wname, bname, hl, use_scale, name):
        dd = []
        W = wsbs[wname]
        bcol = bsbs[bname][(hl % 2) * 64:(hl % 2) * 64 + 64, hl // 2: hl // 2 + 1]
        for tc in range(8):
            pb = tc % 2
            mms = None
            for hc in range(8):
                mms = S.add("tensor", lambda t, hc=hc, tc=tc, pb=pb, hl=hl, W=W: t.matmul(
                    out=ps_q[pb][0:64, 0:512],
                    lhsT=W[:, hc * 512 + hl * 64: hc * 512 + (hl + 1) * 64],
                    rhs=xn_sb[:, hc * T + tc * 512: hc * T + (tc + 1) * 512],
                    start=(hc == 0), stop=(hc == 7)),
                    deps=[lds[wname], ld_xn] + ([prev.get(('psq', pb))] if hc == 0 else []))
            if use_scale:
                e1 = S.add("vector", lambda v, tc=tc, pb=pb: v.tensor_tensor(
                    out=qe1[:], in0=ps_q[pb][0:64, 0:512], in1=s_rep[0:64, tc * 512:(tc + 1) * 512],
                    op=ALU.mult), deps=[mms, rr[tc], prev.get('qe1')])
                e2 = S.add("vector", lambda v, tc=tc, dst=dst, bcol=bcol: v.tensor_scalar(
                    out=dst[:, tc * 512:(tc + 1) * 512], in0=qe1[:], scalar1=bcol, scalar2=None,
                    op0=ALU.add), deps=[e1, lds[bname], prev.get((name, tc))])
                prev['qe1'] = e2
                prev[('psq', pb)] = e1
            else:
                e2 = S.add("vector", lambda v, tc=tc, dst=dst, pb=pb, bcol=bcol: v.tensor_scalar(
                    out=dst[:, tc * 512:(tc + 1) * 512], in0=ps_q[pb][0:64, 0:512], scalar1=bcol,
                    scalar2=None, op0=ALU.add), deps=[mms, lds[bname], prev.get((name, tc))])
                prev[('psq', pb)] = e2
            prev[(name, tc)] = e2
            dd.append(e2)
        return dd

    def proj_tok(wname, bname, hl, use_scale, name):
        mmb = S.add("tensor", lambda t, hl=hl: t.matmul(
            out=ps_r[:, 0:64], lhsT=ones_sb[:], rhs=bsbs[bname][0:1, hl * 64:(hl + 1) * 64],
            start=True, stop=True), deps=[lds[bname], lds["ones"], prev.get('psr')])
        evb = S.add("vector", lambda v: v.tensor_copy(out=brep[:], in_=ps_r[:, 0:64]),
                    deps=[mmb, prev.get('brep')])
        prev['psr'] = evb; prev['brep'] = evb
        dd = []
        W = wsbs[wname]
        for tt in range(32):
            pb = tt % 2
            mms = None
            for hc in range(8):
                mms = S.add("tensor", lambda t, hc=hc, tt=tt, pb=pb, hl=hl, W=W: t.matmul(
                    out=ps_q[pb][:, 0:64],
                    lhsT=xn_sb[:, hc * T + tt * 128: hc * T + (tt + 1) * 128],
                    rhs=W[:, hc * 512 + hl * 64: hc * 512 + (hl + 1) * 64],
                    start=(hc == 0), stop=(hc == 7)),
                    deps=[lds[wname], ld_xn] + ([prev.get(('psq', pb))] if hc == 0 else []))
            sc = s_tok[:, tt:tt + 1] if use_scale else 1.0
            e2 = S.add("vector", lambda v, tt=tt, pb=pb, sc=sc: v.scalar_tensor_tensor(
                out=v_h[:, tt * 64:(tt + 1) * 64], in0=ps_q[pb][:, 0:64],
                scalar=sc, in1=brep[:], op0=ALU.mult, op1=ALU.add),
                deps=[mms, evb, prev.get((name, tt % 4))] + (stok_deps if use_scale else []))
            prev[('psq', pb)] = e2
            prev[(name, tt % 4)] = e2
            dd.append(e2)
        return dd

    for hl in range(8):
        qd = proj_fm(q_h, "wq", "bqc", hl, True, 'q')
        kd = proj_fm(k_h, "wk", "bkc", hl, True, 'k')
        vd = proj_tok("wv", "bvrow", hl, True, 'v')
        mz = S.add("vector", lambda v: v.memset(z_sb[:], 0.0), deps=prev.get('zuse', []) + bar)
        zdeps = []
        for kbp in range(16):
            qt0 = (2 * kbp) // 4
            c0 = qt0 * 512
            for kpar in range(2):
                kb = 2 * kbp + kpar
                scd = []
                for qt in range(qt0, 8):
                    pb = qt % 2
                    mm = S.add("tensor", lambda t, kb=kb, qt=qt, pb=pb: t.matmul(
                        out=ps_s[pb][:], lhsT=k_h[:, kb * 128:(kb + 1) * 128],
                        rhs=q_h[:, qt * 512:(qt + 1) * 512], start=True, stop=True),
                        deps=[kd[kb // 4], qd[qt], prev.get(('pss', pb))])
                    if qt == qt0:
                        vv = kb % 4
                        d1 = S.add("vector", lambda v, qt=qt, pb=pb, vv=vv: v.scalar_tensor_tensor(
                            out=stage[:, qt * 512:(qt + 1) * 512], in0=ps_s[pb][:], scalar=BIGM,
                            in1=caus_sb[:, vv * 512:(vv + 1) * 512], op0=ALU.add, op1=ALU.mult),
                            deps=[mm, ld_caus[vv], prev.get(('stg', qt))])
                        ev = S.add("vector", lambda v, qt=qt: v.tensor_scalar(
                            out=stage[:, qt * 512:(qt + 1) * 512], in0=stage[:, qt * 512:(qt + 1) * 512],
                            scalar1=-BIGM, scalar2=None, op0=ALU.add), deps=[d1])
                    else:
                        ev = S.add("scalar", lambda s, qt=qt, pb=pb: s.activation(
                            out=stage[:, qt * 512:(qt + 1) * 512], in_=ps_s[pb][:], func=AF.Copy),
                            deps=[mm, prev.get(('stg', qt))])
                    prev[('pss', pb)] = ev
                    prev[('stg', qt)] = ev
                    scd.append(ev)
                mrow = S.add("vector", lambda v, c0=c0: v.reduce_max(out=negm[:], in_=stage[:, c0:T], axis=AX.X),
                             deps=scd + [prev.get('negm')])
                ng = S.add("vector", lambda v: v.tensor_scalar(out=negm[:], in0=negm[:], scalar1=-1.0,
                           scalar2=None, op0=ALU.mult), deps=[mrow])
                ex = S.add("scalar", lambda s, c0=c0, kpar=kpar: s.activation(
                    out=P_sb[kpar][:, c0:T], in_=stage[:, c0:T], func=AF.Exp, bias=negm[:], accum_out=Dcol[:]),
                    deps=[ng, prev.get('dcol'), prev.get(('puse', kpar))] + scd)
                for qt in range(qt0, 8):
                    prev[('stg', qt)] = ex
                rd = S.add("vector", lambda v: v.reciprocal(out=rD[:], in_=Dcol[:]), deps=[ex, prev.get('rd')])
                vpo = S.add("vector", lambda v, kb=kb, kpar=kpar: v.tensor_scalar(
                    out=vp[kpar][:], in0=v_h[:, kb * 64:(kb + 1) * 64], scalar1=rD[:], scalar2=None, op0=ALU.mult),
                    deps=[rd, vd[kb // 8], prev.get(('vp', kpar))])
                prev['dcol'] = rd; prev['rd'] = vpo; prev['negm'] = ex
                prev[('ex', kpar)] = ex; prev[('vpo', kpar)] = vpo
            lastav = None
            for qt in range(qt0, 8):
                pb = qt % 2
                for kpar in range(2):
                    lastav = S.add("tensor", lambda t, qt=qt, pb=pb, kpar=kpar: t.matmul(
                        out=ps_z[pb][0:64, 0:512], lhsT=vp[kpar][:], rhs=P_sb[kpar][:, qt * 512:(qt + 1) * 512],
                        start=(kpar == 0), stop=(kpar == 1)),
                        deps=[prev[('vpo', kpar)], prev[('ex', kpar)]] +
                             ([prev.get(('psz', pb))] if kpar == 0 else []))
                za = S.add("vector", lambda v, qt=qt, pb=pb: v.tensor_tensor(
                    out=z_sb[0:64, qt * 512:(qt + 1) * 512], in0=z_sb[0:64, qt * 512:(qt + 1) * 512],
                    in1=ps_z[pb][0:64, 0:512], op=ALU.add),
                    deps=[lastav, mz, prev.get(('za', qt))])
                prev[('psz', pb)] = za
                prev[('za', qt)] = za
                zdeps.append(za)
            prev[('puse', 0)] = lastav; prev[('puse', 1)] = lastav
            prev[('vp', 0)] = lastav; prev[('vp', 1)] = lastav
        zcs = []
        for tc in range(8):
            zb = 0
            zc = S.add("vector", lambda v, tc=tc, zb=zb: v.tensor_copy(
                out=zos[zb][:], in_=z_sb[0:64, tc * 512:(tc + 1) * 512]),
                deps=zdeps[-8:] + [prev.get(('zob', zb))])
            sto = S.dma("sync", o_z1(hl, tc), zos[zb][:], deps=[zc])
            outs.append(sto)
            prev[('zob', zb)] = sto
            zcs.append(zc)
        prev['zuse'] = zcs
        prev['q'] = qd[-1]

    wdone = [prev['q'], prev['zuse'][0]]
    for nm in ("fwq", "fwk", "fwv"):
        lds[nm] = S.dma("gpsimd", wsbs[nm][:].rearrange("p (hc c) -> p hc c", c=512),
                        fw_srcs[nm].rearrange("(hc p) c -> p hc c", p=128), deps=wdone)
    vaug_init = S.add("vector", lambda v: v.memset(vaug[:], 1.0), deps=bar)
    for hl in range(8):
        qd = proj_fm(q_h, "fwq", "fbqc", hl, False, 'fq')
        kd = proj_fm(k_h, "fwk", "fbkc", hl, False, 'fk')
        vd = proj_tok("fwv", "fbvrow", hl, False, 'fv')
        cpv = S.add("vector", lambda v: v.tensor_copy(
            out=vaug[:].rearrange("p (kb c) -> p kb c", c=65)[:, :, 0:64],
            in_=v_h[:].rearrange("p (kb c) -> p kb c", c=64)),
            deps=vd + [vaug_init] + prev.get('vauguse', []))
        mz = S.add("vector", lambda v: v.memset(z2_sb[:], 0.0),
                   deps=prev.get('z2use', []) + prev.get('zuse', []) + bar)
        zdeps = []
        for kb in range(32):
            for qt in range(8):
                pb = qt % 2
                mm = S.add("tensor", lambda t, kb=kb, qt=qt, pb=pb: t.matmul(
                    out=ps_s[pb][:], lhsT=k_h[:, kb * 128:(kb + 1) * 128],
                    rhs=q_h[:, qt * 512:(qt + 1) * 512], start=True, stop=True),
                    deps=[kd[kb // 4], qd[qt], prev.get(('pss', pb))])
                ex = S.add("scalar", lambda s, pb=pb: s.activation(
                    out=Pq[pb][:], in_=ps_s[pb][:], func=AF.Exp, scale=0.125),
                    deps=[mm, prev.get(('pq', pb))])
                prev[('pss', pb)] = ex
                av = S.add("tensor", lambda t, kb=kb, pb=pb: t.matmul(
                    out=ps_z[pb][0:65, 0:512], lhsT=vaug[:, kb * 65:(kb + 1) * 65], rhs=Pq[pb][:],
                    start=True, stop=True),
                    deps=[ex, cpv, prev.get(('psz', pb))])
                za = S.add("vector", lambda v, qt=qt, pb=pb: v.tensor_tensor(
                    out=z2_sb[:, qt * 512:(qt + 1) * 512], in0=z2_sb[:, qt * 512:(qt + 1) * 512],
                    in1=ps_z[pb][0:65, 0:512], op=ALU.add),
                    deps=[av, mz, prev.get(('za2', qt))])
                prev[('pq', pb)] = av
                prev[('psz', pb)] = za
                prev[('za2', qt)] = za
                zdeps.append(za)
        gr = S.add("vector", lambda v: v.reciprocal(out=s_row[:], in_=z2_sb[64:65, :]),
                   deps=zdeps[-8:] + [prev.get('grow')])
        zcs = []
        for tc in range(8):
            zb = 0
            mm = S.add("tensor", lambda t, tc=tc: t.matmul(
                out=ps_r[0:64, 0:512], lhsT=ones_sb[0:1, 0:64], rhs=s_row[0:1, tc * 512:(tc + 1) * 512],
                start=True, stop=True), deps=[gr, lds["ones"], prev.get('psr')])
            zo2 = S.add("vector", lambda v, tc=tc, zb=zb: v.tensor_tensor(
                out=zos[zb][:], in0=ps_r[0:64, 0:512],
                in1=z2_sb[0:64, tc * 512:(tc + 1) * 512], op=ALU.mult),
                deps=[mm, prev.get(('zob', zb))])
            prev['psr'] = zo2
            sto = S.dma("sync", o_z2(hl, tc), zos[zb][:], deps=[zo2])
            outs.append(sto)
            prev[('zob', zb)] = sto
            zcs.append(zo2)
        prev['z2use'] = zcs
        prev['vauguse'] = [prev[('pq', 0)], prev[('pq', 1)]]
        prev['grow'] = zcs[-1]

    return outs



def build_l1fin():
    nc = bass.Bass("TRN2", target_bir_lowering=False, debug=False)
    io = {}
    for nm, shp, dt in (("x1", [TOWN, D], F32), ("xn1fm", [1024, TOWN], BF16), ("z1fm", [1024, TOWN], BF16),
                        ("z2fm", [1024, TOWN], BF16), ("woT", [1024, 1024], F32), ("bo", [1, 1024], F32),
                        ("wgate", [1024, 8], F32), ("bgate", [1, 8], F32), ("wexp", [8, 1024, 1024], F32),
                        ("bexp", [1, 8 * 1024], F32), ("identb", [128, 128], BF16), ("ones128", [1, 128], F32)):
        io[nm] = nc.dram_tensor(nm, shp, dt, kind="ExternalInput").ap()
    io["out"] = nc.dram_tensor("out", [TOWN, D], F32, kind="ExternalOutput").ap()
    es = contextlib.ExitStack()
    S = Sched()
    outs = emit_l1fin(nc, S, es, io, [])
    S.emit(nc, es, final_waits=outs)
    es.close()
    return nc


def emit_l1fin(nc, S, es, io, bar, pfx="n_"):
    i_x1 = io["x1"]; i_xn = io["xn1fm"]
    if "z1get" in io:
        z1get = io["z1get"]; z2get = io["z2get"]
    else:
        i_z1 = io["z1fm"]; i_z2 = io["z2fm"]
        z1get = lambda c: i_z1[c * 128:(c + 1) * 128]
        z2get = lambda c: i_z2[c * 128:(c + 1) * 128]
    dyn = "goff" in io
    i_wo = io["woT"]; i_bo = io["bo"]; i_wg = io["wgate"]; i_bg = io["bgate"]
    i_we = io["wexp"]; i_be = io["bexp"]; i_idb = io["identb"]; i_ones = io["ones128"]
    o_out = io["out"]
    sb = lambda nme, s, d=F32: es.enter_context(nc.sbuf_tensor(pfx + nme, s, d))
    psum = lambda nme, s, d=F32: es.enter_context(nc.psum_tensor(pfx + nme, s, d))
    outs = []
    prev = {}

    xn_sb = sb("xn_sb", [128, 8 * TOWN], BF16)
    macc = sb("macc", [128, 16 * 1024])
    wo_sb = sb("wo_sb", [128, 8 * 1024], BF16)
    we_sb = [sb("we_sb0", [128, 8 * 1024], BF16)]
    bo_sb = sb("bo_sb", [1, 1024], BF16)
    be_sb = sb("be_sb", [1, 1024], BF16)
    bg_sb = sb("bg_sb", [1, 8], BF16)
    wg_sb = sb("wg_sb", [128, 8 * 8], BF16)
    onesb = sb("onesb", [1, 128], BF16)
    ones_sb = sb("ones_sb", [1, 128])
    id_sb = sb("id_sb", [128, 128], BF16)
    if dyn:
        z1res = sb("z1res", [128, 8 * TOWN], BF16)
        z2res = sb("z2res", [128, 8 * TOWN], BF16)
    else:
        z1t = [sb(f"z1t{i}", [128, 8 * 128], BF16) for i in range(2)]
        z2t = [sb(f"z2t{i}", [128, 8 * 128], BF16) for i in range(2)]
    x1t = [sb(f"x1t{i}", [128, 1024]) for i in range(2)]
    lgt = sb("lgt", [128, 8]); exl = sb("exl", [128, 8])
    m1c = sb("m1c", [128, 1]); ssc = sb("ssc", [128, 1]); rsc = sb("rsc", [128, 1])
    io8 = sb("io8", [128, 8]); cnd = sb("cnd", [128, 8])
    i1c = sb("i1c", [128, 1]); i2c = sb("i2c", [128, 1])
    oh1 = sb("oh1", [128, 8]); oh2 = sb("oh2", [128, 8]); lg2 = sb("lg2", [128, 8])
    cw_all = sb("cw_all", [128, 16 * 8])
    ps_a = [psum(f"ps_a{i}", [128, 512]) for i in range(2)]
    ps_t = [psum(f"ps_t{i}", [128, 1024], BF16) for i in range(2)]
    ps_g = psum("ps_g", [128, 512])
    ps_y = [psum(f"ps_y{i}", [128, 512]) for i in range(2)]

    ld_xn = S.dma("sync", xn_sb[:].rearrange("p (c t) -> p c t", t=TOWN),
                  i_xn.rearrange("(c p) t -> p c t", p=128), deps=bar)
    ld_wo = S.dma("gpsimd", wo_sb[:].rearrange("p (c d) -> p c d", d=1024),
                  i_wo.rearrange("(c p) d -> p c d", p=128), deps=bar)
    ld_bo = S.dma("gpsimd", bo_sb[:], i_bo, deps=bar)

    ld_bg = S.dma("gpsimd", bg_sb[:], i_bg, deps=bar)
    ld_wg = S.dma("gpsimd", wg_sb[:].rearrange("p (c e) -> p c e", e=8),
                  i_wg.rearrange("(c p) e -> p c e", p=128), deps=bar)
    ld_ones = S.dma("sync", ones_sb[:], i_ones, deps=bar)
    ld_id = S.dma("sync", id_sb[:], i_idb, deps=bar)
    cob = S.add("vector", lambda v: v.tensor_copy(out=onesb[:], in_=ones_sb[:]), deps=[ld_ones])
    iog = S.add("gpsimd", lambda g: g.iota(out=io8[:], pattern=[[1, 8]], base=0, channel_multiplier=0,
                                           allow_small_or_imprecise_dtypes=True), deps=bar)

    if dyn:
        goff = io["goff"]
        zldep = []
        for c in range(8):
            zldep.append(S.dma("sync", z1res[:, c * TOWN:(c + 1) * TOWN],
                               lambda c=c: z1get(c)[:, bass.ds(goff[0], TOWN)],
                               deps=bar + [io.get("goff_dep")] + zldep[-1:]))
            zldep.append(S.dma("sync", z2res[:, c * TOWN:(c + 1) * TOWN],
                               lambda c=c: z2get(c)[:, bass.ds(goff[0], TOWN)],
                               deps=bar + [io.get("goff_dep")] + zldep[-1:]))

    for i in range(NT):
        b = i % 2
        if dyn:
            lz1 = zldep[-1]
            lz2 = zldep[-1]
            z1sl = lambda c, i=i: z1res[:, c * TOWN + i * 128: c * TOWN + (i + 1) * 128]
            z2sl = lambda c, i=i: z2res[:, c * TOWN + i * 128: c * TOWN + (i + 1) * 128]
        else:
            lz1 = [S.dma("sync", z1t[b][:, c * 128:(c + 1) * 128],
                         z1get(c)[:, i * 128:(i + 1) * 128],
                         deps=prev.get(('z1', b), []) + bar) for c in range(8)][-1]
            lz2 = [S.dma("sync", z2t[b][:, c * 128:(c + 1) * 128],
                         z2get(c)[:, i * 128:(i + 1) * 128],
                         deps=prev.get(('z2', b), []) + bar) for c in range(8)][-1]
            z1sl = lambda c, b=b: z1t[b][:, c * 128:(c + 1) * 128]
            z2sl = lambda c, b=b: z2t[b][:, c * 128:(c + 1) * 128]
        lx1 = S.dma("sync", x1t[b][:], i_x1[i * 128:(i + 1) * 128, :], deps=prev.get(('x1', b), []) + bar)
        ztr = []
        for c in range(8):
            cb = c % 2
            tr = S.add("tensor", lambda t, c=c, cb=cb, z2sl=z2sl: t.transpose(
                out=ps_t[cb][:, 0:128], in_=z2sl(c), identity=id_sb[:]),
                deps=[lz2, ld_id] + prev.get(('pst', cb), []))
            ad = S.add("vector", lambda v, b=b, c=c, cb=cb: v.tensor_tensor(
                out=x1t[b][:, c * 128:(c + 1) * 128], in0=x1t[b][:, c * 128:(c + 1) * 128],
                in1=ps_t[cb][:, 0:128], op=ALU.add), deps=[tr, lx1, prev.get(('x1a', b))])
            prev[('pst', cb)] = [ad]
            prev[('x1a', b)] = ad
            ztr.append(ad)
        for dc in range(2):
            pb = dc
            mms = None
            for zc in range(8):
                mms = S.add("tensor", lambda t, zc=zc, dc=dc, pb=pb, z1sl=z1sl: t.matmul(
                    out=ps_a[pb][:], lhsT=z1sl(zc),
                    rhs=wo_sb[:, zc * 1024 + dc * 512: zc * 1024 + (dc + 1) * 512],
                    start=(zc == 0), stop=False),
                    deps=[lz1, ld_wo] + ([prev.get(('psa', pb))] if zc == 0 else []))
            mb = S.add("tensor", lambda t, dc=dc, pb=pb: t.matmul(
                out=ps_a[pb][:], lhsT=onesb[:], rhs=bo_sb[:, dc * 512:(dc + 1) * 512],
                start=False, stop=True), deps=[ld_bo, cob])
            ad = S.add("vector", lambda v, b=b, dc=dc, pb=pb, i=i: v.scalar_tensor_tensor(
                out=macc[:, i * 1024 + dc * 512: i * 1024 + (dc + 1) * 512], in0=ps_a[pb][:],
                scalar=1.0, in1=x1t[b][:, dc * 512:(dc + 1) * 512], op0=ALU.mult, op1=ALU.add),
                deps=[mb, prev[('x1a', b)], ztr[-1]] + prev.get(('macc', i), []))
            prev[('psa', pb)] = ad
            prev[('macc', i)] = [ad]
        prev[('z1', b)] = [prev[('psa', 0)], prev[('psa', 1)]]
        prev[('z2', b)] = ztr
        prev[('x1', b)] = [prev[('psa', 0)], prev[('psa', 1)]]
        mmg = None
        for hc in range(8):
            mmg = S.add("tensor", lambda t, hc=hc, i=i: t.matmul(
                out=ps_g[:, 0:8], lhsT=xn_sb[:, hc * TOWN + i * 128: hc * TOWN + (i + 1) * 128],
                rhs=wg_sb[:, hc * 8:(hc + 1) * 8], start=(hc == 0), stop=False),
                deps=[ld_xn, ld_wg] + ([prev.get('psg')] if hc == 0 else []))
        mgb = S.add("tensor", lambda t: t.matmul(out=ps_g[:, 0:8], lhsT=onesb[:], rhs=bg_sb[:],
                    start=False, stop=True), deps=[ld_bg, cob])
        e0 = S.add("vector", lambda v: v.tensor_copy(out=lgt[:], in_=ps_g[:, 0:8]), deps=[mgb, prev.get('lgt')])
        prev['psg'] = e0
        e1 = S.add("vector", lambda v: v.reduce_max(out=m1c[:], in_=lgt[:], axis=AX.X), deps=[e0, prev.get('m1c')])
        e1b = S.add("vector", lambda v: v.tensor_scalar(out=m1c[:], in0=m1c[:], scalar1=-1.0, scalar2=None, op0=ALU.mult), deps=[e1])
        e2 = S.add("scalar", lambda s: s.activation(out=exl[:], in_=lgt[:], func=AF.Exp, bias=m1c[:],
                   accum_out=ssc[:]), deps=[e1b, prev.get('exl')])
        e3 = S.add("vector", lambda v: v.reciprocal(out=rsc[:], in_=ssc[:]), deps=[e2, prev.get('rsc')])
        e4 = S.add("vector", lambda v: v.tensor_scalar(out=exl[:], in0=exl[:], scalar1=rsc[:], scalar2=None, op0=ALU.mult), deps=[e3])
        f1 = S.add("vector", lambda v: v.reduce_max(out=i1c[:], in_=exl[:], axis=AX.X), deps=[e4, prev.get('i1c')])
        f2 = S.add("vector", lambda v: v.tensor_scalar(out=cnd[:], in0=exl[:], scalar1=i1c[:], scalar2=None, op0=ALU.is_ge), deps=[f1, prev.get('cnd')])
        f3 = S.add("vector", lambda v: v.scalar_tensor_tensor(out=cnd[:], in0=cnd[:], scalar=-BIGC, in1=io8[:], op0=ALU.mult, op1=ALU.add), deps=[f2, iog])
        f4 = S.add("vector", lambda v: v.tensor_reduce(out=i1c[:], in_=cnd[:], axis=AX.X, op=ALU.min), deps=[f3])
        f4b = S.add("vector", lambda v: v.tensor_scalar(out=i1c[:], in0=i1c[:], scalar1=BIGC, scalar2=None, op0=ALU.add), deps=[f4])
        f5 = S.add("vector", lambda v: v.tensor_scalar(out=oh1[:], in0=io8[:], scalar1=i1c[:], scalar2=None, op0=ALU.is_equal), deps=[f4b, prev.get('oh1')])
        f6 = S.add("vector", lambda v: v.scalar_tensor_tensor(out=lg2[:], in0=oh1[:], scalar=-BIGC, in1=exl[:], op0=ALU.mult, op1=ALU.add), deps=[f5, prev.get('lg2')])
        f7 = S.add("vector", lambda v: v.reduce_max(out=i2c[:], in_=lg2[:], axis=AX.X), deps=[f6, prev.get('i2c')])
        f8 = S.add("vector", lambda v: v.tensor_scalar(out=cnd[:], in0=lg2[:], scalar1=i2c[:], scalar2=None, op0=ALU.is_ge), deps=[f7, f3])
        f9 = S.add("vector", lambda v: v.scalar_tensor_tensor(out=cnd[:], in0=cnd[:], scalar=-BIGC, in1=io8[:], op0=ALU.mult, op1=ALU.add), deps=[f8])
        f10 = S.add("vector", lambda v: v.tensor_reduce(out=i2c[:], in_=cnd[:], axis=AX.X, op=ALU.min), deps=[f9, f7])
        f10b = S.add("vector", lambda v: v.tensor_scalar(out=i2c[:], in0=i2c[:], scalar1=BIGC, scalar2=None, op0=ALU.add), deps=[f10])
        f11 = S.add("vector", lambda v: v.tensor_scalar(out=oh2[:], in0=io8[:], scalar1=i2c[:], scalar2=None, op0=ALU.is_equal), deps=[f10b, prev.get('oh2')])
        f12 = S.add("vector", lambda v: v.tensor_tensor(out=oh1[:], in0=oh1[:], in1=oh2[:], op=ALU.add), deps=[f11, f6])
        f13 = S.add("vector", lambda v, i=i: v.tensor_tensor(out=cw_all[:, i * 8:(i + 1) * 8], in0=exl[:], in1=oh1[:], op=ALU.mult), deps=[f12, e4, prev.get(('cw', i))])
        prev['lgt'] = f13; prev['m1c'] = f13; prev['exl'] = f13; prev['rsc'] = e4
        prev['i1c'] = f5; prev['cnd'] = f10; prev['oh1'] = f13; prev['lg2'] = f8
        prev['i2c'] = f11; prev['oh2'] = f12
        prev[('cw', i)] = [f13]
        prev[('cwd', i)] = f13

    for e in range(8):
        eb = 0
        lwe = S.dma("gpsimd", we_sb[eb][:].rearrange("p (c d) -> p c d", d=1024),
                    i_we[e].rearrange("(c p) d -> p c d", p=128), deps=prev.get(('we', eb), []) + bar)
        ld_be = S.dma("gpsimd", be_sb[:], i_be[:, e * 1024:(e + 1) * 1024],
                      deps=prev.get('beuse', []) + bar)
        last_mm = {}
        for i in range(NT):
            for dc in range(2):
                pb = dc
                mms = None
                for hc in range(8):
                    mms = S.add("tensor", lambda t, hc=hc, i=i, dc=dc, pb=pb, eb=eb: t.matmul(
                        out=ps_y[pb][:], lhsT=xn_sb[:, hc * TOWN + i * 128: hc * TOWN + (i + 1) * 128],
                        rhs=we_sb[eb][:, hc * 1024 + dc * 512: hc * 1024 + (dc + 1) * 512],
                        start=(hc == 0), stop=False),
                        deps=[ld_xn, lwe] + ([prev.get(('psy', pb))] if hc == 0 else []))
                mb = S.add("tensor", lambda t, dc=dc, pb=pb: t.matmul(
                    out=ps_y[pb][:], lhsT=onesb[:], rhs=be_sb[:, dc * 512:(dc + 1) * 512],
                    start=False, stop=True), deps=[ld_be, cob])
                ev = S.add("vector", lambda v, i=i, dc=dc, pb=pb, e=e: v.scalar_tensor_tensor(
                    out=macc[:, i * 1024 + dc * 512: i * 1024 + (dc + 1) * 512], in0=ps_y[pb][:],
                    scalar=cw_all[:, i * 8 + e: i * 8 + e + 1],
                    in1=macc[:, i * 1024 + dc * 512: i * 1024 + (dc + 1) * 512],
                    op0=ALU.mult, op1=ALU.add),
                    deps=[mb, prev[('cwd', i)], prev.get(('psy', pb))] + prev.get(('macc', i), []))
                prev[('psy', pb)] = ev
                prev[('macc', i)] = [ev]
                last_mm[i] = mms
        prev[('we', eb)] = [last_mm[NT - 1]]
        prev['beuse'] = [last_mm[NT - 1]]
    for i in range(NT):
        sto = S.dma("sync", o_out[i * 128:(i + 1) * 128, :], macc[:, i * 1024:(i + 1) * 1024],
                    deps=prev[('macc', i)])
        outs.append(sto)

    return outs



def kernel_unfused(**inputs):
    x = np.ascontiguousarray(inputs["L0_attn_hidden_states"], dtype=np.float32)
    f32 = lambda a: np.ascontiguousarray(a, dtype=np.float32)

    cq, sq_, ck, sk = xpos_tables()
    rotT = rot_matT()
    wqkv = f32(inputs["L0_attn_W_qkv"])
    wg = f32(inputs["L0_attn_W_g"])
    nc1 = build_l0prep()
    im1 = []
    for p in range(4):
        for half in range(2):
            sl = slice(half * TOWN, (half + 1) * TOWN)
            im1.append({
                "x": np.ascontiguousarray(x[p, sl]), "wqkv": wqkv, "wg": wg,
                "cos_q": np.ascontiguousarray(cq[sl].T).astype(BF), "sin_q": np.ascontiguousarray(sq_[sl].T).astype(BF),
                "cos_k": np.ascontiguousarray(ck[sl].T).astype(BF), "sin_k": np.ascontiguousarray(sk[sl].T).astype(BF),
                "rotmT": rotT, "ident": IDENT_BF,
            })
    r1 = run_bass_kernel_spmd(nc1, im1, core_ids=list(range(8))).results

    decay = f32(inputs["L0_attn_decay"])
    caus_f = np.stack([(np.arange(512)[None, :] - np.arange(128)[:, None] - 128 * v >= 0)
                       for v in range(4)]).astype(np.float32)
    nc2 = build_l0ret()
    im2 = []
    for p in range(4):
        qr_full = np.concatenate([r1[2 * p]["qr"], r1[2 * p + 1]["qr"]], axis=1)
        kr_full = np.concatenate([r1[2 * p]["kr"], r1[2 * p + 1]["kr"]], axis=1)
        v_full = np.concatenate([r1[2 * p]["v"], r1[2 * p + 1]["v"]], axis=0)
        g_full = np.concatenate([r1[2 * p]["gate"], r1[2 * p + 1]["gate"]], axis=0)
        for h2 in range(2):
            heads = [h2 * 4 + j for j in range(4)]
            dec_in = np.zeros((1, 8), np.float32)
            dec_in[0, :4] = decay[heads]
            dec_in[0, 4:] = 0.5
            im2.append({
                "qh": np.ascontiguousarray(np.stack([qr_full[h * 128:(h + 1) * 128] for h in heads])),
                "kh": np.ascontiguousarray(np.stack([kr_full[h * 128:(h + 1) * 128] for h in heads])),
                "vh": np.ascontiguousarray(np.stack([v_full[:, h * 256:(h + 1) * 256] for h in heads])),
                "gh": np.ascontiguousarray(np.stack([g_full[:, h * 256:(h + 1) * 256] for h in heads])),
                "decay": dec_in, "causal": caus_f, "ident": IDENT_BF,
                "ones128": np.ones((1, 128), np.float32), "hoff": np.zeros((1, 1), np.float32),
            })
    r2 = run_bass_kernel_spmd(nc2, im2, core_ids=list(range(8))).results

    nc3 = build_l0ffn()
    im3 = []
    for p in range(4):
        gfm_full = np.zeros((2048, T), dtype=BF)
        for h in range(8):
            gfm_full[h * 256:(h + 1) * 256] = r2[2 * p + (0 if h < 4 else 1)]["gfm"][h % 4]
        for half in range(2):
            sl = slice(half * TOWN, (half + 1) * TOWN)
            im3.append({
                "x": np.ascontiguousarray(x[p, sl]),
                "gfm": np.ascontiguousarray(gfm_full[:, sl]),
                "wproj": f32(inputs["L0_attn_W_proj"]),
                "wgr": f32(inputs["L0_ffn_Wg"]), "bg": f32(inputs["L0_ffn_bg"]),
                "w1": f32(inputs["L0_ffn_W1"]),
                "b1t": np.ascontiguousarray(f32(inputs["L0_ffn_b1"]).reshape(4, 32, 128).transpose(2, 0, 1).reshape(128, 128)),
                "w2": f32(inputs["L0_ffn_W2"]), "b2": f32(inputs["L0_ffn_b2"]),
                "identb": IDENT_BF, "ones128": np.ones((1, 128), np.float32),
            })
    r3 = run_bass_kernel_spmd(nc3, im3, core_ids=list(range(8))).results

    Wr = f32(inputs["L1_attn_W_qkv"]).reshape(16, 64, 3, 1024)
    br = f32(inputs["L1_attn_b_qkv"]).reshape(16, 64, 3)
    fW = f32(inputs["L1_ffn_W_qkv"])
    fb = f32(inputs["L1_ffn_b_qkv"])
    caus_b = caus_f.astype(BF)
    nc4 = build_l1attn()
    im4 = []
    for p in range(4):
        xn1_full = np.concatenate([r3[2 * p]["xn1fm"], r3[2 * p + 1]["xn1fm"]], axis=1)
        for h2 in range(2):
            heads = [h2 * 8 + j for j in range(8)]
            def bcol(which):
                out = np.zeros((128, 4), np.float32)
                for hl, h in enumerate(heads):
                    out[(hl % 2) * 64:(hl % 2) * 64 + 64, hl // 2] = br[h, :, which]
                return out
            bvrow = np.zeros((1, 512), np.float32)
            fbvrow = np.zeros((1, 512), np.float32)
            for hl, h in enumerate(heads):
                bvrow[0, hl * 64:(hl + 1) * 64] = br[h, :, 2]
                fbvrow[0, hl * 64:(hl + 1) * 64] = fb[2048 + h * 64: 2048 + (h + 1) * 64]
            def fbcol(which):
                out = np.zeros((128, 4), np.float32)
                for hl, h in enumerate(heads):
                    out[(hl % 2) * 64:(hl % 2) * 64 + 64, hl // 2] = fb[which * 1024 + h * 64: which * 1024 + (h + 1) * 64]
                return out
            cols = [h * 64 + j for h in heads for j in range(64)]
            im4.append({
                "xn1fm": np.ascontiguousarray(xn1_full),
                "wq": np.ascontiguousarray(np.concatenate([Wr[h, :, 0, :].T for h in heads], axis=1)),
                "wk": np.ascontiguousarray(np.concatenate([Wr[h, :, 1, :].T for h in heads], axis=1)),
                "wv": np.ascontiguousarray(np.concatenate([Wr[h, :, 2, :].T for h in heads], axis=1)),
                "bqc": bcol(0), "bkc": bcol(1), "bvrow": bvrow,
                "fwq": np.ascontiguousarray(fW[:, 0:1024][:, cols]),
                "fwk": np.ascontiguousarray(fW[:, 1024:2048][:, cols]),
                "fwv": np.ascontiguousarray(fW[:, 2048:3072][:, cols]),
                "fbqc": fbcol(0), "fbkc": fbcol(1), "fbvrow": fbvrow,
                "causal": caus_b, "ones128": np.ones((1, 128), np.float32),
                "onescol": np.ones((128, 1), np.float32), "identf": IDENT_F32,
            })
    r4 = run_bass_kernel_spmd(nc4, im4, core_ids=list(range(8))).results

    nc5 = build_l1fin()
    im5 = []
    for p in range(4):
        z1_full = np.zeros((1024, T), dtype=BF)
        z2_full = np.zeros((1024, T), dtype=BF)
        for h in range(16):
            src = r4[2 * p + (0 if h < 8 else 1)]
            z1_full[h * 64:(h + 1) * 64] = src["z1"][h % 8]
            z2_full[h * 64:(h + 1) * 64] = src["z2"][h % 8]
        for half in range(2):
            sl = slice(half * TOWN, (half + 1) * TOWN)
            im5.append({
                "x1": r3[2 * p + half]["x1"],
                "xn1fm": r3[2 * p + half]["xn1fm"],
                "z1fm": np.ascontiguousarray(z1_full[:, sl]),
                "z2fm": np.ascontiguousarray(z2_full[:, sl]),
                "woT": np.ascontiguousarray(f32(inputs["L1_attn_W_o"]).T),
                "bo": f32(inputs["L1_attn_b_o"]).reshape(1, 1024),
                "wgate": f32(inputs["L1_ffn_W_gate"]),
                "bgate": f32(inputs["L1_ffn_b_gate"]).reshape(1, 8),
                "wexp": f32(inputs["L1_ffn_W_experts"]),
                "bexp": f32(inputs["L1_ffn_b_experts"]).reshape(1, 8192),
                "identb": IDENT_BF, "ones128": np.ones((1, 128), np.float32),
            })
    r5 = run_bass_kernel_spmd(nc5, im5, core_ids=list(range(8))).results

    out = np.zeros((B, T, D), dtype=np.float32)
    for p in range(4):
        for half in range(2):
            out[p, half * TOWN:(half + 1) * TOWN] = r5[2 * p + half]["out"]
    return out



def build_fused(groups=None, detect_races=True):
    groups = groups or [[0, 1], [2, 3], [4, 5], [6, 7]]
    nc = bass.Bass("TRN2", target_bir_lowering=False, debug=False, detect_race_conditions=detect_races)
    S = Sched()
    es_main = contextlib.ExitStack()
    dram = lambda nm, shp, dt=F32, kind="ExternalInput": nc.dram_tensor(nm, shp, dt, kind=kind).ap()

    x_full = dram("x", [T, D])
    x_own = dram("x_own", [TOWN, D])
    wqkv_my = dram("wqkv", [D, 2048])
    wg_my = dram("wg", [D, 1024])
    tabs = {nm: dram(nm, [512, T], BF16) for nm in ("cos_q", "sin_q", "cos_k", "sin_k")}
    rotmT_i = dram("rotmT", [128, 128], BF16)
    identb_i = dram("ident", [128, 128], BF16)
    identf_i = dram("identf", [128, 128], F32)
    ones_i = dram("ones128", [1, 128])
    onesc_i = dram("onescol", [128, 1])
    decay_i = dram("decay", [1, 8])
    causf_i = dram("causf", [4, 128, 512])
    causb_i = dram("causb", [4, 128, 512], BF16)
    wproj_i = dram("wproj", [2048, 1024])
    wgr_i = dram("wgr", [4, 1024, 4])
    bg_i = dram("bg", [4, 4])
    w1_i = dram("w1", [4, 1024, 4096])
    b1t_i = dram("b1t", [128, 128])
    w2_i = dram("w2", [4, 4096, 1024])
    b2_i = dram("b2", [4, 1024])
    l1w = {nm: dram(nm, [1024, 512]) for nm in ("wq", "wk", "wv", "fwq", "fwk", "fwv")}
    l1b = {nm: dram(nm, [128, 4]) for nm in ("bqc", "bkc", "fbqc", "fbkc")}
    l1br = {nm: dram(nm, [1, 512]) for nm in ("bvrow", "fbvrow")}
    woT_i = dram("woT", [1024, 1024])
    bo_i = dram("bo", [1, 1024])
    wgate_i = dram("wgate", [1024, 8])
    bgate_i = dram("bgate", [1, 8])
    wexp_i = dram("wexp", [8, 1024, 1024])
    bexp_i = dram("bexp", [1, 8 * 1024])
    o_out = dram("out", [TOWN, D], F32, kind="ExternalOutput")

    qr_t = dram("qr_t", [512, T], BF16, kind="Internal")
    kr_t = dram("kr_t", [512, T], BF16, kind="Internal")
    v_t = dram("v_t", [T, 1024], BF16, kind="Internal")
    g_t = dram("g_t", [T, 1024], BF16, kind="Internal")
    ccg_in = dram("ccg_in", [1024, T], BF16, kind="Internal")
    ccg_out = dram("ccg_out", [2048, T], BF16, kind="Internal")
    ccx_in = dram("ccx_in", [1024, TOWN], BF16, kind="Internal")
    ccx_out = dram("ccx_out", [2048, TOWN], BF16, kind="Internal")
    ccz_in = dram("ccz_in", [1024, T], BF16, kind="Internal")
    ccz_out = dram("ccz_out", [2048, T], BF16, kind="Internal")
    own0_i = dram("own0", [1, 1], mybir.dt.int32)
    x1_t = dram("x1_t", [TOWN, D], F32, kind="Internal")
    w1b_t = dram("w1b", [4, 1024, 4096], BF16, kind="Internal")
    w2b_t = dram("w2b", [4, 4096, 1024], BF16, kind="Internal")
    xpa_t = dram("xpa", [TOWN, D], F32, kind="Internal")
    upd_t = dram("updsum", [TOWN, D], F32, kind="Internal")

    esE = contextlib.ExitStack()
    esE2 = contextlib.ExitStack()

    off_sb = es_main.enter_context(nc.sbuf_tensor("off_sb", [1, 1], mybir.dt.int32))
    goff_cell = [None]
    ld_off = S.dma("sync", off_sb[:], own0_i)
    def _setup_off(sync):
        reg = sync.alloc_register("own0reg")
        inst = sync.reg_load(reg, off_sb[0:1, 0:1])
        goff_cell[0] = sync.snap(reg)
        return inst
    off_op = S.add("sync", _setup_off, deps=[ld_off])

    es0 = contextlib.ExitStack()
    io0 = {"x": x_full, "wqkv": wqkv_my, "wg": wg_my, "rotmT": rotmT_i, "ident": identb_i,
           "qr": qr_t, "kr": kr_t, "v": v_t, "gate": g_t,
           "cos_q": tabs["cos_q"], "sin_q": tabs["sin_q"], "cos_k": tabs["cos_k"], "sin_k": tabs["sin_k"]}
    p0 = emit_l0prep(nc, S, es0, io0, [], ntok=T, nq=4, nv=1024, ng=1024, store_xnfm=False, pfx="p_")
    es0.close()

    es2_ = contextlib.ExitStack()
    qr_v = qr_t.rearrange("(h p) t -> h p t", p=128)
    kr_v = kr_t.rearrange("(h p) t -> h p t", p=128)
    io2 = {"qh": lambda m: qr_v[m], "kh": lambda m: kr_v[m],
           "vh": lambda m: v_t[:, m * 256:(m + 1) * 256],
           "gh": lambda m: g_t[:, m * 256:(m + 1) * 256],
           "decay": decay_i, "causal": causf_i, "ident": identb_i, "ones128": ones_i,
           "gfm": lambda m: ccg_in[m * 256:(m + 1) * 256]}
    p2 = emit_l0ret(nc, S, es2_, io2, bar=p0, pfx="r_")
    es2_.close()

    cc1 = S.add("gpsimd", lambda g: g.collective_compute(
        "AllGather", ALU.bypass, replica_groups=groups, ins=[ccg_in.opt()], outs=[ccg_out.opt()]),
        deps=p2)

    es3 = contextlib.ExitStack()
    io3 = {"x": x_own, "gfm": ccg_out, "goff": goff_cell, "goff_dep": off_op,
           "wproj": wproj_i, "wgr": wgr_i, "bg": bg_i,
           "w1": w1_i, "b1t": b1t_i, "w2": w2_i, "b2": b2_i, "identb": identb_i,
           "ones128": ones_i, "x1": x1_t, "xn1fm": ccx_in,
           "w1b": w1b_t, "w2b": w2b_t, "xpa": xpa_t, "updsum": upd_t}
    p3 = emit_l0ffn(nc, S, es3, io3, bar=[cc1] + p2 + p0, pfx="f_", es_early=esE, bar_early=[], es_ln_ext=esE2)
    es3.close()
    esE.close()

    cc2 = S.add("gpsimd", lambda g: g.collective_compute(
        "AllGather", ALU.bypass, replica_groups=groups, ins=[ccx_in.opt()], outs=[ccx_out.opt()]),
        deps=p3)

    es4 = contextlib.ExitStack()
    io4 = {"xn1fm_parts": [(ccx_out[0:1024], 0), (ccx_out[1024:2048], TOWN)],
           "causal": causb_i, "ones128": ones_i, "onescol": onesc_i, "identf": identf_i,
           "z1": lambda h: ccz_in[h * 64:(h + 1) * 64],
           "z2": lambda h: ccz_in[512 + h * 64: 512 + (h + 1) * 64]}
    io4.update(l1w); io4.update(l1b); io4.update(l1br)
    p4 = emit_l1attn(nc, S, es4, io4, bar=[cc2] + p3, pfx="a_")
    es4.close()

    cc3 = S.add("gpsimd", lambda g: g.collective_compute(
        "AllGather", ALU.bypass, replica_groups=groups, ins=[ccz_in.opt()], outs=[ccz_out.opt()]),
        deps=p4)

    es5 = contextlib.ExitStack()
    io5 = {"x1": x1_t, "xn1fm": ccx_in, "goff": goff_cell, "goff_dep": off_op,
           "z1get": lambda c: ccz_out[c * 128:(c + 1) * 128] if c < 4 else ccz_out[1024 + (c - 4) * 128: 1024 + (c - 3) * 128],
           "z2get": lambda c: ccz_out[512 + c * 128: 512 + (c + 1) * 128] if c < 4 else ccz_out[1536 + (c - 4) * 128: 1536 + (c - 3) * 128],
           "woT": woT_i, "bo": bo_i, "wgate": wgate_i, "bgate": bgate_i,
           "wexp": wexp_i, "bexp": bexp_i, "identb": identb_i, "ones128": ones_i,
           "out": o_out}
    p5 = emit_l1fin(nc, S, es5, io5, bar=[cc3] + p4, pfx="n_")
    es5.close()

    S.emit(nc, es_main, final_waits=p5)
    es_main.close()
    return nc


def fused_in_maps(inputs):
    x = np.ascontiguousarray(inputs["L0_attn_hidden_states"], dtype=np.float32)
    f32 = lambda a: np.ascontiguousarray(a, dtype=np.float32)
    cq, sq_, ck, sk = xpos_tables()
    rotT = rot_matT()
    Wqkv = f32(inputs["L0_attn_W_qkv"])
    Wg = f32(inputs["L0_attn_W_g"])
    decay = f32(inputs["L0_attn_decay"])
    caus_f = np.stack([(np.arange(512)[None, :] - np.arange(128)[:, None] - 128 * v >= 0)
                       for v in range(4)]).astype(np.float32)
    Wr = f32(inputs["L1_attn_W_qkv"]).reshape(16, 64, 3, 1024)
    br = f32(inputs["L1_attn_b_qkv"]).reshape(16, 64, 3)
    fW = f32(inputs["L1_ffn_W_qkv"])
    fb = f32(inputs["L1_ffn_b_qkv"])
    maps = []
    for p in range(4):
        for h2 in range(2):
            heads0 = [h2 * 4 + j for j in range(4)]
            heads1 = [h2 * 8 + j for j in range(8)]
            own = slice(h2 * TOWN, (h2 + 1) * TOWN)
            qcols = [h * 128 + j for h in heads0 for j in range(128)]
            vcols = [h * 256 + j for h in heads0 for j in range(256)]
            wqkv_my = np.concatenate([Wqkv[:, qcols], Wqkv[:, [1024 + c for c in qcols]],
                                      Wqkv[:, [2048 + c for c in vcols]]], axis=1)
            dec_in = np.zeros((1, 8), np.float32)
            dec_in[0, :4] = decay[heads0]
            dec_in[0, 4:] = 0.5
            def bcol(which):
                o = np.zeros((128, 4), np.float32)
                for hl, h in enumerate(heads1):
                    o[(hl % 2) * 64:(hl % 2) * 64 + 64, hl // 2] = br[h, :, which]
                return o
            def fbcol(which):
                o = np.zeros((128, 4), np.float32)
                for hl, h in enumerate(heads1):
                    o[(hl % 2) * 64:(hl % 2) * 64 + 64, hl // 2] = fb[which * 1024 + h * 64: which * 1024 + (h + 1) * 64]
                return o
            bvrow = np.zeros((1, 512), np.float32)
            fbvrow = np.zeros((1, 512), np.float32)
            for hl, h in enumerate(heads1):
                bvrow[0, hl * 64:(hl + 1) * 64] = br[h, :, 2]
                fbvrow[0, hl * 64:(hl + 1) * 64] = fb[2048 + h * 64: 2048 + (h + 1) * 64]
            cols1 = [h * 64 + j for h in heads1 for j in range(64)]
            maps.append({
                "x": x[p], "x_own": np.ascontiguousarray(x[p, own]),
                "own0": np.array([[h2 * TOWN]], np.int32),
                "wqkv": np.ascontiguousarray(wqkv_my), "wg": np.ascontiguousarray(Wg[:, vcols]),
                "cos_q": np.ascontiguousarray(cq.T[qcols]).astype(BF),
                "sin_q": np.ascontiguousarray(sq_.T[qcols]).astype(BF),
                "cos_k": np.ascontiguousarray(ck.T[qcols]).astype(BF),
                "sin_k": np.ascontiguousarray(sk.T[qcols]).astype(BF),
                "rotmT": rotT, "ident": IDENT_BF, "identf": IDENT_F32,
                "ones128": np.ones((1, 128), np.float32), "onescol": np.ones((128, 1), np.float32),
                "decay": dec_in, "causf": caus_f, "causb": caus_f.astype(BF),
                "wproj": f32(inputs["L0_attn_W_proj"]),
                "wgr": f32(inputs["L0_ffn_Wg"]), "bg": f32(inputs["L0_ffn_bg"]),
                "w1": f32(inputs["L0_ffn_W1"]),
                "b1t": np.ascontiguousarray(f32(inputs["L0_ffn_b1"]).reshape(4, 32, 128).transpose(2, 0, 1).reshape(128, 128)),
                "w2": f32(inputs["L0_ffn_W2"]), "b2": f32(inputs["L0_ffn_b2"]),
                "wq": np.ascontiguousarray(np.concatenate([Wr[h, :, 0, :].T for h in heads1], axis=1)),
                "wk": np.ascontiguousarray(np.concatenate([Wr[h, :, 1, :].T for h in heads1], axis=1)),
                "wv": np.ascontiguousarray(np.concatenate([Wr[h, :, 2, :].T for h in heads1], axis=1)),
                "bqc": bcol(0), "bkc": bcol(1), "bvrow": bvrow,
                "fwq": np.ascontiguousarray(fW[:, 0:1024][:, cols1]),
                "fwk": np.ascontiguousarray(fW[:, 1024:2048][:, cols1]),
                "fwv": np.ascontiguousarray(fW[:, 2048:3072][:, cols1]),
                "fbqc": fbcol(0), "fbkc": fbcol(1), "fbvrow": fbvrow,
                "woT": np.ascontiguousarray(f32(inputs["L1_attn_W_o"]).T),
                "bo": f32(inputs["L1_attn_b_o"]).reshape(1, 1024),
                "wgate": f32(inputs["L1_ffn_W_gate"]),
                "bgate": f32(inputs["L1_ffn_b_gate"]).reshape(1, 8),
                "wexp": f32(inputs["L1_ffn_W_experts"]),
                "bexp": f32(inputs["L1_ffn_b_experts"]).reshape(1, 8192),
            })
    return maps


def kernel(**inputs):
    maps = fused_in_maps(inputs)
    nc = build_fused()
    res = run_bass_kernel_spmd(nc, maps, core_ids=list(range(8))).results
    out = np.zeros((B, T, D), dtype=np.float32)
    for p in range(4):
        for half in range(2):
            out[p, half * TOWN:(half + 1) * TOWN] = res[2 * p + half]["out"]
    return out


# version 07
# speedup vs baseline: 1.0775x; 1.0313x over previous
import numpy as np
import ml_dtypes
import contextlib
import concourse.bass as bass
import concourse.mybir as mybir
from concourse.bass_utils import run_bass_kernel_spmd

F32 = mybir.dt.float32
BF16 = mybir.dt.bfloat16
AF = mybir.ActivationFunctionType
ALU = mybir.AluOpType
AX = mybir.AxisListType
BF = ml_dtypes.bfloat16

ENGS = ("tensor", "vector", "scalar", "gpsimd", "sync")

B, T, D = 4, 4096, 1024
TOWN = T // 2
NT = TOWN // 128


class Sched:

    def __init__(self):
        self.ops = []

    def add(self, eng, emit, deps=()):
        tok = len(self.ops)
        self.ops.append([eng, emit, [d for d in deps if d is not None], False])
        return tok

    def dma(self, eng, out, in_, deps=()):
        tok = len(self.ops)
        self.ops.append([eng, (out, in_, None), [d for d in deps if d is not None], True])
        return tok

    def dma_acc(self, eng, out, in_, op, deps=()):
        tok = len(self.ops)
        self.ops.append([eng, (out, in_, op), [d for d in deps if d is not None], True])
        return tok

    def emit(self, nc, es, final_waits=()):
        import os
        ops = self.ops
        pfx = os.environ.get("SCHED_PREFIX")
        if pfx:
            ops = ops[:int(pfx)]
            final_waits = [d for d in final_waits if d < len(ops)]
        n = len(ops)
        need_inc = [False] * n
        for (eng, emit, deps, isdma) in ops:
            for d in deps:
                need_inc[d] = True
        final_waits = list(final_waits)
        for i, (eng, emit, deps, isdma) in enumerate(ops):
            if isdma and not need_inc[i]:
                need_inc[i] = True
                final_waits.append(i)
        for d in final_waits:
            need_inc[d] = True
        val = [0] * n
        cnt = {}
        for i, (eng, emit, deps, isdma) in enumerate(ops):
            key = (eng, isdma)
            if need_inc[i]:
                cnt[key] = cnt.get(key, 0) + (16 if isdma else 1)
            val[i] = cnt.get(key, 0)
        sems = {}
        for e in ENGS:
            sems[(e, False)] = es.enter_context(nc.semaphore("Sc" + e))
            sems[(e, True)] = es.enter_context(nc.semaphore("Sd" + e))
        blk = es.enter_context(nc.Block())
        per_eng = {e: [i for i in range(n) if ops[i][0] == e] for e in ENGS}

        def run_engine(engine_obj, e):
            waited = {}
            for i in per_eng[e]:
                eng, emit, deps, isdma = ops[i]
                needs = {}
                for d in deps:
                    k2 = (ops[d][0], ops[d][3])
                    if k2 == (e, False) and e == "tensor":
                        continue
                    needs[k2] = max(needs.get(k2, 0), val[d])
                if isdma:
                    prev = val[i] - 16
                    if prev > 0:
                        needs[(e, True)] = max(needs.get((e, True), 0), prev)
                for k2 in sorted(needs):
                    if needs[k2] > waited.get(k2, 0):
                        engine_obj.wait_ge(sems[k2], needs[k2])
                        waited[k2] = needs[k2]
                if isdma:
                    out, in_, aop = emit
                    if callable(out):
                        out = out()
                    if callable(in_):
                        in_ = in_()
                    if aop is None:
                        try:
                            inst = engine_obj.dma_start(out=out, in_=in_)
                        except ValueError as _e:
                            import traceback as _tb
                            _tb.print_exc()
                            raise
                    else:
                        inst = engine_obj.dma_start(out=out, in_=in_, accum_op=aop)
                else:
                    inst = emit(engine_obj)
                if need_inc[i]:
                    inst.then_inc(sems[(eng, isdma)], 16 if isdma else 1)
            if e == "sync":
                fw = {}
                for d in final_waits:
                    k2 = (ops[d][0], ops[d][3])
                    fw[k2] = max(fw.get(k2, 0), val[d])
                for k2 in sorted(fw):
                    if fw[k2] > waited.get(k2, 0):
                        engine_obj.wait_ge(sems[k2], fw[k2])

        for e in ENGS:
            getattr(blk, e)(lambda engine_obj, e=e: run_engine(engine_obj, e))



def xpos_tables():
    QK = 1024
    SCALE_BASE = 512.0
    j2 = np.arange(0, QK, 2, dtype=np.float32)
    base_scale = (j2 + 0.4 * QK) / (1.4 * QK)
    pos = np.arange(T, dtype=np.float32)
    scale = base_scale[None, :] ** (pos[:, None] / SCALE_BASE)
    inv_freq = (1.0 / 10000.0 ** (np.arange(512, dtype=np.float32) / 512.0))
    sinu = pos[:, None] * inv_freq[None, :]
    sin, cos = np.sin(sinu), np.cos(sinu)
    def rep2(x):
        return np.repeat(x, 2, axis=1)
    cos_q = rep2(cos * scale)
    sin_q = rep2(sin * scale)
    cos_k = rep2(cos / scale)
    sin_k = rep2(sin / scale)
    return cos_q, sin_q, cos_k, sin_k


def rot_matT():
    P = np.zeros((128, 128), dtype=np.float32)
    for j in range(64):
        P[2 * j, 2 * j + 1] = -1.0
        P[2 * j + 1, 2 * j] = 1.0
    return np.ascontiguousarray(P.T).astype(BF)


IDENT_BF = np.eye(128, dtype=BF)
IDENT_F32 = np.eye(128, dtype=np.float32)



def build_l0prep():
    nc = bass.Bass("TRN2", target_bir_lowering=False, debug=False)
    io = {}
    io["x"] = nc.dram_tensor("x", [TOWN, D], F32, kind="ExternalInput").ap()
    io["wqkv"] = nc.dram_tensor("wqkv", [D, 4096], F32, kind="ExternalInput").ap()
    io["wg"] = nc.dram_tensor("wg", [D, 2048], F32, kind="ExternalInput").ap()
    io["cos_q"] = nc.dram_tensor("cos_q", [1024, TOWN], BF16, kind="ExternalInput").ap()
    io["sin_q"] = nc.dram_tensor("sin_q", [1024, TOWN], BF16, kind="ExternalInput").ap()
    io["cos_k"] = nc.dram_tensor("cos_k", [1024, TOWN], BF16, kind="ExternalInput").ap()
    io["sin_k"] = nc.dram_tensor("sin_k", [1024, TOWN], BF16, kind="ExternalInput").ap()
    io["rotmT"] = nc.dram_tensor("rotmT", [128, 128], BF16, kind="ExternalInput").ap()
    io["ident"] = nc.dram_tensor("ident", [128, 128], BF16, kind="ExternalInput").ap()
    io["qr"] = nc.dram_tensor("qr", [1024, TOWN], BF16, kind="ExternalOutput").ap()
    io["kr"] = nc.dram_tensor("kr", [1024, TOWN], BF16, kind="ExternalOutput").ap()
    io["v"] = nc.dram_tensor("v", [TOWN, 2048], BF16, kind="ExternalOutput").ap()
    io["gate"] = nc.dram_tensor("gate", [TOWN, 2048], BF16, kind="ExternalOutput").ap()
    io["xnfm"] = nc.dram_tensor("xnfm", [1024, TOWN], BF16, kind="ExternalOutput").ap()
    es = contextlib.ExitStack()
    S = Sched()
    outs = emit_l0prep(nc, S, es, io, [], ntok=TOWN)
    S.emit(nc, es, final_waits=outs)
    es.close()
    return nc


def emit_l0prep(nc, S, es, io, bar, ntok, nq=8, nv=2048, ng=2048, store_xnfm=True, pfx=""):
    i_x = io["x"]
    i_wqkv = io["wqkv"]; i_wg = io["wg"]
    i_cq = io["cos_q"]; i_sq = io["sin_q"]; i_ck = io["cos_k"]; i_sk = io["sin_k"]
    i_rot = io["rotmT"]; i_id = io["ident"]
    o_qr = io["qr"]; o_kr = io["kr"]; o_v = io["v"]; o_gate = io["gate"]
    o_xnfm = io.get("xnfm")
    TOWN = ntok
    NT = TOWN // 128
    NCOLS = 2 * 128 * nq + nv
    sb = lambda nme, s, d=F32: es.enter_context(nc.sbuf_tensor(pfx + nme, s, d))
    psum = lambda nme, s, d=F32: es.enter_context(nc.psum_tensor(pfx + nme, s, d))
    outs = []

    w_sb = sb("w_sb", [128, 8 * NCOLS], BF16)
    wg_sb = sb("wg_sb", [128, 8 * ng], BF16)
    rot_sb = sb("rot_sb", [128, 128], BF16)
    id_sb = sb("id_sb", [128, 128], BF16)
    ld_w = [S.dma("gpsimd", w_sb[:, hc * NCOLS:(hc + 1) * NCOLS], i_wqkv[hc * 128:(hc + 1) * 128, :], deps=bar)
            for hc in range(8)]
    ld_wg = [S.dma("gpsimd", wg_sb[:, hc * ng:(hc + 1) * ng], i_wg[hc * 128:(hc + 1) * 128, :], deps=bar)
             for hc in range(8)]
    ld_rot = S.dma("sync", rot_sb[:], i_rot, deps=bar)
    ld_id = S.dma("sync", id_sb[:], i_id, deps=bar)

    xn_fm = sb("xn_fm", [128, 8 * TOWN], BF16)
    x_t = [sb("x_t0", [128, 1024])]
    x2_t = sb("x2_t", [128, 1024])
    s1 = sb("s1", [128, 1]); s2 = sb("s2", [128, 1])
    mu = sb("mu", [128, 1]); m2 = sb("m2", [128, 1]); var = sb("var", [128, 1])
    sq = sb("sq", [128, 1]); rstd = sb("rstd", [128, 1])
    xn_tok = [sb(f"xn_tok{i}", [128, 1024], BF16) for i in range(2)]
    ps_tr = [psum("ps_tr0", [128, 128], BF16)]

    prev_tile_done = [None, None]
    prev_xn_done = [None, None]
    prev_tr_evac = [None, None]
    xn_ready = []
    for i in range(NT):
        bi = 0
        ld = S.dma("sync", x_t[bi][:], i_x[i * 128:(i + 1) * 128, :],
                   deps=(prev_tile_done[bi] or []) + bar)
        sqr = S.add("scalar", lambda s, bi=bi: s.activation(out=x2_t[:], in_=x_t[bi][:], func=AF.Square),
                    deps=[ld, prev_xn_done[(i + 1) % 2]])
        r1 = S.add("vector", lambda v, bi=bi: v.reduce_sum(out=s1[:], in_=x_t[bi][:], axis=AX.X), deps=[ld])
        r2 = S.add("vector", lambda v: v.reduce_sum(out=s2[:], in_=x2_t[:], axis=AX.X), deps=[sqr, r1])
        c1 = S.add("vector", lambda v: v.tensor_scalar(out=mu[:], in0=s1[:], scalar1=1.0 / 1024, scalar2=None, op0=ALU.mult), deps=[r2])
        c2 = S.add("vector", lambda v: v.tensor_tensor(out=m2[:], in0=mu[:], in1=mu[:], op=ALU.mult), deps=[c1])
        c3 = S.add("vector", lambda v: v.tensor_scalar(out=var[:], in0=s2[:], scalar1=1.0 / 1024, scalar2=None, op0=ALU.mult), deps=[c2])
        c4 = S.add("vector", lambda v: v.tensor_sub(out=var[:], in0=var[:], in1=m2[:]), deps=[c3])
        c4b = S.add("vector", lambda v: v.tensor_scalar(out=var[:], in0=var[:], scalar1=1e-5, scalar2=None, op0=ALU.add), deps=[c4])
        c5 = S.add("scalar", lambda s: s.activation(out=sq[:], in_=var[:], func=AF.Sqrt), deps=[c4b])
        c6 = S.add("vector", lambda v: v.reciprocal(out=rstd[:], in_=sq[:]), deps=[c5])
        xo = S.add("vector", lambda v, bi=bi: v.tensor_scalar(out=xn_tok[bi][:], in0=x_t[bi][:],
                   scalar1=mu[:], scalar2=rstd[:], op0=ALU.subtract, op1=ALU.mult),
                   deps=[c6, prev_xn_done[bi]])
        prev_tile_done[bi] = [xo, sqr]
        evs = []
        for c in range(8):
            tb = 0
            tr = S.add("tensor", lambda t, bi=bi, c=c, tb=tb: t.transpose(
                out=ps_tr[tb][:], in_=xn_tok[bi][:, c * 128:(c + 1) * 128], identity=id_sb[:]),
                deps=[xo, ld_id, prev_tr_evac[tb]])
            ev = S.add("vector", lambda v, i=i, c=c, tb=tb: v.tensor_copy(
                out=xn_fm[:, c * TOWN + i * 128: c * TOWN + (i + 1) * 128], in_=ps_tr[tb][:]),
                deps=[tr])
            prev_tr_evac[tb] = ev
            evs.append(ev)
        prev_xn_done[bi] = evs[-1]
        xn_ready.append(evs)
    xn_all = [e for evs in xn_ready for e in evs]

    if store_xnfm:
        for c in range(8):
            outs.append(S.dma("sync", o_xnfm[c * 128:(c + 1) * 128, :], xn_fm[:, c * TOWN:(c + 1) * TOWN],
                              deps=[e for evs in xn_ready for e in evs[c:c + 1]]))

    ps_qkv = [psum(f"ps_qkv{i}", [128, 512]) for i in range(2)]
    ps_rot = [psum(f"ps_rot{i}", [128, 512]) for i in range(2)]
    cs_t = [sb("cs_t0", [128, 2 * TOWN], BF16)]
    qf = [sb(f"qf{i}", [128, 512]) for i in range(2)]
    qb = [sb(f"qb{i}", [128, 512], BF16) for i in range(2)]
    t1b = [sb(f"t1b{i}", [128, 512]) for i in range(2)]
    qrot = [sb(f"qrot{i}", [128, 512], BF16) for i in range(2)]
    prev_cs_use = [None, None]
    prev_q_sto = {}
    for c in range(2 * nq):
        cb = 0
        is_q = c < nq
        cc = c if is_q else c - nq
        src_c = i_cq if is_q else i_ck
        src_s = i_sq if is_q else i_sk
        ldc = S.dma("sync", cs_t[cb][:, 0:TOWN], src_c[cc * 128:(cc + 1) * 128, :], deps=[prev_cs_use[cb]])
        lds = S.dma("sync", cs_t[cb][:, TOWN:2 * TOWN], src_s[cc * 128:(cc + 1) * 128, :], deps=[prev_cs_use[cb]])
        for j in range(TOWN // 512):
            pb = (c * (TOWN // 512) + j) % 2
            mms = []
            for h in range(8):
                mms.append(S.add("tensor", lambda t, h=h, c=c, j=j, pb=pb: t.matmul(
                    out=ps_qkv[pb][:], lhsT=w_sb[:, h * NCOLS + c * 128:h * NCOLS + (c + 1) * 128],
                    rhs=xn_fm[:, h * TOWN + j * 512:h * TOWN + (j + 1) * 512],
                    start=(h == 0), stop=(h == 7)),
                    deps=([ld_w[h]] + (xn_all if h == 0 else []) + ([prev_q_sto.get((pb, 'f'))] if h == 0 else []))))
            e1 = S.add("vector", lambda v, pb=pb: v.tensor_copy(out=qf[pb][:], in_=ps_qkv[pb][:]),
                       deps=[mms[-1], prev_q_sto.get((pb, 'use_f'))])
            e2 = S.add("vector", lambda v, pb=pb: v.tensor_copy(out=qb[pb][:], in_=qf[pb][:]), deps=[e1])
            rot = S.add("tensor", lambda t, pb=pb: t.matmul(out=ps_rot[pb][:], lhsT=rot_sb[:], rhs=qb[pb][:],
                        start=True, stop=True), deps=[e2, ld_rot, prev_q_sto.get((pb, 'rot'))])
            m1 = S.add("vector", lambda v, pb=pb, cb=cb, j=j: v.tensor_tensor(
                out=t1b[pb][:], in0=qf[pb][:], in1=cs_t[cb][:, j * 512:(j + 1) * 512], op=ALU.mult),
                deps=[e1, ldc])
            m2o = S.add("vector", lambda v, pb=pb, cb=cb, j=j: v.tensor_tensor(
                out=qf[pb][:], in0=ps_rot[pb][:], in1=cs_t[cb][:, TOWN + j * 512:TOWN + (j + 1) * 512], op=ALU.mult),
                deps=[rot, lds, m1])
            a1 = S.add("vector", lambda v, pb=pb: v.tensor_add(out=qrot[pb][:], in0=t1b[pb][:], in1=qf[pb][:]),
                       deps=[m2o, prev_q_sto.get((pb, 'sto'))])
            dst = o_qr if is_q else o_kr
            sto = S.dma("sync", dst[cc * 128:(cc + 1) * 128, j * 512:(j + 1) * 512], qrot[pb][:], deps=[a1])
            outs.append(sto)
            prev_cs_use[cb] = m2o
            prev_q_sto[(pb, 'f')] = e1
            prev_q_sto[(pb, 'use_f')] = m2o
            prev_q_sto[(pb, 'rot')] = m2o
            prev_q_sto[(pb, 'sto')] = sto
            prev_q_sto[(pb, 'use_f')] = a1

    ps_v = [psum(f"ps_v{i}", [128, 512]) for i in range(2)]
    vstg = [sb(f"vstg{i}", [128, 512], BF16) for i in range(2)]
    prev_v = {}
    for i in range(NT):
        for which, wbase, wwidth, dst, act in (("v", 2 * 128 * nq, nv, o_v, None), ("g", 0, ng, o_gate, AF.Silu)):
            for oc in range(wwidth // 512):
                pb = (i * 8 + (0 if which == "v" else 4) + oc) % 2
                mms = []
                for h in range(8):
                    if which == "v":
                        rhs = w_sb[:, h * NCOLS + wbase + oc * 512: h * NCOLS + wbase + (oc + 1) * 512]
                    else:
                        rhs = wg_sb[:, h * ng + oc * 512: h * ng + (oc + 1) * 512]
                    mms.append(S.add("tensor", lambda t, h=h, i=i, rhs=rhs, pb=pb: t.matmul(
                        out=ps_v[pb][:], lhsT=xn_fm[:, h * TOWN + i * 128:h * TOWN + (i + 1) * 128],
                        rhs=rhs, start=(h == 0), stop=(h == 7)),
                        deps=([ld_w[h] if which == "v" else ld_wg[h]] + (xn_all if h == 0 else [])
                              + ([prev_v.get((pb, 'ev'))] if h == 0 else []))))
                if act is None:
                    ev = S.add("vector", lambda v, pb=pb: v.tensor_copy(out=vstg[pb][:], in_=ps_v[pb][:]),
                               deps=[mms[-1], prev_v.get((pb, 'sto'))])
                else:
                    ev = S.add("scalar", lambda s, pb=pb: s.activation(out=vstg[pb][:], in_=ps_v[pb][:], func=AF.Silu),
                               deps=[mms[-1], prev_v.get((pb, 'sto'))])
                sto = S.dma("sync", dst[i * 128:(i + 1) * 128, oc * 512:(oc + 1) * 512], vstg[pb][:], deps=[ev])
                outs.append(sto)
                prev_v[(pb, 'ev')] = ev
                prev_v[(pb, 'sto')] = sto

    return outs



def shard_tokens(arr_btd):
    out = []
    for p in range(4):
        for half in range(2):
            out.append(np.ascontiguousarray(arr_btd[p, half * TOWN:(half + 1) * TOWN]))
    return out


def run_l0prep(inputs, run=run_bass_kernel_spmd):
    x = inputs["L0_attn_hidden_states"]
    cq, sq_, ck, sk = xpos_tables()
    rotT = rot_matT()
    wqkv = np.ascontiguousarray(inputs["L0_attn_W_qkv"], dtype=np.float32)
    wg = np.ascontiguousarray(inputs["L0_attn_W_g"], dtype=np.float32)
    nc = build_l0prep()
    in_maps = []
    for p in range(4):
        for half in range(2):
            sl = slice(half * TOWN, (half + 1) * TOWN)
            in_maps.append({
                "x": np.ascontiguousarray(x[p, sl]),
                "wqkv": wqkv, "wg": wg,
                "cos_q": np.ascontiguousarray(cq[sl].T),
                "sin_q": np.ascontiguousarray(sq_[sl].T),
                "cos_k": np.ascontiguousarray(ck[sl].T),
                "sin_k": np.ascontiguousarray(sk[sl].T),
                "rotmT": rotT, "ident": IDENT_BF,
            })
    res = run(nc, in_maps, core_ids=list(range(8)))
    return res.results






def build_l0ret():
    nc = bass.Bass("TRN2", target_bir_lowering=False, debug=False)
    io = {}
    q_ = nc.dram_tensor("qh", [4, 128, T], BF16, kind="ExternalInput").ap()
    k_ = nc.dram_tensor("kh", [4, 128, T], BF16, kind="ExternalInput").ap()
    v_ = nc.dram_tensor("vh", [4, T, 256], BF16, kind="ExternalInput").ap()
    g_ = nc.dram_tensor("gh", [4, T, 256], BF16, kind="ExternalInput").ap()
    io["qh"] = lambda m: q_[m]
    io["kh"] = lambda m: k_[m]
    io["vh"] = lambda m: v_[m]
    io["gh"] = lambda m: g_[m]
    io["decay"] = nc.dram_tensor("decay", [1, 8], F32, kind="ExternalInput").ap()
    io["causal"] = nc.dram_tensor("causal", [4, 128, 512], F32, kind="ExternalInput").ap()
    io["ident"] = nc.dram_tensor("ident", [128, 128], BF16, kind="ExternalInput").ap()
    io["ones128"] = nc.dram_tensor("ones128", [1, 128], F32, kind="ExternalInput").ap()
    nc.dram_tensor("hoff", [1, 1], F32, kind="ExternalInput").ap()
    gfm_ = nc.dram_tensor("gfm", [4, 256, T], BF16, kind="ExternalOutput").ap()
    io["gfm"] = lambda m: gfm_[m]
    es = contextlib.ExitStack()
    S = Sched()
    outs = emit_l0ret(nc, S, es, io, [])
    S.emit(nc, es, final_waits=outs)
    es.close()
    return nc


def emit_l0ret(nc, S, es, io, bar, pfx="r_"):
    i_q = io["qh"]; i_k = io["kh"]; i_v = io["vh"]; i_g = io["gh"]
    i_dec = io["decay"]; i_caus = io["causal"]; i_id = io["ident"]; i_ones = io["ones128"]
    o_gfm = io["gfm"]
    sb = lambda nme, s, d=F32: es.enter_context(nc.sbuf_tensor(pfx + nme, s, d))
    psum = lambda nme, s, d=F32: es.enter_context(nc.psum_tensor(pfx + nme, s, d))
    outs = []

    q_sb = sb("q_sb", [128, T], BF16)
    k_sb = sb("k_sb", [128, T], BF16)
    v_sb = sb("v_sb", [128, 32 * 256], BF16)
    g_sb = sb("g_sb", [128, 32 * 256], BF16)
    gfm_sb = sb("gfm_sb", [128, 2 * T], BF16)
    dec_sb = sb("dec_sb", [1, 8])
    lnd_sb = sb("lnd_sb", [1, 8])
    lndc = sb("lndc", [128, 8])
    ones_sb = sb("ones_sb", [1, 128])
    id_sb = sb("id_sb", [128, 128], BF16)
    caus_sb = sb("caus_sb", [128, 4 * 512])
    iob = sb("iob", [128, 512])
    io2 = sb("io2", [128, 36])
    Bt = [sb(f"Bt{v}", [128, 512]) for v in range(5)]
    dpow = sb("dpow", [128, 36])
    Pbf = [sb(f"Pbf{i}", [128, 512], BF16) for i in range(2)]
    sqt = sb("sqt", [128, 256])
    ot_sb = sb("ot_sb", [128, 256])
    s1c = sb("s1c", [128, 1]); s2c = sb("s2c", [128, 1])
    muc = sb("muc", [128, 1]); m2c = sb("m2c", [128, 1]); varc = sb("varc", [128, 1])
    sqc = sb("sqc", [128, 1]); rsc = sb("rsc", [128, 1])
    nmt = sb("nmt", [128, 256])
    gtd = sb("gtd", [128, 256], BF16)

    ps_ret = [psum(f"ps_ret{i}", [128, 512]) for i in range(2)]
    ps_av = [psum(f"ps_av{i}", [128, 512]) for i in range(4)]
    ps_tr = [psum("ps_tr0", [128, 1024], BF16)]

    ld_dec = S.dma("sync", dec_sb[:], i_dec, deps=bar)
    ld_ones = S.dma("sync", ones_sb[:], i_ones, deps=bar)
    ld_id = S.dma("sync", id_sb[:], i_id, deps=bar)
    ld_caus = [S.dma("sync", caus_sb[:, v * 512:(v + 1) * 512], i_caus[v], deps=bar) for v in range(4)]
    io1 = S.add("gpsimd", lambda g: g.iota(out=iob[:], pattern=[[1, 512]], base=0, channel_multiplier=-1,
                                           allow_small_or_imprecise_dtypes=True), deps=bar)
    io2o = S.add("gpsimd", lambda g: g.iota(out=io2[:], pattern=[[128, 36]], base=-384, channel_multiplier=0,
                                            allow_small_or_imprecise_dtypes=True), deps=bar)
    lno = S.add("scalar", lambda s: s.activation(out=lnd_sb[:], in_=dec_sb[:], func=AF.Ln), deps=[ld_dec])

    prev_map = {}
    for m in range(4):
        ldq = S.dma("sync", q_sb[:], i_q(m), deps=[prev_map.get('q')] + bar)
        ldk = S.dma("sync", k_sb[:], i_k(m), deps=[prev_map.get('k')] + bar)
        ldv = S.dma("sync", v_sb[:].rearrange("p (st c) -> p st c", c=256),
                    i_v(m).rearrange("(st p) c -> p st c", p=128), deps=[prev_map.get('v')] + bar)
        ldg = S.dma("sync", g_sb[:].rearrange("p (st c) -> p st c", c=256),
                    i_g(m).rearrange("(st p) c -> p st c", p=128), deps=[prev_map.get('g')] + bar)
        rep = S.add("tensor", lambda t: t.matmul(out=ps_ret[0][:, 0:8], lhsT=ones_sb[:], rhs=lnd_sb[:],
                                                 start=True, stop=True),
                    deps=[lno, ld_ones, prev_map.get('ret0')])
        cpl = S.add("vector", lambda v: v.tensor_copy(out=lndc[:], in_=ps_ret[0][:, 0:8]), deps=[rep, prev_map.get('lndc')])
        bfull = S.add("scalar", lambda s, m=m: s.activation(out=Bt[4][:], in_=iob[:], func=AF.Exp,
                                                            scale=lndc[:, m:m + 1]), deps=[cpl, io1, prev_map.get('Bfull')])
        bvs = [S.add("vector", lambda v, v_=v_: v.tensor_tensor(out=Bt[v_][:], in0=Bt[4][:],
                     in1=caus_sb[:, v_ * 512:(v_ + 1) * 512], op=ALU.mult),
                     deps=[bfull, ld_caus[v_], prev_map.get('Bv%d' % v_)]) for v_ in range(4)]
        dp1 = S.add("scalar", lambda s, m=m: s.activation(out=dpow[:], in_=io2[:], func=AF.Exp,
                                                          scale=lndc[:, m:m + 1]), deps=[cpl, io2o, prev_map.get('dpow')])
        dp2 = S.add("vector", lambda v: v.tensor_scalar(out=dpow[:], in0=dpow[:], scalar1=0.08838834764831845,
                                                        scalar2=None, op0=ALU.mult), deps=[dp1])

        ret_prev = {0: cpl, 1: None}
        stt_prev = {0: None, 1: None}
        av_last = {}
        gn_done_prev = [None, None, None, None]
        trev_prev = prev_map.get('trev')
        sto_g = []
        for tb in range(8):
            t0 = tb * 512
            n_st = tb * 4 + 4
            av_first = {}
            for st in range(n_st):
                s0 = st * 128
                rb = st % 2
                pb = st % 2
                mi = (t0 - s0) // 128 + 3
                var_idx = 4 if s0 <= t0 - 128 else (s0 - t0) // 128
                mm = S.add("tensor", lambda t, m=m, s0=s0, t0=t0, rb=rb: t.matmul(
                    out=ps_ret[rb][:], lhsT=k_sb[:, s0:s0 + 128], rhs=q_sb[:, t0:t0 + 512],
                    start=True, stop=True),
                    deps=[ldq, ldk, ret_prev[rb]])
                stt = S.add("vector", lambda v, rb=rb, pb=pb, mi=mi, var_idx=var_idx: v.scalar_tensor_tensor(
                    out=Pbf[pb][:], in0=ps_ret[rb][:], scalar=dpow[:, mi:mi + 1],
                    in1=Bt[var_idx][:], op0=ALU.mult, op1=ALU.mult),
                    deps=[mm, dp2, (bvs[var_idx] if var_idx < 4 else bfull), stt_prev[pb]])
                ret_prev[rb] = stt
                avd = []
                for ts4 in range(4):
                    first = (st == 0)
                    last = (st == n_st - 1)
                    av = S.add("tensor", lambda t, pb=pb, ts4=ts4, st=st, first=first, last=last: t.matmul(
                        out=ps_av[ts4][:, 0:256], lhsT=Pbf[pb][:, ts4 * 128:(ts4 + 1) * 128],
                        rhs=v_sb[:, st * 256:(st + 1) * 256],
                        start=first, stop=last),
                        deps=[stt, ldv] + ([gn_done_prev[ts4]] if first else []))
                    avd.append(av)
                    av_last[ts4] = av
                stt_prev[pb] = avd[-1]
            for ts4 in range(4):
                tt = tb * 4 + ts4
                pav = ps_av[ts4]
                d0 = av_last[ts4]
                g0 = S.add("vector", lambda v, pav=pav: v.tensor_copy(out=ot_sb[:], in_=pav[:, 0:256]),
                           deps=[d0, prev_map.get('ot')])
                g1 = S.add("scalar", lambda s: s.activation(out=sqt[:], in_=ot_sb[:], func=AF.Square),
                           deps=[g0, prev_map.get('sqt')])
                g2 = S.add("vector", lambda v: v.reduce_sum(out=s1c[:], in_=ot_sb[:], axis=AX.X),
                           deps=[g0, prev_map.get('s1c')])
                g3 = S.add("vector", lambda v: v.reduce_sum(out=s2c[:], in_=sqt[:], axis=AX.X), deps=[g1, g2])
                g4 = S.add("vector", lambda v: v.tensor_scalar(out=muc[:], in0=s1c[:], scalar1=1.0 / 256, scalar2=None, op0=ALU.mult), deps=[g3])
                g5 = S.add("vector", lambda v: v.tensor_tensor(out=m2c[:], in0=muc[:], in1=muc[:], op=ALU.mult), deps=[g4])
                g6 = S.add("vector", lambda v: v.tensor_scalar(out=varc[:], in0=s2c[:], scalar1=1.0 / 256, scalar2=1e-5, op0=ALU.mult, op1=ALU.add), deps=[g5])
                g7 = S.add("vector", lambda v: v.tensor_sub(out=varc[:], in0=varc[:], in1=m2c[:]), deps=[g6])
                g8 = S.add("scalar", lambda s: s.activation(out=sqc[:], in_=varc[:], func=AF.Sqrt), deps=[g7])
                g9 = S.add("vector", lambda v: v.reciprocal(out=rsc[:], in_=sqc[:]), deps=[g8])
                g10 = S.add("vector", lambda v: v.tensor_scalar(out=nmt[:], in0=ot_sb[:],
                            scalar1=muc[:], scalar2=rsc[:], op0=ALU.subtract, op1=ALU.mult), deps=[g9, prev_map.get('nmt')])
                prev_map['ot'] = g10
                g11 = S.add("vector", lambda v, tt=tt: v.tensor_tensor(out=gtd[:], in0=nmt[:],
                            in1=g_sb[:, tt * 256:(tt + 1) * 256], op=ALU.mult), deps=[g10, ldg, prev_map.get('gtd')])
                gn_done_prev[ts4] = g10
                prev_map['sqt'] = g3; prev_map['s1c'] = g4; prev_map['nmt'] = g11
                trs = []
                for half in range(2):
                    hb = 0
                    tr = S.add("tensor", lambda t, half=half, hb=hb: t.transpose(
                        out=ps_tr[hb][:, 0:128], in_=gtd[:, half * 128:(half + 1) * 128], identity=id_sb[:]),
                        deps=[g11, ld_id, trev_prev])
                    ev = S.add("vector", lambda v, half=half, hb=hb, tt=tt: v.tensor_copy(
                        out=gfm_sb[:, half * T + tt * 128: half * T + (tt + 1) * 128], in_=ps_tr[hb][:, 0:128]),
                        deps=[tr, prev_map.get('gfm%d' % ((tt * 2 + half) % 4))])
                    prev_map['gfm%d' % ((tt * 2 + half) % 4)] = ev
                    trev_prev = ev
                    trs.append(ev)
                prev_map['gtd'] = trs[0]
        if io.get("gfm_3d"):
            sto0 = S.dma("sync", o_gfm(m)[0:128],
                         gfm_sb[:, 0:T].rearrange("p (s t) -> p s t", s=2), deps=[trev_prev])
            sto1 = S.dma("sync", o_gfm(m)[128:256],
                         gfm_sb[:, T:2 * T].rearrange("p (s t) -> p s t", s=2), deps=[trev_prev])
        else:
            sto0 = S.dma("sync", o_gfm(m)[0:128, :], gfm_sb[:, 0:T], deps=[trev_prev])
            sto1 = S.dma("sync", o_gfm(m)[128:256, :], gfm_sb[:, T:2 * T], deps=[trev_prev])
        outs += [sto0, sto1]
        prev_map.update({'q': sto0, 'k': sto0, 'v': sto0, 'g': sto0, 'ret0': sto0,
                         'lndc': bfull, 'Bfull': sto0, 'dpow': sto0, 'trev': sto1,
                         'Bv0': sto0, 'Bv1': sto0, 'Bv2': sto0, 'Bv3': sto0})
        prev_map['trev'] = sto1

    return outs



GELU_A = 0.044715
FFN_ROUNDS = 4
GELU_2C = 2.0 * 0.7978845608028654
BIGC = float(2 ** 20)


def build_l0ffn():
    nc = bass.Bass("TRN2", target_bir_lowering=False, debug=False)
    io = {}
    io["x"] = nc.dram_tensor("x", [TOWN, D], F32, kind="ExternalInput").ap()
    io["gfm"] = nc.dram_tensor("gfm", [2048, TOWN], BF16, kind="ExternalInput").ap()
    io["wproj"] = nc.dram_tensor("wproj", [2048, 1024], F32, kind="ExternalInput").ap()
    io["wgr"] = nc.dram_tensor("wgr", [4, 1024, 4], F32, kind="ExternalInput").ap()
    io["bg"] = nc.dram_tensor("bg", [4, 4], F32, kind="ExternalInput").ap()
    io["w1"] = nc.dram_tensor("w1", [4, 1024, 4096], F32, kind="ExternalInput").ap()
    io["b1t"] = nc.dram_tensor("b1t", [128, 128], F32, kind="ExternalInput").ap()
    io["w2"] = nc.dram_tensor("w2", [4, 4096, 1024], F32, kind="ExternalInput").ap()
    io["b2"] = nc.dram_tensor("b2", [4, 1024], F32, kind="ExternalInput").ap()
    io["identb"] = nc.dram_tensor("identb", [128, 128], BF16, kind="ExternalInput").ap()
    io["ones128"] = nc.dram_tensor("ones128", [1, 128], F32, kind="ExternalInput").ap()
    io["x1"] = nc.dram_tensor("x1", [TOWN, D], F32, kind="ExternalOutput").ap()
    io["xn1fm"] = nc.dram_tensor("xn1fm", [1024, TOWN], BF16, kind="ExternalOutput").ap()
    io["w1b"] = nc.dram_tensor("w1b", [4, 1024, 4096], BF16, kind="Internal").ap()
    io["w2b"] = nc.dram_tensor("w2b", [4, 4096, 1024], BF16, kind="Internal").ap()
    io["xpa"] = nc.dram_tensor("xpa", [TOWN, D], F32, kind="Internal").ap()
    io["updsum"] = nc.dram_tensor("updsum", [TOWN, D], F32, kind="Internal").ap()
    es = contextlib.ExitStack()
    S = Sched()
    outs = emit_l0ffn(nc, S, es, io, [])
    S.emit(nc, es, final_waits=outs)
    es.close()
    return nc


def emit_l0ffn(nc, S, es, io, bar, pfx="f_", es_early=None, bar_early=None, es_ln_ext=None):
    if es_early is None:
        es_early = es
    if bar_early is None:
        bar_early = bar
    i_x = io["x"]; i_gfm = io["gfm"]; i_wp = io["wproj"]
    i_wgr = io["wgr"]; i_bg = io["bg"]; i_w1 = io["w1"]; i_b1 = io["b1t"]
    i_w2 = io["w2"]; i_b2 = io["b2"]; i_idb = io["identb"]; i_ones = io["ones128"]
    o_x1 = io["x1"]; o_xn1 = io["xn1fm"]
    s_w1b = io["w1b"]; s_w2b = io["w2b"]; s_xpa = io["xpa"]; s_upd = io["updsum"]
    sb = lambda nme, s, d=F32: es.enter_context(nc.sbuf_tensor(pfx + nme, s, d))
    psum = lambda nme, s, d=F32: es.enter_context(nc.psum_tensor(pfx + nme, s, d))
    outs = []

    standalone = es_early is es
    cur_fm = es_early.enter_context(nc.sbuf_tensor(pfx + "cur_fm", [128, 8 * TOWN], BF16))
    id_sb = es_early.enter_context(nc.sbuf_tensor(pfx + "id_sb", [128, 128], BF16))
    ps_ct = es_early.enter_context(nc.psum_tensor(pfx + "ps_ct", [128, 1024], BF16))
    es_ln = es_ln_ext if es_ln_ext is not None else contextlib.ExitStack()
    sbE = lambda nme, s, d=F32: es_ln.enter_context(nc.sbuf_tensor(pfx + nme, s, d))
    ld_id = S.dma("sync", id_sb[:], i_idb, deps=bar_early)
    es2 = contextlib.ExitStack()
    sb2 = lambda nme, s, d=F32: es2.enter_context(nc.sbuf_tensor(pfx + nme, s, d))

    wc = sbE("wc", [128, 4096], BF16)
    cast_done = {}
    prev_c = [None, None]
    ci = 0
    for e in range(4):
        toks = []
        for hc in range(8):
            l1 = S.dma("gpsimd", wc[:], i_w1[e, hc * 128:(hc + 1) * 128, :],
                       deps=[prev_c[0]] + bar_early)
            st = S.dma("gpsimd", s_w1b[e, hc * 128:(hc + 1) * 128, :], wc[:],
                       deps=[l1])
            prev_c[0] = st
            toks.append(st)
        cast_done[('w1', e)] = toks[-1]
        toks = []
        for fc8 in range(8):
            l1 = S.dma("gpsimd", wc[:].rearrange("p (s d) -> p s d", d=1024),
                       i_w2[e, fc8 * 512:(fc8 + 1) * 512, :].rearrange("(s p) d -> p s d", p=128),
                       deps=[prev_c[0]] + bar_early)
            st = S.dma("gpsimd", s_w2b[e, fc8 * 512:(fc8 + 1) * 512, :].rearrange("(s p) d -> p s d", p=128),
                       wc[:].rearrange("p (s d) -> p s d", d=1024),
                       deps=[l1])
            prev_c[0] = st
            toks.append(st)
        cast_done[('w2', e)] = toks[-1]

    xt_sb = [sbE("xt_sb0", [128, 1024])]
    xnt_sb = sbE("xnt_sb", [128, 1024])
    xnb_sb = sbE("xnb_sb", [128, 1024], BF16)
    x2p = sbE("x2p", [128, 1024])
    s1p = sbE("s1p", [128, 1]); s2p = sbE("s2p", [128, 1]); mup = sbE("mup", [128, 1])
    m2p = sbE("m2p", [128, 1]); varp = sbE("varp", [128, 1]); sqp = sbE("sqp", [128, 1]); rsp = sbE("rsp", [128, 1])
    xpa_st = x2p
    prev_p = {}
    xpa_stos = []
    cur_init = []
    for i in range(NT):
        b = 0
        lx = S.dma("sync", xt_sb[b][:], i_x[i * 128:(i + 1) * 128, :], deps=prev_p.get(('x', b), []) + bar_early)
        c1 = S.add("scalar", lambda s, b=b: s.activation(out=x2p[:], in_=xt_sb[b][:], func=AF.Square),
                   deps=[lx, prev_p.get('x2p')])
        c2 = S.add("vector", lambda v, b=b: v.reduce_sum(out=s1p[:], in_=xt_sb[b][:], axis=AX.X),
                   deps=[lx, prev_p.get('s1p')])
        c3 = S.add("vector", lambda v: v.reduce_sum(out=s2p[:], in_=x2p[:], axis=AX.X), deps=[c1, c2])
        c4 = S.add("vector", lambda v: v.tensor_scalar(out=mup[:], in0=s1p[:], scalar1=1.0 / 1024, scalar2=None, op0=ALU.mult), deps=[c3])
        c5 = S.add("vector", lambda v: v.tensor_tensor(out=m2p[:], in0=mup[:], in1=mup[:], op=ALU.mult), deps=[c4])
        c6 = S.add("vector", lambda v: v.tensor_scalar(out=varp[:], in0=s2p[:], scalar1=1.0 / 1024, scalar2=1e-5, op0=ALU.mult, op1=ALU.add), deps=[c5])
        c7 = S.add("vector", lambda v: v.tensor_sub(out=varp[:], in0=varp[:], in1=m2p[:]), deps=[c6])
        c8 = S.add("scalar", lambda s: s.activation(out=sqp[:], in_=varp[:], func=AF.Sqrt), deps=[c7, prev_p.get('sqp')])
        c9 = S.add("vector", lambda v: v.reciprocal(out=rsp[:], in_=sqp[:]), deps=[c8])
        c10 = S.add("vector", lambda v, b=b: v.tensor_scalar(out=xnt_sb[:], in0=xt_sb[b][:],
                    scalar1=mup[:], scalar2=rsp[:], op0=ALU.subtract, op1=ALU.mult),
                    deps=[c9, prev_p.get('xnt')])
        prev_p['s1p'] = c4; prev_p['sqp'] = c9
        cb10 = S.add("vector", lambda v: v.tensor_copy(out=xnb_sb[:], in_=xnt_sb[:]), deps=[c10, prev_p.get('xnb')])
        a0 = S.add("vector", lambda v, b=b: v.tensor_tensor(out=xpa_st[:], in0=xnt_sb[:], in1=xt_sb[b][:],
                   op=ALU.add), deps=[c10, prev_p.get('xpa')])
        prev_p['xnt'] = a0
        sto = S.dma("sync", s_xpa[i * 128:(i + 1) * 128, :], xpa_st[:], deps=[a0])
        prev_p['xpa'] = sto
        prev_p['x2p'] = sto
        prev_p[('x', b)] = [sto, c1]
        xpa_stos.append(sto)
        for c in range(8):
            tr = S.add("tensor", lambda t, c=c: t.transpose(
                out=ps_ct[:, 0:128], in_=xnb_sb[:, c * 128:(c + 1) * 128], identity=id_sb[:]),
                deps=[cb10, ld_id, prev_p.get('trx')])
            cc = S.add("vector", lambda v, c=c, i=i: v.tensor_copy(
                out=cur_fm[:, c * TOWN + i * 128: c * TOWN + (i + 1) * 128], in_=ps_ct[:, 0:128]),
                deps=[tr])
            prev_p['trx'] = cc
            cur_init.append(cc)
        prev_p['xnb'] = prev_p['trx']

    es2b = contextlib.ExitStack()
    sb2b = lambda nme, s, d=F32: es2b.enter_context(nc.sbuf_tensor(pfx + nme, s, d))
    ps2b = lambda nme, s, d=F32: es2b.enter_context(nc.psum_tensor(pfx + nme, s, d))
    wp_sb = sb2b("wp_sb", [128, 16 * 1024], BF16)
    stg_p = [sb2b(f"stg_p{i}", [128, 512]) for i in range(2)]
    ps_pj = [ps2b(f"ps_pj{i}", [128, 512]) for i in range(2)]
    ld_wp = [S.dma("gpsimd", wp_sb[:, vc * 1024:(vc + 1) * 1024], i_wp[vc * 128:(vc + 1) * 128, :], deps=bar)
             for vc in range(16)]
    if "goff" in io:
        gt4_sb = [sb2b(f"gt4_sb{i}", [128, 16 * 512], BF16) for i in range(2)]
        goff = io["goff"]
    else:
        gt_sb = [sb2b(f"gt_sb{i}", [128, 16 * 128], BF16) for i in range(2)]
    for i in range(NT):
        b = i % 2
        if "goff" in io:
            gb = (i // 4) % 2
            if i % 4 == 0:
                lgblk = S.dma("sync", gt4_sb[gb][:].rearrange("p (c t) -> p c t", t=512),
                              lambda i=i: i_gfm[:, bass.ds(goff[0] + (i // 4) * 512, 512)].rearrange("(c p) t -> p c t", p=128),
                              deps=[prev_p.get(('g4', gb)), io.get("goff_dep")] + bar)
                prev_p[('lg4', gb)] = lgblk
            lg = prev_p[('lg4', gb)]
            gt_cur = gt4_sb[gb]
            gt_off = lambda vc, i=i, gt_cur=gt_cur: gt_cur[:, vc * 512 + (i % 4) * 128: vc * 512 + (i % 4) * 128 + 128]
        else:
            lg = S.dma("sync", gt_sb[b][:].rearrange("p (c t) -> p c t", t=128),
                       i_gfm[:, i * 128:(i + 1) * 128].rearrange("(c p) t -> p c t", p=128),
                       deps=[prev_p.get(('g', b))] + bar)
            gt_b = gt_sb[b]
            gt_off = lambda vc, gt_b=gt_b: gt_b[:, vc * 128:(vc + 1) * 128]
        for dc in range(2):
            pb = (i * 2 + dc) % 2
            mms = [S.add("tensor", lambda t, vc=vc, dc=dc, pb=pb, gt_off=gt_off: t.matmul(
                out=ps_pj[pb][:], lhsT=gt_off(vc),
                rhs=wp_sb[:, vc * 1024 + dc * 512: vc * 1024 + (dc + 1) * 512],
                start=(vc == 0), stop=(vc == 15)),
                deps=[lg, ld_wp[vc]] + ([prev_p.get(('pp', pb))] if vc == 0 else []))
                for vc in range(16)]
            ev = S.add("vector", lambda v, pb=pb: v.tensor_copy(out=stg_p[pb][:], in_=ps_pj[pb][:]),
                       deps=[mms[-1], prev_p.get(('stg', pb))])
            acc = S.dma_acc("gpsimd", s_xpa[i * 128:(i + 1) * 128, dc * 512:(dc + 1) * 512],
                            stg_p[pb][:], ALU.add, deps=[ev, xpa_stos[i]])
            prev_p[('pp', pb)] = ev
            prev_p[('stg', pb)] = acc
            prev_p[('xpa_acc', i, dc)] = acc
        prev_p[('g', b)] = prev_p[('stg', 1)]
        if "goff" in io and i % 4 == 3:
            prev_p[('g4', (i // 4) % 2)] = prev_p[('stg', (i * 2 + 1) % 2)]
    xpa_stos = [prev_p[('xpa_acc', i, dc)] for i in range(NT) for dc in range(2)]
    xpa_accs = {i: [prev_p[('xpa_acc', i, 0)], prev_p[('xpa_acc', i, 1)]] for i in range(NT)}
    es2b.close()
    ln_end = [prev_p['xpa'], cur_init[-1], cast_done[('w1', 3)], cast_done[('w2', 3)]]
    es_ln.close()
    bar = bar + ln_end
    es2.close()

    ld_cur = None
    hmid = sb("hmid", [128, 32 * 1024], BF16)
    w2res = sb("w2res", [128, 32 * 1024], BF16)
    w1blk = [sb(f"w1blk{i}", [128, 8 * 512], BF16) for i in range(2)]
    upd = sb("upd", [128, 8 * 1024], BF16)
    wgr_sb = sb("wgr_sb", [128, 4 * 8 * 4], BF16)
    bgr_sb = sb("bgr_sb", [1, 16], BF16)
    b1_sb = sb("b1_sb", [128, 32])
    b2_sb = sb("b2_sb", [1, 1024], BF16)
    onesb_sb = sb("onesb_sb", [1, 128], BF16)
    xg = sb("xg", [128, 1024])
    tg2 = sb("tg2", [128, 1024])
    lgt = sb("lgt", [128, 4])
    io4 = sb("io4", [128, 4])
    io4b = sb("io4b", [128, 4])
    m1c = sb("m1c", [128, 1]); m2cg = sb("m2cg", [128, 1])
    eqt = sb("eqt", [128, 4]); cnd = sb("cnd", [128, 4])
    i1c = sb("i1c", [128, 1]); i2c = sb("i2c", [128, 1])
    oh1 = sb("oh1", [128, 4]); oh2 = sb("oh2", [128, 4])
    lg2 = sb("lg2", [128, 4])
    w1gc = sb("w1gc", [128, 1]); w2gc = sb("w2gc", [128, 1])
    cwt1 = sb("cwt1", [128, 4]); cwt2 = sb("cwt2", [128, 4])
    cw_all = sb("cw_all", [128, 16 * 4])
    us_sb = sb("us_sb", [128, 1024])
    ps_h = [psum(f"ps_h{i}", [128, 512]) for i in range(2)]
    ps_y = [psum(f"ps_y{i}", [128, 1024]) for i in range(2)]
    ps_g = psum("ps_g", [128, 512])

    ld_wgr = S.dma("gpsimd", wgr_sb[:].rearrange("p (r hc e) -> p r hc e", hc=8, e=4),
                   i_wgr.rearrange("r (hc p) e -> p r hc e", p=128), deps=bar)
    ld_bgr = S.dma("gpsimd", bgr_sb[:].rearrange("a (r e) -> a r e", e=4), i_bg.rearrange("(o r) e -> o r e", o=1),
                   deps=bar)

    cob = S.dma("gpsimd", onesb_sb[:], i_ones, deps=bar)
    iog1 = S.add("gpsimd", lambda g: g.iota(out=io4[:], pattern=[[1, 4]], base=0, channel_multiplier=0,
                                            allow_small_or_imprecise_dtypes=True), deps=bar)
    iog2 = S.add("gpsimd", lambda g: g.iota(out=io4b[:], pattern=[[1, 4]], base=int(BIGC), channel_multiplier=0,
                                            allow_small_or_imprecise_dtypes=True), deps=bar)

    prev_f = {'curdep': cur_init}
    upddeps = {}
    for r in range(FFN_ROUNDS):
        cw_deps = []
        for gt in range(NT):
            mmg = [S.add("tensor", lambda t, hc=hc, gt=gt, r=r: t.matmul(
                out=ps_g[:, 0:4], lhsT=cur_fm[:, hc * TOWN + gt * 128: hc * TOWN + (gt + 1) * 128],
                rhs=wgr_sb[:, r * 32 + hc * 4: r * 32 + (hc + 1) * 4],
                start=(hc == 0), stop=False),
                deps=[ld_wgr, ld_cur] + ([prev_f.get('psg')] if hc == 0 else []) + prev_f.get('curdep', []))
                for hc in range(8)]
            mgb = S.add("tensor", lambda t, r=r: t.matmul(
                out=ps_g[:, 0:4], lhsT=onesb_sb[:], rhs=bgr_sb[:, r * 4:(r + 1) * 4],
                start=False, stop=True), deps=[ld_bgr, cob])
            e0 = S.add("vector", lambda v: v.tensor_copy(out=lgt[:], in_=ps_g[:, 0:4]), deps=[mgb, prev_f.get('lgt')])
            prev_f['psg'] = e0
            e1 = S.add("vector", lambda v: v.reduce_max(out=m1c[:], in_=lgt[:], axis=AX.X), deps=[e0, prev_f.get('m1c')])
            e2 = S.add("vector", lambda v: v.tensor_scalar(out=eqt[:], in0=lgt[:], scalar1=m1c[:], scalar2=None, op0=ALU.is_ge), deps=[e1, prev_f.get('eqt')])
            e3 = S.add("vector", lambda v: v.scalar_tensor_tensor(out=cnd[:], in0=eqt[:], scalar=-BIGC, in1=io4b[:], op0=ALU.mult, op1=ALU.add), deps=[e2, iog2, prev_f.get('cnd')])
            e4 = S.add("vector", lambda v: v.tensor_reduce(out=i1c[:], in_=cnd[:], axis=AX.X, op=ALU.min), deps=[e3, prev_f.get('i1c')])
            e6 = S.add("vector", lambda v: v.tensor_scalar(out=oh1[:], in0=io4[:], scalar1=i1c[:], scalar2=None, op0=ALU.is_equal), deps=[e4, iog1, prev_f.get('oh1')])
            e7 = S.add("vector", lambda v: v.scalar_tensor_tensor(out=lg2[:], in0=oh1[:], scalar=-BIGC, in1=lgt[:], op0=ALU.mult, op1=ALU.add), deps=[e6, prev_f.get('lg2')])
            e8 = S.add("vector", lambda v: v.reduce_max(out=m2cg[:], in_=lg2[:], axis=AX.X), deps=[e7, prev_f.get('m2cg')])
            e9 = S.add("vector", lambda v: v.tensor_scalar(out=eqt[:], in0=lg2[:], scalar1=m2cg[:], scalar2=None, op0=ALU.is_ge), deps=[e8, e3])
            e10 = S.add("vector", lambda v: v.scalar_tensor_tensor(out=cnd[:], in0=eqt[:], scalar=-BIGC, in1=io4b[:], op0=ALU.mult, op1=ALU.add), deps=[e9, e4])
            e11 = S.add("vector", lambda v: v.tensor_reduce(out=i2c[:], in_=cnd[:], axis=AX.X, op=ALU.min), deps=[e10, prev_f.get('i2c')])
            e13 = S.add("vector", lambda v: v.tensor_scalar(out=oh2[:], in0=io4[:], scalar1=i2c[:], scalar2=None, op0=ALU.is_equal), deps=[e11, prev_f.get('oh2')])
            e14 = S.add("vector", lambda v: v.tensor_tensor(out=m2cg[:], in0=m1c[:], in1=m2cg[:], op=ALU.subtract), deps=[e8, e13, e9])
            e15 = S.add("scalar", lambda s: s.activation(out=w1gc[:], in_=m2cg[:], func=AF.Sigmoid), deps=[e14, prev_f.get('w1g')])
            e16 = S.add("vector", lambda v: v.tensor_scalar(out=w2gc[:], in0=w1gc[:], scalar1=-1.0, scalar2=1.0, op0=ALU.mult, op1=ALU.add), deps=[e15, prev_f.get('w2g')])
            e17 = S.add("vector", lambda v: v.tensor_scalar(out=cwt1[:], in0=oh1[:], scalar1=w1gc[:], scalar2=None, op0=ALU.mult), deps=[e16, e7, prev_f.get('cwt1')])
            e18 = S.add("vector", lambda v: v.tensor_scalar(out=cwt2[:], in0=oh2[:], scalar1=w2gc[:], scalar2=None, op0=ALU.mult), deps=[e16, e13, prev_f.get('cwt2')])
            e19 = S.add("vector", lambda v, gt=gt: v.tensor_tensor(out=cw_all[:, gt * 4:(gt + 1) * 4], in0=cwt1[:], in1=cwt2[:], op=ALU.add), deps=[e17, e18, prev_f.get('cw%d' % gt)])
            prev_f['oh1'] = e17; prev_f['oh2'] = e18; prev_f['w1g'] = e16; prev_f['w2g'] = e17
            prev_f['m1c'] = e17; prev_f['m2cg'] = e15; prev_f['eqt'] = e10; prev_f['cnd'] = e11
            prev_f['i1c'] = e6; prev_f['i2c'] = e13; prev_f['lg2'] = e9
            prev_f['lgt'] = e9; prev_f['cwt1'] = e19; prev_f['cwt2'] = e19
            prev_f['cw%d' % gt] = e19
            cw_deps.append(e19)
        for tblock in range(2):
            upd_dep = {}
            for e in range(4):
                ld_w2r = S.dma("scalar", w2res[:].rearrange("p (fc d) -> p fc d", d=1024),
                               s_w2b[e].rearrange("(fc p) d -> p fc d", p=128),
                               deps=[cast_done[('w2', e)]] + prev_f.get('w2res', []))
                ld_b2 = S.dma("gpsimd", b2_sb[:], i_b2[e:e + 1, :], deps=prev_f.get('b2use', []) + bar)
                ld_b1 = S.dma("sync", b1_sb[:], i_b1[:, e * 32:(e + 1) * 32],
                              deps=[prev_f.get(('psh', 0)), prev_f.get(('psh', 1))] + bar)
                hm_last = {}
                for fb in range(8):
                    wb_ = fb % 2
                    lw1 = S.dma("sync", w1blk[wb_][:].rearrange("p (hc f) -> p hc f", f=512),
                                s_w1b[e][:, fb * 512:(fb + 1) * 512].rearrange("(hc p) f -> p hc f", p=128),
                                deps=[cast_done[('w1', e)]] + prev_f.get(('w1b', wb_), []))
                    w1_readers = []
                    for fc4 in range(4):
                        fc = fb * 4 + fc4
                        b1col = b1_sb[:, fc: fc + 1]
                        g1s = []
                        for tc in range(2):
                            pb = tc
                            tcol = tblock * 1024 + tc * 512
                            mms = [S.add("tensor", lambda t, hc=hc, fc4=fc4, tcol=tcol, pb=pb, wb_=wb_: t.matmul(
                                out=ps_h[pb][:], lhsT=w1blk[wb_][:, hc * 512 + fc4 * 128: hc * 512 + (fc4 + 1) * 128],
                                rhs=cur_fm[:, hc * TOWN + tcol: hc * TOWN + tcol + 512],
                                start=(hc == 0), stop=(hc == 7)),
                                deps=[lw1, ld_cur] + ([prev_f.get(('psh', pb))] if hc == 0 else []) + prev_f.get('curdep', []))
                                for hc in range(8)]
                            w1_readers.append(mms[-1])
                            g1 = S.add("scalar", lambda s, pb=pb, tc=tc, b1col=b1col: s.activation(
                                out=xg[:, tc * 512:(tc + 1) * 512], in_=ps_h[pb][:], func=AF.Identity, bias=b1col),
                                deps=[mms[-1], ld_b1] + ([prev_f.get('xg')] if tc == 0 else []))
                            prev_f[('psh', pb)] = g1
                            g1s.append(g1)
                        g2 = S.add("scalar", lambda s: s.activation(out=tg2[:], in_=xg[:], func=AF.Square), deps=g1s + [prev_f.get('tg2')])
                        g3 = S.add("vector", lambda v: v.tensor_scalar(out=tg2[:], in0=tg2[:], scalar1=GELU_A, scalar2=1.0, op0=ALU.mult, op1=ALU.add), deps=[g2])
                        g4 = S.add("vector", lambda v: v.tensor_tensor(out=tg2[:], in0=tg2[:], in1=xg[:], op=ALU.mult), deps=[g3])
                        g5 = S.add("scalar", lambda s: s.activation(out=tg2[:], in_=tg2[:], func=AF.Sigmoid, scale=GELU_2C), deps=[g4])
                        g6 = S.add("vector", lambda v, fc=fc: v.tensor_tensor(
                            out=hmid[:, fc * 1024:(fc + 1) * 1024],
                            in0=xg[:], in1=tg2[:], op=ALU.mult), deps=[g5, prev_f.get(('hm', fc, 0)), prev_f.get(('hm', fc, 1))])
                        prev_f['xg'] = g6; prev_f['tg2'] = g6
                        prev_f[('hm', fc, 0)] = g6; prev_f[('hm', fc, 1)] = g6
                        hm_last[(fc, 0)] = g6; hm_last[(fc, 1)] = g6
                    prev_f[('w1b', wb_)] = w1_readers
                for tg in range(2):
                    for tt4 in range(4):
                        lt = tg * 4 + tt4
                        gt = tblock * 8 + lt
                        pi = lt % 2
                        pyt = ps_y[pi]
                        mybs = []
                        for dc in range(2):
                            mms = [S.add("tensor", lambda t, fc=fc, lt=lt, dc=dc, pyt=pyt: t.matmul(
                                out=pyt[:, dc * 512:(dc + 1) * 512],
                                lhsT=hmid[:, fc * 1024 + lt * 128: fc * 1024 + (lt + 1) * 128],
                                rhs=w2res[:, fc * 1024 + dc * 512: fc * 1024 + (dc + 1) * 512],
                                start=(fc == 0), stop=False),
                                deps=([ld_w2r, hm_last[(31, lt // 4)]] +
                                      ([prev_f.get(('psy', pi))] if dc == 0 else []) if fc == 0 else []))
                                for fc in range(32)]
                            myb = S.add("tensor", lambda t, dc=dc, pyt=pyt: t.matmul(
                                out=pyt[:, dc * 512:(dc + 1) * 512], lhsT=onesb_sb[:],
                                rhs=b2_sb[:, dc * 512:(dc + 1) * 512],
                                start=False, stop=True), deps=[ld_b2, cob])
                            mybs.append(myb)
                        u0 = lt * 2
                        if e == 0:
                            ev = S.add("vector", lambda v, pyt=pyt, lt=lt, gt=gt: v.tensor_scalar(
                                out=upd[:, lt * 1024:(lt + 1) * 1024], in0=pyt[:],
                                scalar1=cw_all[:, gt * 4: gt * 4 + 1], scalar2=None, op0=ALU.mult),
                                deps=mybs + [cw_deps[gt], upd_dep.get(u0), upd_dep.get(u0 + 1)]
                                     + upddeps.get(u0, []) + upddeps.get(u0 + 1, []))
                        else:
                            ev = S.add("vector", lambda v, pyt=pyt, lt=lt, gt=gt, e=e: v.scalar_tensor_tensor(
                                out=upd[:, lt * 1024:(lt + 1) * 1024], in0=pyt[:],
                                scalar=cw_all[:, gt * 4 + e: gt * 4 + e + 1],
                                in1=upd[:, lt * 1024:(lt + 1) * 1024], op0=ALU.mult, op1=ALU.add),
                                deps=mybs + [cw_deps[gt], upd_dep.get(u0), upd_dep.get(u0 + 1)])
                        upd_dep[u0] = ev; upd_dep[u0 + 1] = ev
                        upddeps[u0] = []; upddeps[u0 + 1] = []
                        prev_f[('psy', pi)] = ev
                prev_f['w2res'] = [upd_dep[15]]
                prev_f['b2use'] = [upd_dep[15]]
            curdeps = []
            for lt in range(8):
                gt = tblock * 8 + lt
                if r > 0:
                    lus = S.dma("sync", us_sb[:], s_upd[gt * 128:(gt + 1) * 128, :],
                                deps=prev_f.get('usdep', []))
                    a1 = S.add("vector", lambda v, lt=lt: v.tensor_tensor(
                        out=us_sb[:], in0=us_sb[:],
                        in1=upd[:, lt * 1024:(lt + 1) * 1024], op=ALU.add),
                        deps=[lus, upd_dep[lt * 2], upd_dep[lt * 2 + 1]])
                else:
                    a1 = S.add("vector", lambda v, lt=lt: v.tensor_copy(
                        out=us_sb[:], in_=upd[:, lt * 1024:(lt + 1) * 1024]),
                        deps=[upd_dep[lt * 2], upd_dep[lt * 2 + 1]] + prev_f.get('usdep', []))
                upddeps.setdefault(lt * 2, []).append(a1)
                upddeps.setdefault(lt * 2 + 1, []).append(a1)
                if r < FFN_ROUNDS - 1:
                    sus = S.dma("sync", s_upd[gt * 128:(gt + 1) * 128, :], us_sb[:], deps=[a1])
                    prev_f['usdep'] = [sus]
                    for c in range(8):
                        tr = S.add("tensor", lambda t, lt=lt, c=c: t.transpose(
                            out=ps_ct[:, 0:128], in_=upd[:, lt * 1024 + c * 128: lt * 1024 + (c + 1) * 128],
                            identity=id_sb[:]),
                            deps=[upd_dep[lt * 2 + c // 4], ld_id] + prev_f.get('ctdep', []))
                        ad = S.add("vector", lambda v, c=c, gt=gt: v.tensor_tensor(
                            out=cur_fm[:, c * TOWN + gt * 128: c * TOWN + (gt + 1) * 128],
                            in0=cur_fm[:, c * TOWN + gt * 128: c * TOWN + (gt + 1) * 128],
                            in1=ps_ct[:, 0:128], op=ALU.add),
                            deps=[tr])
                        prev_f['ctdep'] = [ad]
                        upddeps.setdefault(lt * 2 + c // 4, []).append(tr)
                        curdeps.append(ad)
                else:
                    a2 = S.dma_acc("gpsimd", us_sb[:], s_xpa[gt * 128:(gt + 1) * 128, :], ALU.add,
                                   deps=xpa_accs[gt] + [a1])
                    prev_f['usdep'] = [a2]
                    sx1 = S.dma("sync", o_x1[gt * 128:(gt + 1) * 128, :], us_sb[:], deps=[a2])
                    outs.append(sx1)
                    l1 = S.add("scalar", lambda s: s.activation(out=xg[:, 0:512], in_=us_sb[:, 0:512], func=AF.Square), deps=[a2, prev_f.get('xg'), prev_f.get('lnx2a')])
                    l1b = S.add("scalar", lambda s: s.activation(out=tg2[:, 0:512], in_=us_sb[:, 512:1024], func=AF.Square), deps=[a2, prev_f.get('tg2'), l1, prev_f.get('lnx2b')])
                    l2 = S.add("vector", lambda v: v.reduce_sum(out=m1c[:], in_=us_sb[:], axis=AX.X), deps=[a2, prev_f.get('m1c')])
                    l3 = S.add("vector", lambda v: v.reduce_sum(out=m2cg[:], in_=xg[:, 0:512], axis=AX.X), deps=[l1, l2, prev_f.get('m2cg')])
                    l3b = S.add("vector", lambda v: v.reduce_sum(out=i1c[:], in_=tg2[:, 0:512], axis=AX.X), deps=[l1b, l3, prev_f.get('i1c')])
                    l4 = S.add("vector", lambda v: v.tensor_tensor(out=m2cg[:], in0=m2cg[:], in1=i1c[:], op=ALU.add), deps=[l3b])
                    l5 = S.add("vector", lambda v: v.tensor_scalar(out=m1c[:], in0=m1c[:], scalar1=1.0 / 1024, scalar2=None, op0=ALU.mult), deps=[l4, l2])
                    l6 = S.add("vector", lambda v: v.tensor_tensor(out=i1c[:], in0=m1c[:], in1=m1c[:], op=ALU.mult), deps=[l5])
                    l7 = S.add("vector", lambda v: v.tensor_scalar(out=m2cg[:], in0=m2cg[:], scalar1=1.0 / 1024, scalar2=None, op0=ALU.mult), deps=[l6])
                    l8 = S.add("vector", lambda v: v.tensor_tensor(out=m2cg[:], in0=m2cg[:], in1=i1c[:], op=ALU.subtract), deps=[l7])
                    l9 = S.add("vector", lambda v: v.tensor_scalar(out=m2cg[:], in0=m2cg[:], scalar1=1e-5, scalar2=None, op0=ALU.add), deps=[l8])
                    l10 = S.add("scalar", lambda s: s.activation(out=i2c[:], in_=m2cg[:], func=AF.Sqrt), deps=[l9, prev_f.get('i2c')])
                    l11 = S.add("vector", lambda v: v.reciprocal(out=i2c[:], in_=i2c[:]), deps=[l10])
                    l12 = S.add("vector", lambda v, lt=lt: v.tensor_scalar(
                                out=upd[:, lt * 1024:(lt + 1) * 1024], in0=us_sb[:],
                                scalar1=m1c[:], scalar2=i2c[:], op0=ALU.subtract, op1=ALU.mult),
                                deps=[l11, prev_f.get('xn1t'), a1])
                    prev_f['lnx2a'] = l3; prev_f['lnx2b'] = l3b; prev_f['m1c'] = l12; prev_f['m2cg'] = l8
                    prev_f['i1c'] = l8; prev_f['i2c'] = l12; prev_f['xg'] = l3; prev_f['tg2'] = l3b
                    prev_f['x1dep'] = [l12, sx1]
                    for c in range(8):
                        tr = S.add("tensor", lambda t, c=c, lt=lt: t.transpose(
                            out=ps_ct[:, 0:128], in_=upd[:, lt * 1024 + c * 128: lt * 1024 + (c + 1) * 128],
                            identity=id_sb[:]),
                            deps=[l12, ld_id] + prev_f.get('ctdep', []))
                        stg = S.add("vector", lambda v, c=c: v.tensor_copy(
                            out=upd[:, c * 128:(c + 1) * 128], in_=ps_ct[:, 0:128]),
                            deps=[tr] + upddeps.get(c // 4, []) + prev_f.get(('xn1s', c), []))
                        sto = S.dma("sync", o_xn1[c * 128:(c + 1) * 128, gt * 128:(gt + 1) * 128],
                                    upd[:, c * 128:(c + 1) * 128], deps=[stg])
                        outs.append(sto)
                        prev_f['ctdep'] = [stg]
                        prev_f[('xn1s', c)] = [sto]
                        upddeps.setdefault(c // 4, []).append(sto)
                    prev_f['xn1t'] = prev_f['ctdep'][0]
            if r < FFN_ROUNDS - 1:
                prev_f['curdep'] = curdeps

    return outs



RMS_EPS = 1.1920929e-07


def build_l1attn():
    nc = bass.Bass("TRN2", target_bir_lowering=False, debug=False)
    io = {}
    for nm, shp, dt in (("xn1fm", [1024, T], BF16), ("wq", [1024, 512], F32), ("wk", [1024, 512], F32),
                        ("wv", [1024, 512], F32), ("bqc", [128, 4], F32), ("bkc", [128, 4], F32),
                        ("bvrow", [1, 512], F32), ("fwq", [1024, 512], F32), ("fwk", [1024, 512], F32),
                        ("fwv", [1024, 512], F32), ("fbqc", [128, 4], F32), ("fbkc", [128, 4], F32),
                        ("fbvrow", [1, 512], F32), ("causal", [4, 128, 512], BF16),
                        ("ones128", [1, 128], F32), ("onescol", [128, 1], F32), ("identf", [128, 128], F32)):
        io[nm] = nc.dram_tensor(nm, shp, dt, kind="ExternalInput").ap()
    z1_ = nc.dram_tensor("z1", [8, 64, T], BF16, kind="ExternalOutput").ap()
    z2_ = nc.dram_tensor("z2", [8, 64, T], BF16, kind="ExternalOutput").ap()
    io["z1"] = lambda h: z1_[h]
    io["z2"] = lambda h: z2_[h]
    es = contextlib.ExitStack()
    S = Sched()
    outs = emit_l1attn(nc, S, es, io, [])
    S.emit(nc, es, final_waits=outs)
    es.close()
    return nc


def emit_l1attn(nc, S, es, io, bar, pfx="a_"):
    i_xn = io.get("xn1fm")
    i_wq = io["wq"]; i_wk = io["wk"]; i_wv = io["wv"]
    i_bq = io["bqc"]; i_bk = io["bkc"]; i_bv = io["bvrow"]
    i_fwq = io["fwq"]; i_fwk = io["fwk"]; i_fwv = io["fwv"]
    i_fbq = io["fbqc"]; i_fbk = io["fbkc"]; i_fbv = io["fbvrow"]
    i_caus = io["causal"]; i_ones = io["ones128"]; i_onesc = io["onescol"]; i_idf = io["identf"]
    if "zslice" in io:
        zsl = io["zslice"]
        o_z1 = lambda h, tc: zsl(0, h, tc)
        o_z2 = lambda h, tc: zsl(1, h, tc)
    else:
        _z1 = io["z1"]; _z2 = io["z2"]
        o_z1 = lambda h, tc: _z1(h)[:, tc * 512:(tc + 1) * 512]
        o_z2 = lambda h, tc: _z2(h)[:, tc * 512:(tc + 1) * 512]
    sb = lambda nme, s, d=F32: es.enter_context(nc.sbuf_tensor(pfx + nme, s, d))
    psum = lambda nme, s, d=F32: es.enter_context(nc.psum_tensor(pfx + nme, s, d))
    outs = []
    prev = {}

    xn_sb = sb("xn_sb", [128, 8 * T], BF16)
    s_rep = sb("s_rep", [128, T], BF16)
    s_tok = sb("s_tok", [128, 32])
    s_row = sb("s_row", [1, T])
    wsbs = {}
    for nm in ("wq", "wk", "wv"):
        wsbs[nm] = sb(nm + "_sb", [128, 8 * 512], BF16)
    wsbs["fwq"] = wsbs["wq"]; wsbs["fwk"] = wsbs["wk"]; wsbs["fwv"] = wsbs["wv"]
    bsbs = {}
    for nm, src_, shp in (("bqc", i_bq, [128, 4]), ("bkc", i_bk, [128, 4]), ("bvrow", i_bv, [1, 512]),
                          ("fbqc", i_fbq, [128, 4]), ("fbkc", i_fbk, [128, 4]), ("fbvrow", i_fbv, [1, 512])):
        bsbs[nm] = sb(nm + "_sb", shp)
    caus_sb = sb("caus_sb", [128, 4 * 512], BF16)
    ones_sb = sb("ones_sb", [1, 128])
    onesc_sb = sb("onesc_sb", [128, 1])
    idf_sb = sb("idf_sb", [128, 128])
    q_h = sb("q_h", [64, T], BF16)
    k_h = sb("k_h", [64, T], BF16)
    v_h = sb("v_h", [128, 32 * 64], BF16)
    vaug = sb("vaug", [128, 32 * 65], BF16)
    vp = [sb(f"vp{i}", [128, 64], BF16) for i in range(2)]
    brep = sb("brep", [128, 64])
    stage = sb("stage", [128, T])
    P_sb = [sb(f"P_sb{i}", [128, T], BF16) for i in range(2)]
    Pq = [sb(f"Pq{i}", [128, 512], BF16) for i in range(2)]
    qe1 = sb("qe1", [64, 512])
    mcol = sb("mcol", [128, 1]); tmpc = sb("tmpc", [128, 1]); negm = sb("negm", [128, 1])
    Dcol = sb("Dcol", [128, 1]); rD = sb("rD", [128, 1])
    z_sb = sb("z_sb", [65, T])
    z2_sb = z_sb
    zos = [sb("zos0", [64, 512], BF16)]
    ps_s = [psum(f"ps_s{i}", [128, 512]) for i in range(2)]
    ps_z = [psum(f"ps_z{i}", [128, 512]) for i in range(2)]
    ps_q = [psum(f"ps_q{i}", [128, 512]) for i in range(2)]
    ps_r = psum("ps_r", [128, 512])

    lds = {}
    for nm, src_ in (("wq", i_wq), ("wk", i_wk), ("wv", i_wv)):
        lds[nm] = S.dma("gpsimd", wsbs[nm][:].rearrange("p (hc c) -> p hc c", c=512),
                        src_.rearrange("(hc p) c -> p hc c", p=128), deps=bar)
    fw_srcs = {"fwq": i_fwq, "fwk": i_fwk, "fwv": i_fwv}
    for nm, src_ in (("bqc", i_bq), ("bkc", i_bk), ("bvrow", i_bv), ("fbqc", i_fbq), ("fbkc", i_fbk),
                     ("fbvrow", i_fbv)):
        lds[nm] = S.dma("sync", bsbs[nm][:], src_, deps=bar)
    lds["ones"] = S.dma("sync", ones_sb[:], i_ones, deps=bar)
    lds["onesc"] = S.dma("sync", onesc_sb[:], i_onesc, deps=bar)
    if "xn1fm_parts" in io:
        parts = io["xn1fm_parts"]
        ld_xns = []
        for ap_, col0 in parts:
            w_ = TOWN
            ld_xns.append(S.dma("sync",
                xn_sb[:].rearrange("p (c t) -> p c t", t=T)[:, :, col0:col0 + w_],
                ap_.rearrange("(c p) t -> p c t", p=128), deps=bar + ld_xns[-1:]))
        ld_xn = ld_xns[-1]
    else:
        ld_xn = S.dma("sync", xn_sb[:].rearrange("p (c t) -> p c t", t=T),
                      i_xn.rearrange("(c p) t -> p c t", p=128), deps=bar)
    ld_caus = [S.dma("sync", caus_sb[:, v * 512:(v + 1) * 512], i_caus[v], deps=bar) for v in range(4)]

    srep = []
    for tc in range(8):
        mml = None
        for hc in range(8):
            sq = S.add("scalar", lambda s, hc=hc, tc=tc: s.activation(
                out=stage[:, 0:512], in_=xn_sb[:, hc * T + tc * 512: hc * T + (tc + 1) * 512], func=AF.Square),
                deps=[ld_xn, prev.get('sq')])
            mml = S.add("tensor", lambda t, hc=hc: t.matmul(
                out=ps_r[0:1, 0:512], lhsT=onesc_sb[:], rhs=stage[:, 0:512], start=(hc == 0), stop=(hc == 7)),
                deps=[sq, lds["onesc"]] + ([prev.get('psr')] if hc == 0 else []))
            prev['sq'] = mml
        ev = S.add("vector", lambda v, tc=tc: v.tensor_copy(
            out=s_row[0:1, tc * 512:(tc + 1) * 512], in_=ps_r[0:1, 0:512]),
            deps=[mml, prev.get('srow')])
        prev['psr'] = ev; prev['srow'] = ev
        srep.append(ev)
    r1 = S.add("vector", lambda v: v.tensor_scalar(out=s_row[:], in0=s_row[:], scalar1=1.0 / 1024,
               scalar2=RMS_EPS, op0=ALU.mult, op1=ALU.add), deps=srep)
    r2 = S.add("scalar", lambda s: s.activation(out=s_row[:], in_=s_row[:], func=AF.Sqrt), deps=[r1])
    r3 = S.add("vector", lambda v: v.reciprocal(out=s_row[:], in_=s_row[:]), deps=[r2])
    rr = []
    for tc in range(8):
        mm = S.add("tensor", lambda t, tc=tc: t.matmul(
            out=ps_r[:, 0:512], lhsT=ones_sb[:], rhs=s_row[0:1, tc * 512:(tc + 1) * 512],
            start=True, stop=True), deps=[r3, lds["ones"], prev.get('psr')])
        ev = S.add("vector", lambda v, tc=tc: v.tensor_copy(out=s_rep[:, tc * 512:(tc + 1) * 512],
                   in_=ps_r[:, 0:512]), deps=[mm, prev.get('srep2')])
        prev['psr'] = ev; prev['srep2'] = ev
        rr.append(ev)
    mkid = S.dma("sync", idf_sb[:], i_idf, deps=bar)
    for _qt in range(8):
        prev[('stg', _qt)] = prev['sq']
    srep32 = sb("srep32", [128, 128])
    stok_deps = []
    for tt in range(32):
        cp32 = S.add("vector", lambda v, tt=tt: v.tensor_copy(out=srep32[:], in_=s_rep[:, tt * 128:(tt + 1) * 128]),
                     deps=[rr[tt // 4], prev.get('srep32')])
        tr = S.add("tensor", lambda t, tt=tt: t.transpose(
            out=ps_r[:, 0:128], in_=srep32[:], identity=idf_sb[:]),
            deps=[cp32, mkid, prev.get('psr')])
        ev = S.add("vector", lambda v, tt=tt: v.tensor_copy(out=s_tok[:, tt:tt + 1], in_=ps_r[:, 0:1]),
                   deps=[tr, prev.get('stok')])
        prev['srep32'] = tr
        prev['psr'] = ev; prev['stok'] = ev
        stok_deps.append(ev)

    BIGM = float(2 ** 20)

    def proj_fm(dst, wname, bname, hl, use_scale, name):
        dd = []
        W = wsbs[wname]
        bcol = bsbs[bname][(hl % 2) * 64:(hl % 2) * 64 + 64, hl // 2: hl // 2 + 1]
        for tc in range(8):
            pb = tc % 2
            mms = None
            for hc in range(8):
                mms = S.add("tensor", lambda t, hc=hc, tc=tc, pb=pb, hl=hl, W=W: t.matmul(
                    out=ps_q[pb][0:64, 0:512],
                    lhsT=W[:, hc * 512 + hl * 64: hc * 512 + (hl + 1) * 64],
                    rhs=xn_sb[:, hc * T + tc * 512: hc * T + (tc + 1) * 512],
                    start=(hc == 0), stop=(hc == 7)),
                    deps=[lds[wname], ld_xn] + ([prev.get(('psq', pb))] if hc == 0 else []))
            if use_scale:
                e1 = S.add("vector", lambda v, tc=tc, pb=pb: v.tensor_tensor(
                    out=qe1[:], in0=ps_q[pb][0:64, 0:512], in1=s_rep[0:64, tc * 512:(tc + 1) * 512],
                    op=ALU.mult), deps=[mms, rr[tc], prev.get('qe1')])
                e2 = S.add("vector", lambda v, tc=tc, dst=dst, bcol=bcol: v.tensor_scalar(
                    out=dst[:, tc * 512:(tc + 1) * 512], in0=qe1[:], scalar1=bcol, scalar2=None,
                    op0=ALU.add), deps=[e1, lds[bname], prev.get((name, tc))])
                prev['qe1'] = e2
                prev[('psq', pb)] = e1
            else:
                e2 = S.add("vector", lambda v, tc=tc, dst=dst, pb=pb, bcol=bcol: v.tensor_scalar(
                    out=dst[:, tc * 512:(tc + 1) * 512], in0=ps_q[pb][0:64, 0:512], scalar1=bcol,
                    scalar2=None, op0=ALU.add), deps=[mms, lds[bname], prev.get((name, tc))])
                prev[('psq', pb)] = e2
            prev[(name, tc)] = e2
            dd.append(e2)
        return dd

    def proj_tok(wname, bname, hl, use_scale, name):
        mmb = S.add("tensor", lambda t, hl=hl: t.matmul(
            out=ps_r[:, 0:64], lhsT=ones_sb[:], rhs=bsbs[bname][0:1, hl * 64:(hl + 1) * 64],
            start=True, stop=True), deps=[lds[bname], lds["ones"], prev.get('psr')])
        evb = S.add("vector", lambda v: v.tensor_copy(out=brep[:], in_=ps_r[:, 0:64]),
                    deps=[mmb, prev.get('brep')])
        prev['psr'] = evb; prev['brep'] = evb
        dd = []
        W = wsbs[wname]
        for tt in range(32):
            pb = tt % 2
            mms = None
            for hc in range(8):
                mms = S.add("tensor", lambda t, hc=hc, tt=tt, pb=pb, hl=hl, W=W: t.matmul(
                    out=ps_q[pb][:, 0:64],
                    lhsT=xn_sb[:, hc * T + tt * 128: hc * T + (tt + 1) * 128],
                    rhs=W[:, hc * 512 + hl * 64: hc * 512 + (hl + 1) * 64],
                    start=(hc == 0), stop=(hc == 7)),
                    deps=[lds[wname], ld_xn] + ([prev.get(('psq', pb))] if hc == 0 else []))
            sc = s_tok[:, tt:tt + 1] if use_scale else 1.0
            e2 = S.add("vector", lambda v, tt=tt, pb=pb, sc=sc: v.scalar_tensor_tensor(
                out=v_h[:, tt * 64:(tt + 1) * 64], in0=ps_q[pb][:, 0:64],
                scalar=sc, in1=brep[:], op0=ALU.mult, op1=ALU.add),
                deps=[mms, evb, prev.get((name, tt % 4))] + (stok_deps if use_scale else []))
            prev[('psq', pb)] = e2
            prev[(name, tt % 4)] = e2
            dd.append(e2)
        return dd

    for hl in range(8):
        qd = proj_fm(q_h, "wq", "bqc", hl, True, 'q')
        kd = proj_fm(k_h, "wk", "bkc", hl, True, 'k')
        vd = proj_tok("wv", "bvrow", hl, True, 'v')
        mz = S.add("vector", lambda v: v.memset(z_sb[:], 0.0), deps=prev.get('zuse', []) + bar)
        zdeps = []
        for kbp in range(16):
            qt0 = (2 * kbp) // 4
            c0 = qt0 * 512
            for kpar in range(2):
                kb = 2 * kbp + kpar
                scd = []
                for qt in range(qt0, 8):
                    pb = qt % 2
                    mm = S.add("tensor", lambda t, kb=kb, qt=qt, pb=pb: t.matmul(
                        out=ps_s[pb][:], lhsT=k_h[:, kb * 128:(kb + 1) * 128],
                        rhs=q_h[:, qt * 512:(qt + 1) * 512], start=True, stop=True),
                        deps=[kd[kb // 4], qd[qt], prev.get(('pss', pb))])
                    if qt == qt0:
                        vv = kb % 4
                        d1 = S.add("vector", lambda v, qt=qt, pb=pb, vv=vv: v.scalar_tensor_tensor(
                            out=stage[:, qt * 512:(qt + 1) * 512], in0=ps_s[pb][:], scalar=BIGM,
                            in1=caus_sb[:, vv * 512:(vv + 1) * 512], op0=ALU.add, op1=ALU.mult),
                            deps=[mm, ld_caus[vv], prev.get(('stg', qt))])
                        ev = S.add("vector", lambda v, qt=qt: v.tensor_scalar(
                            out=stage[:, qt * 512:(qt + 1) * 512], in0=stage[:, qt * 512:(qt + 1) * 512],
                            scalar1=-BIGM, scalar2=None, op0=ALU.add), deps=[d1])
                    else:
                        ev = S.add("scalar", lambda s, qt=qt, pb=pb: s.activation(
                            out=stage[:, qt * 512:(qt + 1) * 512], in_=ps_s[pb][:], func=AF.Copy),
                            deps=[mm, prev.get(('stg', qt))])
                    prev[('pss', pb)] = ev
                    prev[('stg', qt)] = ev
                    scd.append(ev)
                mrow = S.add("vector", lambda v, c0=c0: v.reduce_max(out=negm[:], in_=stage[:, c0:T], axis=AX.X),
                             deps=scd + [prev.get('negm')])
                ng = S.add("vector", lambda v: v.tensor_scalar(out=negm[:], in0=negm[:], scalar1=-1.0,
                           scalar2=None, op0=ALU.mult), deps=[mrow])
                ex = S.add("scalar", lambda s, c0=c0, kpar=kpar: s.activation(
                    out=P_sb[kpar][:, c0:T], in_=stage[:, c0:T], func=AF.Exp, bias=negm[:], accum_out=Dcol[:]),
                    deps=[ng, prev.get('dcol'), prev.get(('puse', kpar))] + scd)
                for qt in range(qt0, 8):
                    prev[('stg', qt)] = ex
                rd = S.add("vector", lambda v: v.reciprocal(out=rD[:], in_=Dcol[:]), deps=[ex, prev.get('rd')])
                vpo = S.add("vector", lambda v, kb=kb, kpar=kpar: v.tensor_scalar(
                    out=vp[kpar][:], in0=v_h[:, kb * 64:(kb + 1) * 64], scalar1=rD[:], scalar2=None, op0=ALU.mult),
                    deps=[rd, vd[kb // 8], prev.get(('vp', kpar))])
                prev['dcol'] = rd; prev['rd'] = vpo; prev['negm'] = ex
                prev[('ex', kpar)] = ex; prev[('vpo', kpar)] = vpo
            lastav = None
            for qt in range(qt0, 8):
                pb = qt % 2
                for kpar in range(2):
                    lastav = S.add("tensor", lambda t, qt=qt, pb=pb, kpar=kpar: t.matmul(
                        out=ps_z[pb][0:64, 0:512], lhsT=vp[kpar][:], rhs=P_sb[kpar][:, qt * 512:(qt + 1) * 512],
                        start=(kpar == 0), stop=(kpar == 1)),
                        deps=[prev[('vpo', kpar)], prev[('ex', kpar)]] +
                             ([prev.get(('psz', pb))] if kpar == 0 else []))
                za = S.add("vector", lambda v, qt=qt, pb=pb: v.tensor_tensor(
                    out=z_sb[0:64, qt * 512:(qt + 1) * 512], in0=z_sb[0:64, qt * 512:(qt + 1) * 512],
                    in1=ps_z[pb][0:64, 0:512], op=ALU.add),
                    deps=[lastav, mz, prev.get(('za', qt))])
                prev[('psz', pb)] = za
                prev[('za', qt)] = za
                zdeps.append(za)
            prev[('puse', 0)] = lastav; prev[('puse', 1)] = lastav
            prev[('vp', 0)] = lastav; prev[('vp', 1)] = lastav
        zcs = []
        for tc in range(8):
            zb = 0
            zc = S.add("vector", lambda v, tc=tc, zb=zb: v.tensor_copy(
                out=zos[zb][:], in_=z_sb[0:64, tc * 512:(tc + 1) * 512]),
                deps=zdeps[-8:] + [prev.get(('zob', zb))])
            sto = S.dma("sync", o_z1(hl, tc), zos[zb][:], deps=[zc])
            outs.append(sto)
            prev[('zob', zb)] = sto
            zcs.append(zc)
        prev['zuse'] = zcs
        prev['q'] = qd[-1]

    wdone = [prev['q'], prev['zuse'][0]]
    for nm in ("fwq", "fwk", "fwv"):
        lds[nm] = S.dma("gpsimd", wsbs[nm][:].rearrange("p (hc c) -> p hc c", c=512),
                        fw_srcs[nm].rearrange("(hc p) c -> p hc c", p=128), deps=wdone)
    vaug_init = S.add("vector", lambda v: v.memset(vaug[:], 1.0), deps=bar)
    for hl in range(8):
        qd = proj_fm(q_h, "fwq", "fbqc", hl, False, 'fq')
        kd = proj_fm(k_h, "fwk", "fbkc", hl, False, 'fk')
        vd = proj_tok("fwv", "fbvrow", hl, False, 'fv')
        cpv = S.add("vector", lambda v: v.tensor_copy(
            out=vaug[:].rearrange("p (kb c) -> p kb c", c=65)[:, :, 0:64],
            in_=v_h[:].rearrange("p (kb c) -> p kb c", c=64)),
            deps=vd + [vaug_init] + prev.get('vauguse', []))
        mz = S.add("vector", lambda v: v.memset(z2_sb[:], 0.0),
                   deps=prev.get('z2use', []) + prev.get('zuse', []) + bar)
        zdeps = []
        for kb in range(32):
            for qt in range(8):
                pb = qt % 2
                mm = S.add("tensor", lambda t, kb=kb, qt=qt, pb=pb: t.matmul(
                    out=ps_s[pb][:], lhsT=k_h[:, kb * 128:(kb + 1) * 128],
                    rhs=q_h[:, qt * 512:(qt + 1) * 512], start=True, stop=True),
                    deps=[kd[kb // 4], qd[qt], prev.get(('pss', pb))])
                ex = S.add("scalar", lambda s, pb=pb: s.activation(
                    out=Pq[pb][:], in_=ps_s[pb][:], func=AF.Exp, scale=0.125),
                    deps=[mm, prev.get(('pq', pb))])
                prev[('pss', pb)] = ex
                av = S.add("tensor", lambda t, kb=kb, pb=pb: t.matmul(
                    out=ps_z[pb][0:65, 0:512], lhsT=vaug[:, kb * 65:(kb + 1) * 65], rhs=Pq[pb][:],
                    start=True, stop=True),
                    deps=[ex, cpv, prev.get(('psz', pb))])
                za = S.add("vector", lambda v, qt=qt, pb=pb: v.tensor_tensor(
                    out=z2_sb[:, qt * 512:(qt + 1) * 512], in0=z2_sb[:, qt * 512:(qt + 1) * 512],
                    in1=ps_z[pb][0:65, 0:512], op=ALU.add),
                    deps=[av, mz, prev.get(('za2', qt))])
                prev[('pq', pb)] = av
                prev[('psz', pb)] = za
                prev[('za2', qt)] = za
                zdeps.append(za)
        gr = S.add("vector", lambda v: v.reciprocal(out=s_row[:], in_=z2_sb[64:65, :]),
                   deps=zdeps[-8:] + [prev.get('grow')])
        zcs = []
        for tc in range(8):
            zb = 0
            mm = S.add("tensor", lambda t, tc=tc: t.matmul(
                out=ps_r[0:64, 0:512], lhsT=ones_sb[0:1, 0:64], rhs=s_row[0:1, tc * 512:(tc + 1) * 512],
                start=True, stop=True), deps=[gr, lds["ones"], prev.get('psr')])
            zo2 = S.add("vector", lambda v, tc=tc, zb=zb: v.tensor_tensor(
                out=zos[zb][:], in0=ps_r[0:64, 0:512],
                in1=z2_sb[0:64, tc * 512:(tc + 1) * 512], op=ALU.mult),
                deps=[mm, prev.get(('zob', zb))])
            prev['psr'] = zo2
            sto = S.dma("sync", o_z2(hl, tc), zos[zb][:], deps=[zo2])
            outs.append(sto)
            prev[('zob', zb)] = sto
            zcs.append(zo2)
        prev['z2use'] = zcs
        prev['vauguse'] = [prev[('pq', 0)], prev[('pq', 1)]]
        prev['grow'] = zcs[-1]

    return outs



def build_l1fin():
    nc = bass.Bass("TRN2", target_bir_lowering=False, debug=False)
    io = {}
    for nm, shp, dt in (("x1", [TOWN, D], F32), ("xn1fm", [1024, TOWN], BF16), ("z1fm", [1024, TOWN], BF16),
                        ("z2fm", [1024, TOWN], BF16), ("woT", [1024, 1024], F32), ("bo", [1, 1024], F32),
                        ("wgate", [1024, 8], F32), ("bgate", [1, 8], F32), ("wexp", [8, 1024, 1024], F32),
                        ("bexp", [1, 8 * 1024], F32), ("identb", [128, 128], BF16), ("ones128", [1, 128], F32)):
        io[nm] = nc.dram_tensor(nm, shp, dt, kind="ExternalInput").ap()
    io["out"] = nc.dram_tensor("out", [TOWN, D], F32, kind="ExternalOutput").ap()
    es = contextlib.ExitStack()
    S = Sched()
    outs = emit_l1fin(nc, S, es, io, [])
    S.emit(nc, es, final_waits=outs)
    es.close()
    return nc


def emit_l1fin(nc, S, es, io, bar, pfx="n_"):
    i_x1 = io["x1"]; i_xn = io["xn1fm"]
    if "z1get" in io:
        z1get = io["z1get"]; z2get = io["z2get"]
    else:
        i_z1 = io["z1fm"]; i_z2 = io["z2fm"]
        z1get = lambda c: i_z1[c * 128:(c + 1) * 128]
        z2get = lambda c: i_z2[c * 128:(c + 1) * 128]
    dyn = "goff" in io
    i_wo = io["woT"]; i_bo = io["bo"]; i_wg = io["wgate"]; i_bg = io["bgate"]
    i_we = io["wexp"]; i_be = io["bexp"]; i_idb = io["identb"]; i_ones = io["ones128"]
    o_out = io["out"]
    sb = lambda nme, s, d=F32: es.enter_context(nc.sbuf_tensor(pfx + nme, s, d))
    psum = lambda nme, s, d=F32: es.enter_context(nc.psum_tensor(pfx + nme, s, d))
    outs = []
    prev = {}

    xn_sb = sb("xn_sb", [128, 8 * TOWN], BF16)
    macc = sb("macc", [128, 16 * 1024])
    wo_sb = sb("wo_sb", [128, 8 * 1024], BF16)
    we_sb = [sb("we_sb0", [128, 8 * 1024], BF16)]
    bo_sb = sb("bo_sb", [1, 1024], BF16)
    be_sb = sb("be_sb", [1, 1024], BF16)
    bg_sb = sb("bg_sb", [1, 8], BF16)
    wg_sb = sb("wg_sb", [128, 8 * 8], BF16)
    onesb = sb("onesb", [1, 128], BF16)
    ones_sb = sb("ones_sb", [1, 128])
    id_sb = sb("id_sb", [128, 128], BF16)
    if dyn:
        z1res = sb("z1res", [128, 8 * TOWN], BF16)
        z2res = sb("z2res", [128, 8 * TOWN], BF16)
    else:
        z1t = [sb(f"z1t{i}", [128, 8 * 128], BF16) for i in range(2)]
        z2t = [sb(f"z2t{i}", [128, 8 * 128], BF16) for i in range(2)]
    x1t = [sb(f"x1t{i}", [128, 1024]) for i in range(2)]
    lgt = sb("lgt", [128, 8]); exl = sb("exl", [128, 8])
    m1c = sb("m1c", [128, 1]); ssc = sb("ssc", [128, 1]); rsc = sb("rsc", [128, 1])
    io8 = sb("io8", [128, 8]); cnd = sb("cnd", [128, 8])
    i1c = sb("i1c", [128, 1]); i2c = sb("i2c", [128, 1])
    oh1 = sb("oh1", [128, 8]); oh2 = sb("oh2", [128, 8]); lg2 = sb("lg2", [128, 8])
    cw_all = sb("cw_all", [128, 16 * 8])
    ps_a = [psum(f"ps_a{i}", [128, 512]) for i in range(2)]
    ps_t = [psum(f"ps_t{i}", [128, 1024], BF16) for i in range(2)]
    ps_g = psum("ps_g", [128, 512])
    ps_y = [psum(f"ps_y{i}", [128, 512]) for i in range(2)]

    ld_xn = S.dma("sync", xn_sb[:].rearrange("p (c t) -> p c t", t=TOWN),
                  i_xn.rearrange("(c p) t -> p c t", p=128), deps=bar)
    ld_wo = S.dma("gpsimd", wo_sb[:].rearrange("p (c d) -> p c d", d=1024),
                  i_wo.rearrange("(c p) d -> p c d", p=128), deps=bar)
    ld_bo = S.dma("gpsimd", bo_sb[:], i_bo, deps=bar)

    ld_bg = S.dma("gpsimd", bg_sb[:], i_bg, deps=bar)
    ld_wg = S.dma("gpsimd", wg_sb[:].rearrange("p (c e) -> p c e", e=8),
                  i_wg.rearrange("(c p) e -> p c e", p=128), deps=bar)
    ld_ones = S.dma("sync", ones_sb[:], i_ones, deps=bar)
    ld_id = S.dma("sync", id_sb[:], i_idb, deps=bar)
    cob = S.add("vector", lambda v: v.tensor_copy(out=onesb[:], in_=ones_sb[:]), deps=[ld_ones])
    iog = S.add("gpsimd", lambda g: g.iota(out=io8[:], pattern=[[1, 8]], base=0, channel_multiplier=0,
                                           allow_small_or_imprecise_dtypes=True), deps=bar)

    if dyn:
        goff = io["goff"]
        zldep = []
        for c in range(8):
            zldep.append(S.dma("sync", z1res[:, c * TOWN:(c + 1) * TOWN],
                               lambda c=c: z1get(c)[:, bass.ds(goff[0], TOWN)],
                               deps=bar + [io.get("goff_dep")] + zldep[-1:]))
            zldep.append(S.dma("sync", z2res[:, c * TOWN:(c + 1) * TOWN],
                               lambda c=c: z2get(c)[:, bass.ds(goff[0], TOWN)],
                               deps=bar + [io.get("goff_dep")] + zldep[-1:]))

    for i in range(NT):
        b = i % 2
        if dyn:
            lz1 = zldep[-1]
            lz2 = zldep[-1]
            z1sl = lambda c, i=i: z1res[:, c * TOWN + i * 128: c * TOWN + (i + 1) * 128]
            z2sl = lambda c, i=i: z2res[:, c * TOWN + i * 128: c * TOWN + (i + 1) * 128]
        else:
            lz1 = [S.dma("sync", z1t[b][:, c * 128:(c + 1) * 128],
                         z1get(c)[:, i * 128:(i + 1) * 128],
                         deps=prev.get(('z1', b), []) + bar) for c in range(8)][-1]
            lz2 = [S.dma("sync", z2t[b][:, c * 128:(c + 1) * 128],
                         z2get(c)[:, i * 128:(i + 1) * 128],
                         deps=prev.get(('z2', b), []) + bar) for c in range(8)][-1]
            z1sl = lambda c, b=b: z1t[b][:, c * 128:(c + 1) * 128]
            z2sl = lambda c, b=b: z2t[b][:, c * 128:(c + 1) * 128]
        lx1 = S.dma("sync", x1t[b][:], i_x1[i * 128:(i + 1) * 128, :], deps=prev.get(('x1', b), []) + bar)
        ztr = []
        for c in range(8):
            cb = c % 2
            tr = S.add("tensor", lambda t, c=c, cb=cb, z2sl=z2sl: t.transpose(
                out=ps_t[cb][:, 0:128], in_=z2sl(c), identity=id_sb[:]),
                deps=[lz2, ld_id] + prev.get(('pst', cb), []))
            ad = S.add("vector", lambda v, b=b, c=c, cb=cb: v.tensor_tensor(
                out=x1t[b][:, c * 128:(c + 1) * 128], in0=x1t[b][:, c * 128:(c + 1) * 128],
                in1=ps_t[cb][:, 0:128], op=ALU.add), deps=[tr, lx1, prev.get(('x1a', b))])
            prev[('pst', cb)] = [ad]
            prev[('x1a', b)] = ad
            ztr.append(ad)
        for dc in range(2):
            pb = dc
            mms = None
            for zc in range(8):
                mms = S.add("tensor", lambda t, zc=zc, dc=dc, pb=pb, z1sl=z1sl: t.matmul(
                    out=ps_a[pb][:], lhsT=z1sl(zc),
                    rhs=wo_sb[:, zc * 1024 + dc * 512: zc * 1024 + (dc + 1) * 512],
                    start=(zc == 0), stop=False),
                    deps=[lz1, ld_wo] + ([prev.get(('psa', pb))] if zc == 0 else []))
            mb = S.add("tensor", lambda t, dc=dc, pb=pb: t.matmul(
                out=ps_a[pb][:], lhsT=onesb[:], rhs=bo_sb[:, dc * 512:(dc + 1) * 512],
                start=False, stop=True), deps=[ld_bo, cob])
            ad = S.add("vector", lambda v, b=b, dc=dc, pb=pb, i=i: v.scalar_tensor_tensor(
                out=macc[:, i * 1024 + dc * 512: i * 1024 + (dc + 1) * 512], in0=ps_a[pb][:],
                scalar=1.0, in1=x1t[b][:, dc * 512:(dc + 1) * 512], op0=ALU.mult, op1=ALU.add),
                deps=[mb, prev[('x1a', b)], ztr[-1]] + prev.get(('macc', i), []))
            prev[('psa', pb)] = ad
            prev[('macc', i)] = [ad]
        prev[('z1', b)] = [prev[('psa', 0)], prev[('psa', 1)]]
        prev[('z2', b)] = ztr
        prev[('x1', b)] = [prev[('psa', 0)], prev[('psa', 1)]]
        mmg = None
        for hc in range(8):
            mmg = S.add("tensor", lambda t, hc=hc, i=i: t.matmul(
                out=ps_g[:, 0:8], lhsT=xn_sb[:, hc * TOWN + i * 128: hc * TOWN + (i + 1) * 128],
                rhs=wg_sb[:, hc * 8:(hc + 1) * 8], start=(hc == 0), stop=False),
                deps=[ld_xn, ld_wg] + ([prev.get('psg')] if hc == 0 else []))
        mgb = S.add("tensor", lambda t: t.matmul(out=ps_g[:, 0:8], lhsT=onesb[:], rhs=bg_sb[:],
                    start=False, stop=True), deps=[ld_bg, cob])
        e0 = S.add("vector", lambda v: v.tensor_copy(out=lgt[:], in_=ps_g[:, 0:8]), deps=[mgb, prev.get('lgt')])
        prev['psg'] = e0
        e1 = S.add("vector", lambda v: v.reduce_max(out=m1c[:], in_=lgt[:], axis=AX.X), deps=[e0, prev.get('m1c')])
        e1b = S.add("vector", lambda v: v.tensor_scalar(out=m1c[:], in0=m1c[:], scalar1=-1.0, scalar2=None, op0=ALU.mult), deps=[e1])
        e2 = S.add("scalar", lambda s: s.activation(out=exl[:], in_=lgt[:], func=AF.Exp, bias=m1c[:],
                   accum_out=ssc[:]), deps=[e1b, prev.get('exl')])
        e3 = S.add("vector", lambda v: v.reciprocal(out=rsc[:], in_=ssc[:]), deps=[e2, prev.get('rsc')])
        e4 = S.add("vector", lambda v: v.tensor_scalar(out=exl[:], in0=exl[:], scalar1=rsc[:], scalar2=None, op0=ALU.mult), deps=[e3])
        f1 = S.add("vector", lambda v: v.reduce_max(out=i1c[:], in_=exl[:], axis=AX.X), deps=[e4, prev.get('i1c')])
        f2 = S.add("vector", lambda v: v.tensor_scalar(out=cnd[:], in0=exl[:], scalar1=i1c[:], scalar2=None, op0=ALU.is_ge), deps=[f1, prev.get('cnd')])
        f3 = S.add("vector", lambda v: v.scalar_tensor_tensor(out=cnd[:], in0=cnd[:], scalar=-BIGC, in1=io8[:], op0=ALU.mult, op1=ALU.add), deps=[f2, iog])
        f4 = S.add("vector", lambda v: v.tensor_reduce(out=i1c[:], in_=cnd[:], axis=AX.X, op=ALU.min), deps=[f3])
        f4b = S.add("vector", lambda v: v.tensor_scalar(out=i1c[:], in0=i1c[:], scalar1=BIGC, scalar2=None, op0=ALU.add), deps=[f4])
        f5 = S.add("vector", lambda v: v.tensor_scalar(out=oh1[:], in0=io8[:], scalar1=i1c[:], scalar2=None, op0=ALU.is_equal), deps=[f4b, prev.get('oh1')])
        f6 = S.add("vector", lambda v: v.scalar_tensor_tensor(out=lg2[:], in0=oh1[:], scalar=-BIGC, in1=exl[:], op0=ALU.mult, op1=ALU.add), deps=[f5, prev.get('lg2')])
        f7 = S.add("vector", lambda v: v.reduce_max(out=i2c[:], in_=lg2[:], axis=AX.X), deps=[f6, prev.get('i2c')])
        f8 = S.add("vector", lambda v: v.tensor_scalar(out=cnd[:], in0=lg2[:], scalar1=i2c[:], scalar2=None, op0=ALU.is_ge), deps=[f7, f3])
        f9 = S.add("vector", lambda v: v.scalar_tensor_tensor(out=cnd[:], in0=cnd[:], scalar=-BIGC, in1=io8[:], op0=ALU.mult, op1=ALU.add), deps=[f8])
        f10 = S.add("vector", lambda v: v.tensor_reduce(out=i2c[:], in_=cnd[:], axis=AX.X, op=ALU.min), deps=[f9, f7])
        f10b = S.add("vector", lambda v: v.tensor_scalar(out=i2c[:], in0=i2c[:], scalar1=BIGC, scalar2=None, op0=ALU.add), deps=[f10])
        f11 = S.add("vector", lambda v: v.tensor_scalar(out=oh2[:], in0=io8[:], scalar1=i2c[:], scalar2=None, op0=ALU.is_equal), deps=[f10b, prev.get('oh2')])
        f12 = S.add("vector", lambda v: v.tensor_tensor(out=oh1[:], in0=oh1[:], in1=oh2[:], op=ALU.add), deps=[f11, f6])
        f13 = S.add("vector", lambda v, i=i: v.tensor_tensor(out=cw_all[:, i * 8:(i + 1) * 8], in0=exl[:], in1=oh1[:], op=ALU.mult), deps=[f12, e4, prev.get(('cw', i))])
        prev['lgt'] = f13; prev['m1c'] = f13; prev['exl'] = f13; prev['rsc'] = e4
        prev['i1c'] = f5; prev['cnd'] = f10; prev['oh1'] = f13; prev['lg2'] = f8
        prev['i2c'] = f11; prev['oh2'] = f12
        prev[('cw', i)] = [f13]
        prev[('cwd', i)] = f13

    for e in range(8):
        eb = 0
        lwe = S.dma("gpsimd", we_sb[eb][:].rearrange("p (c d) -> p c d", d=1024),
                    i_we[e].rearrange("(c p) d -> p c d", p=128), deps=prev.get(('we', eb), []) + bar)
        ld_be = S.dma("gpsimd", be_sb[:], i_be[:, e * 1024:(e + 1) * 1024],
                      deps=prev.get('beuse', []) + bar)
        last_mm = {}
        for i in range(NT):
            for dc in range(2):
                pb = dc
                mms = None
                for hc in range(8):
                    mms = S.add("tensor", lambda t, hc=hc, i=i, dc=dc, pb=pb, eb=eb: t.matmul(
                        out=ps_y[pb][:], lhsT=xn_sb[:, hc * TOWN + i * 128: hc * TOWN + (i + 1) * 128],
                        rhs=we_sb[eb][:, hc * 1024 + dc * 512: hc * 1024 + (dc + 1) * 512],
                        start=(hc == 0), stop=False),
                        deps=[ld_xn, lwe] + ([prev.get(('psy', pb))] if hc == 0 else []))
                mb = S.add("tensor", lambda t, dc=dc, pb=pb: t.matmul(
                    out=ps_y[pb][:], lhsT=onesb[:], rhs=be_sb[:, dc * 512:(dc + 1) * 512],
                    start=False, stop=True), deps=[ld_be, cob])
                ev = S.add("vector", lambda v, i=i, dc=dc, pb=pb, e=e: v.scalar_tensor_tensor(
                    out=macc[:, i * 1024 + dc * 512: i * 1024 + (dc + 1) * 512], in0=ps_y[pb][:],
                    scalar=cw_all[:, i * 8 + e: i * 8 + e + 1],
                    in1=macc[:, i * 1024 + dc * 512: i * 1024 + (dc + 1) * 512],
                    op0=ALU.mult, op1=ALU.add),
                    deps=[mb, prev[('cwd', i)], prev.get(('psy', pb))] + prev.get(('macc', i), []))
                prev[('psy', pb)] = ev
                prev[('macc', i)] = [ev]
                last_mm[i] = mms
        prev[('we', eb)] = [last_mm[NT - 1]]
        prev['beuse'] = [last_mm[NT - 1]]
    for i in range(NT):
        sto = S.dma("sync", o_out[i * 128:(i + 1) * 128, :], macc[:, i * 1024:(i + 1) * 1024],
                    deps=prev[('macc', i)])
        outs.append(sto)

    return outs



def kernel_unfused(**inputs):
    x = np.ascontiguousarray(inputs["L0_attn_hidden_states"], dtype=np.float32)
    f32 = lambda a: np.ascontiguousarray(a, dtype=np.float32)

    cq, sq_, ck, sk = xpos_tables()
    rotT = rot_matT()
    wqkv = f32(inputs["L0_attn_W_qkv"])
    wg = f32(inputs["L0_attn_W_g"])
    nc1 = build_l0prep()
    im1 = []
    for p in range(4):
        for half in range(2):
            sl = slice(half * TOWN, (half + 1) * TOWN)
            im1.append({
                "x": np.ascontiguousarray(x[p, sl]), "wqkv": wqkv, "wg": wg,
                "cos_q": np.ascontiguousarray(cq[sl].T).astype(BF), "sin_q": np.ascontiguousarray(sq_[sl].T).astype(BF),
                "cos_k": np.ascontiguousarray(ck[sl].T).astype(BF), "sin_k": np.ascontiguousarray(sk[sl].T).astype(BF),
                "rotmT": rotT, "ident": IDENT_BF,
            })
    r1 = run_bass_kernel_spmd(nc1, im1, core_ids=list(range(8))).results

    decay = f32(inputs["L0_attn_decay"])
    caus_f = np.stack([(np.arange(512)[None, :] - np.arange(128)[:, None] - 128 * v >= 0)
                       for v in range(4)]).astype(np.float32)
    nc2 = build_l0ret()
    im2 = []
    for p in range(4):
        qr_full = np.concatenate([r1[2 * p]["qr"], r1[2 * p + 1]["qr"]], axis=1)
        kr_full = np.concatenate([r1[2 * p]["kr"], r1[2 * p + 1]["kr"]], axis=1)
        v_full = np.concatenate([r1[2 * p]["v"], r1[2 * p + 1]["v"]], axis=0)
        g_full = np.concatenate([r1[2 * p]["gate"], r1[2 * p + 1]["gate"]], axis=0)
        for h2 in range(2):
            heads = [h2 * 4 + j for j in range(4)]
            dec_in = np.zeros((1, 8), np.float32)
            dec_in[0, :4] = decay[heads]
            dec_in[0, 4:] = 0.5
            im2.append({
                "qh": np.ascontiguousarray(np.stack([qr_full[h * 128:(h + 1) * 128] for h in heads])),
                "kh": np.ascontiguousarray(np.stack([kr_full[h * 128:(h + 1) * 128] for h in heads])),
                "vh": np.ascontiguousarray(np.stack([v_full[:, h * 256:(h + 1) * 256] for h in heads])),
                "gh": np.ascontiguousarray(np.stack([g_full[:, h * 256:(h + 1) * 256] for h in heads])),
                "decay": dec_in, "causal": caus_f, "ident": IDENT_BF,
                "ones128": np.ones((1, 128), np.float32), "hoff": np.zeros((1, 1), np.float32),
            })
    r2 = run_bass_kernel_spmd(nc2, im2, core_ids=list(range(8))).results

    nc3 = build_l0ffn()
    im3 = []
    for p in range(4):
        gfm_full = np.zeros((2048, T), dtype=BF)
        for h in range(8):
            gfm_full[h * 256:(h + 1) * 256] = r2[2 * p + (0 if h < 4 else 1)]["gfm"][h % 4]
        for half in range(2):
            sl = slice(half * TOWN, (half + 1) * TOWN)
            im3.append({
                "x": np.ascontiguousarray(x[p, sl]),
                "gfm": np.ascontiguousarray(gfm_full[:, sl]),
                "wproj": f32(inputs["L0_attn_W_proj"]),
                "wgr": f32(inputs["L0_ffn_Wg"]), "bg": f32(inputs["L0_ffn_bg"]),
                "w1": f32(inputs["L0_ffn_W1"]),
                "b1t": np.ascontiguousarray(f32(inputs["L0_ffn_b1"]).reshape(4, 32, 128).transpose(2, 0, 1).reshape(128, 128)),
                "w2": f32(inputs["L0_ffn_W2"]), "b2": f32(inputs["L0_ffn_b2"]),
                "identb": IDENT_BF, "ones128": np.ones((1, 128), np.float32),
            })
    r3 = run_bass_kernel_spmd(nc3, im3, core_ids=list(range(8))).results

    Wr = f32(inputs["L1_attn_W_qkv"]).reshape(16, 64, 3, 1024)
    br = f32(inputs["L1_attn_b_qkv"]).reshape(16, 64, 3)
    fW = f32(inputs["L1_ffn_W_qkv"])
    fb = f32(inputs["L1_ffn_b_qkv"])
    caus_b = caus_f.astype(BF)
    nc4 = build_l1attn()
    im4 = []
    for p in range(4):
        xn1_full = np.concatenate([r3[2 * p]["xn1fm"], r3[2 * p + 1]["xn1fm"]], axis=1)
        for h2 in range(2):
            heads = [h2 * 8 + j for j in range(8)]
            def bcol(which):
                out = np.zeros((128, 4), np.float32)
                for hl, h in enumerate(heads):
                    out[(hl % 2) * 64:(hl % 2) * 64 + 64, hl // 2] = br[h, :, which]
                return out
            bvrow = np.zeros((1, 512), np.float32)
            fbvrow = np.zeros((1, 512), np.float32)
            for hl, h in enumerate(heads):
                bvrow[0, hl * 64:(hl + 1) * 64] = br[h, :, 2]
                fbvrow[0, hl * 64:(hl + 1) * 64] = fb[2048 + h * 64: 2048 + (h + 1) * 64]
            def fbcol(which):
                out = np.zeros((128, 4), np.float32)
                for hl, h in enumerate(heads):
                    out[(hl % 2) * 64:(hl % 2) * 64 + 64, hl // 2] = fb[which * 1024 + h * 64: which * 1024 + (h + 1) * 64]
                return out
            cols = [h * 64 + j for h in heads for j in range(64)]
            im4.append({
                "xn1fm": np.ascontiguousarray(xn1_full),
                "wq": np.ascontiguousarray(np.concatenate([Wr[h, :, 0, :].T for h in heads], axis=1)),
                "wk": np.ascontiguousarray(np.concatenate([Wr[h, :, 1, :].T for h in heads], axis=1)),
                "wv": np.ascontiguousarray(np.concatenate([Wr[h, :, 2, :].T for h in heads], axis=1)),
                "bqc": bcol(0), "bkc": bcol(1), "bvrow": bvrow,
                "fwq": np.ascontiguousarray(fW[:, 0:1024][:, cols]),
                "fwk": np.ascontiguousarray(fW[:, 1024:2048][:, cols]),
                "fwv": np.ascontiguousarray(fW[:, 2048:3072][:, cols]),
                "fbqc": fbcol(0), "fbkc": fbcol(1), "fbvrow": fbvrow,
                "causal": caus_b, "ones128": np.ones((1, 128), np.float32),
                "onescol": np.ones((128, 1), np.float32), "identf": IDENT_F32,
            })
    r4 = run_bass_kernel_spmd(nc4, im4, core_ids=list(range(8))).results

    nc5 = build_l1fin()
    im5 = []
    for p in range(4):
        z1_full = np.zeros((1024, T), dtype=BF)
        z2_full = np.zeros((1024, T), dtype=BF)
        for h in range(16):
            src = r4[2 * p + (0 if h < 8 else 1)]
            z1_full[h * 64:(h + 1) * 64] = src["z1"][h % 8]
            z2_full[h * 64:(h + 1) * 64] = src["z2"][h % 8]
        for half in range(2):
            sl = slice(half * TOWN, (half + 1) * TOWN)
            im5.append({
                "x1": r3[2 * p + half]["x1"],
                "xn1fm": r3[2 * p + half]["xn1fm"],
                "z1fm": np.ascontiguousarray(z1_full[:, sl]),
                "z2fm": np.ascontiguousarray(z2_full[:, sl]),
                "woT": np.ascontiguousarray(f32(inputs["L1_attn_W_o"]).T),
                "bo": f32(inputs["L1_attn_b_o"]).reshape(1, 1024),
                "wgate": f32(inputs["L1_ffn_W_gate"]),
                "bgate": f32(inputs["L1_ffn_b_gate"]).reshape(1, 8),
                "wexp": f32(inputs["L1_ffn_W_experts"]),
                "bexp": f32(inputs["L1_ffn_b_experts"]).reshape(1, 8192),
                "identb": IDENT_BF, "ones128": np.ones((1, 128), np.float32),
            })
    r5 = run_bass_kernel_spmd(nc5, im5, core_ids=list(range(8))).results

    out = np.zeros((B, T, D), dtype=np.float32)
    for p in range(4):
        for half in range(2):
            out[p, half * TOWN:(half + 1) * TOWN] = r5[2 * p + half]["out"]
    return out



def build_fused(groups=None, detect_races=True):
    groups = groups or [[0, 1], [2, 3], [4, 5], [6, 7]]
    nc = bass.Bass("TRN2", target_bir_lowering=False, debug=False, detect_race_conditions=detect_races)
    S = Sched()
    es_main = contextlib.ExitStack()
    dram = lambda nm, shp, dt=F32, kind="ExternalInput": nc.dram_tensor(nm, shp, dt, kind=kind).ap()

    x_full = dram("x", [T, D])
    x_own = dram("x_own", [TOWN, D])
    wqkv_my = dram("wqkv", [D, 2048])
    wg_my = dram("wg", [D, 1024])
    tabs = {nm: dram(nm, [512, T], BF16) for nm in ("cos_q", "sin_q", "cos_k", "sin_k")}
    rotmT_i = dram("rotmT", [128, 128], BF16)
    identb_i = dram("ident", [128, 128], BF16)
    identf_i = dram("identf", [128, 128], F32)
    ones_i = dram("ones128", [1, 128])
    onesc_i = dram("onescol", [128, 1])
    decay_i = dram("decay", [1, 8])
    causf_i = dram("causf", [4, 128, 512])
    causb_i = dram("causb", [4, 128, 512], BF16)
    wproj_i = dram("wproj", [2048, 1024])
    wgr_i = dram("wgr", [4, 1024, 4])
    bg_i = dram("bg", [4, 4])
    w1_i = dram("w1", [4, 1024, 4096])
    b1t_i = dram("b1t", [128, 128])
    w2_i = dram("w2", [4, 4096, 1024])
    b2_i = dram("b2", [4, 1024])
    l1w = {nm: dram(nm, [1024, 512]) for nm in ("wq", "wk", "wv", "fwq", "fwk", "fwv")}
    l1b = {nm: dram(nm, [128, 4]) for nm in ("bqc", "bkc", "fbqc", "fbkc")}
    l1br = {nm: dram(nm, [1, 512]) for nm in ("bvrow", "fbvrow")}
    woT_i = dram("woT", [1024, 1024])
    bo_i = dram("bo", [1, 1024])
    wgate_i = dram("wgate", [1024, 8])
    bgate_i = dram("bgate", [1, 8])
    wexp_i = dram("wexp", [8, 1024, 1024])
    bexp_i = dram("bexp", [1, 8 * 1024])
    o_out = dram("out", [TOWN, D], F32, kind="ExternalOutput")

    qr_t = dram("qr_t", [512, T], BF16, kind="Internal")
    kr_t = dram("kr_t", [512, T], BF16, kind="Internal")
    v_t = dram("v_t", [T, 1024], BF16, kind="Internal")
    g_t = dram("g_t", [T, 1024], BF16, kind="Internal")
    ccg_in = dram("ccg_in", [1024, T], BF16, kind="Internal")
    ccg_out = dram("ccg_out", [2048, T], BF16, kind="Internal")
    ccx_in = dram("ccx_in", [1024, TOWN], BF16, kind="Internal")
    ccx_out = dram("ccx_out", [2048, TOWN], BF16, kind="Internal")
    ccz_in = dram("ccz_in", [1024, T], BF16, kind="Internal")
    ccz_out = dram("ccz_out", [2048, T], BF16, kind="Internal")
    own0_i = dram("own0", [1, 1], mybir.dt.int32)
    x1_t = dram("x1_t", [TOWN, D], F32, kind="Internal")
    w1b_t = dram("w1b", [4, 1024, 4096], BF16, kind="Internal")
    w2b_t = dram("w2b", [4, 4096, 1024], BF16, kind="Internal")
    xpa_t = dram("xpa", [TOWN, D], F32, kind="Internal")
    upd_t = dram("updsum", [TOWN, D], F32, kind="Internal")

    esE = contextlib.ExitStack()
    esE2 = contextlib.ExitStack()

    off_sb = es_main.enter_context(nc.sbuf_tensor("off_sb", [1, 1], mybir.dt.int32))
    goff_cell = [None]
    ld_off = S.dma("sync", off_sb[:], own0_i)
    def _setup_off(sync):
        reg = sync.alloc_register("own0reg")
        inst = sync.reg_load(reg, off_sb[0:1, 0:1])
        goff_cell[0] = sync.snap(reg)
        return inst
    off_op = S.add("sync", _setup_off, deps=[ld_off])

    es0 = contextlib.ExitStack()
    io0 = {"x": x_full, "wqkv": wqkv_my, "wg": wg_my, "rotmT": rotmT_i, "ident": identb_i,
           "qr": qr_t, "kr": kr_t, "v": v_t, "gate": g_t,
           "cos_q": tabs["cos_q"], "sin_q": tabs["sin_q"], "cos_k": tabs["cos_k"], "sin_k": tabs["sin_k"]}
    p0 = emit_l0prep(nc, S, es0, io0, [], ntok=T, nq=4, nv=1024, ng=1024, store_xnfm=False, pfx="p_")
    es0.close()

    es2_ = contextlib.ExitStack()
    qr_v = qr_t.rearrange("(h p) t -> h p t", p=128)
    kr_v = kr_t.rearrange("(h p) t -> h p t", p=128)
    io2 = {"qh": lambda m: qr_v[m], "kh": lambda m: kr_v[m],
           "vh": lambda m: v_t[:, m * 256:(m + 1) * 256],
           "gh": lambda m: g_t[:, m * 256:(m + 1) * 256],
           "decay": decay_i, "causal": causf_i, "ident": identb_i, "ones128": ones_i,
           "gfm": lambda m: ccg_in[m * 256:(m + 1) * 256]}
    p2 = emit_l0ret(nc, S, es2_, io2, bar=p0, pfx="r_")
    es2_.close()

    cc1 = S.add("gpsimd", lambda g: g.collective_compute(
        "AllGather", ALU.bypass, replica_groups=groups, ins=[ccg_in.opt()], outs=[ccg_out.opt()]),
        deps=p2)

    es3 = contextlib.ExitStack()
    io3 = {"x": x_own, "gfm": ccg_out, "goff": goff_cell, "goff_dep": off_op,
           "wproj": wproj_i, "wgr": wgr_i, "bg": bg_i,
           "w1": w1_i, "b1t": b1t_i, "w2": w2_i, "b2": b2_i, "identb": identb_i,
           "ones128": ones_i, "x1": x1_t, "xn1fm": ccx_in,
           "w1b": w1b_t, "w2b": w2b_t, "xpa": xpa_t, "updsum": upd_t}
    p3 = emit_l0ffn(nc, S, es3, io3, bar=[cc1] + p2 + p0, pfx="f_", es_early=esE, bar_early=[], es_ln_ext=esE2)
    es3.close()
    esE.close()

    cc2 = S.add("gpsimd", lambda g: g.collective_compute(
        "AllGather", ALU.bypass, replica_groups=groups, ins=[ccx_in.opt()], outs=[ccx_out.opt()]),
        deps=p3)

    es4 = contextlib.ExitStack()
    io4 = {"xn1fm_parts": [(ccx_out[0:1024], 0), (ccx_out[1024:2048], TOWN)],
           "causal": causb_i, "ones128": ones_i, "onescol": onesc_i, "identf": identf_i,
           "z1": lambda h: ccz_in[h * 64:(h + 1) * 64],
           "z2": lambda h: ccz_in[512 + h * 64: 512 + (h + 1) * 64]}
    io4.update(l1w); io4.update(l1b); io4.update(l1br)
    p4 = emit_l1attn(nc, S, es4, io4, bar=[cc2] + p3, pfx="a_")
    es4.close()

    cc3 = S.add("gpsimd", lambda g: g.collective_compute(
        "AllGather", ALU.bypass, replica_groups=groups, ins=[ccz_in.opt()], outs=[ccz_out.opt()]),
        deps=p4)

    es5 = contextlib.ExitStack()
    io5 = {"x1": x1_t, "xn1fm": ccx_in, "goff": goff_cell, "goff_dep": off_op,
           "z1get": lambda c: ccz_out[c * 128:(c + 1) * 128] if c < 4 else ccz_out[1024 + (c - 4) * 128: 1024 + (c - 3) * 128],
           "z2get": lambda c: ccz_out[512 + c * 128: 512 + (c + 1) * 128] if c < 4 else ccz_out[1536 + (c - 4) * 128: 1536 + (c - 3) * 128],
           "woT": woT_i, "bo": bo_i, "wgate": wgate_i, "bgate": bgate_i,
           "wexp": wexp_i, "bexp": bexp_i, "identb": identb_i, "ones128": ones_i,
           "out": o_out}
    p5 = emit_l1fin(nc, S, es5, io5, bar=[cc3] + p4, pfx="n_")
    es5.close()

    S.emit(nc, es_main, final_waits=p5)
    es_main.close()
    return nc


def fused_in_maps(inputs):
    x = np.ascontiguousarray(inputs["L0_attn_hidden_states"], dtype=np.float32)
    f32 = lambda a: np.ascontiguousarray(a, dtype=np.float32)
    cq, sq_, ck, sk = xpos_tables()
    rotT = rot_matT()
    Wqkv = f32(inputs["L0_attn_W_qkv"])
    Wg = f32(inputs["L0_attn_W_g"])
    decay = f32(inputs["L0_attn_decay"])
    caus_f = np.stack([(np.arange(512)[None, :] - np.arange(128)[:, None] - 128 * v >= 0)
                       for v in range(4)]).astype(np.float32)
    Wr = f32(inputs["L1_attn_W_qkv"]).reshape(16, 64, 3, 1024)
    br = f32(inputs["L1_attn_b_qkv"]).reshape(16, 64, 3)
    fW = f32(inputs["L1_ffn_W_qkv"])
    fb = f32(inputs["L1_ffn_b_qkv"])
    maps = []
    for p in range(4):
        for h2 in range(2):
            heads0 = [h2 * 4 + j for j in range(4)]
            heads1 = [h2 * 8 + j for j in range(8)]
            own = slice(h2 * TOWN, (h2 + 1) * TOWN)
            qcols = [h * 128 + j for h in heads0 for j in range(128)]
            vcols = [h * 256 + j for h in heads0 for j in range(256)]
            wqkv_my = np.concatenate([Wqkv[:, qcols], Wqkv[:, [1024 + c for c in qcols]],
                                      Wqkv[:, [2048 + c for c in vcols]]], axis=1)
            dec_in = np.zeros((1, 8), np.float32)
            dec_in[0, :4] = decay[heads0]
            dec_in[0, 4:] = 0.5
            def bcol(which):
                o = np.zeros((128, 4), np.float32)
                for hl, h in enumerate(heads1):
                    o[(hl % 2) * 64:(hl % 2) * 64 + 64, hl // 2] = br[h, :, which]
                return o
            def fbcol(which):
                o = np.zeros((128, 4), np.float32)
                for hl, h in enumerate(heads1):
                    o[(hl % 2) * 64:(hl % 2) * 64 + 64, hl // 2] = fb[which * 1024 + h * 64: which * 1024 + (h + 1) * 64]
                return o
            bvrow = np.zeros((1, 512), np.float32)
            fbvrow = np.zeros((1, 512), np.float32)
            for hl, h in enumerate(heads1):
                bvrow[0, hl * 64:(hl + 1) * 64] = br[h, :, 2]
                fbvrow[0, hl * 64:(hl + 1) * 64] = fb[2048 + h * 64: 2048 + (h + 1) * 64]
            cols1 = [h * 64 + j for h in heads1 for j in range(64)]
            maps.append({
                "x": x[p], "x_own": np.ascontiguousarray(x[p, own]),
                "own0": np.array([[h2 * TOWN]], np.int32),
                "wqkv": np.ascontiguousarray(wqkv_my), "wg": np.ascontiguousarray(Wg[:, vcols]),
                "cos_q": np.ascontiguousarray(cq.T[qcols]).astype(BF),
                "sin_q": np.ascontiguousarray(sq_.T[qcols]).astype(BF),
                "cos_k": np.ascontiguousarray(ck.T[qcols]).astype(BF),
                "sin_k": np.ascontiguousarray(sk.T[qcols]).astype(BF),
                "rotmT": rotT, "ident": IDENT_BF, "identf": IDENT_F32,
                "ones128": np.ones((1, 128), np.float32), "onescol": np.ones((128, 1), np.float32),
                "decay": dec_in, "causf": caus_f, "causb": caus_f.astype(BF),
                "wproj": f32(inputs["L0_attn_W_proj"]),
                "wgr": f32(inputs["L0_ffn_Wg"]), "bg": f32(inputs["L0_ffn_bg"]),
                "w1": f32(inputs["L0_ffn_W1"]),
                "b1t": np.ascontiguousarray(f32(inputs["L0_ffn_b1"]).reshape(4, 32, 128).transpose(2, 0, 1).reshape(128, 128)),
                "w2": f32(inputs["L0_ffn_W2"]), "b2": f32(inputs["L0_ffn_b2"]),
                "wq": np.ascontiguousarray(np.concatenate([Wr[h, :, 0, :].T for h in heads1], axis=1)),
                "wk": np.ascontiguousarray(np.concatenate([Wr[h, :, 1, :].T for h in heads1], axis=1)),
                "wv": np.ascontiguousarray(np.concatenate([Wr[h, :, 2, :].T for h in heads1], axis=1)),
                "bqc": bcol(0), "bkc": bcol(1), "bvrow": bvrow,
                "fwq": np.ascontiguousarray(fW[:, 0:1024][:, cols1]),
                "fwk": np.ascontiguousarray(fW[:, 1024:2048][:, cols1]),
                "fwv": np.ascontiguousarray(fW[:, 2048:3072][:, cols1]),
                "fbqc": fbcol(0), "fbkc": fbcol(1), "fbvrow": fbvrow,
                "woT": np.ascontiguousarray(f32(inputs["L1_attn_W_o"]).T),
                "bo": f32(inputs["L1_attn_b_o"]).reshape(1, 1024),
                "wgate": f32(inputs["L1_ffn_W_gate"]),
                "bgate": f32(inputs["L1_ffn_b_gate"]).reshape(1, 8),
                "wexp": f32(inputs["L1_ffn_W_experts"]),
                "bexp": f32(inputs["L1_ffn_b_experts"]).reshape(1, 8192),
            })
    return maps


def kernel(**inputs):
    maps = fused_in_maps(inputs)
    nc = build_fused()
    res = run_bass_kernel_spmd(nc, maps, core_ids=list(range(8))).results
    out = np.zeros((B, T, D), dtype=np.float32)
    for p in range(4):
        for half in range(2):
            out[p, half * TOWN:(half + 1) * TOWN] = res[2 * p + half]["out"]
    return out
